# version 01
# baseline (speedup 1.0000x reference)
import numpy as np
import ml_dtypes
from contextlib import ExitStack

import concourse.bass as bass
import concourse.mybir as mybir
from concourse.bass_utils import run_bass_kernel_spmd

F32 = mybir.dt.float32
BF16 = mybir.dt.bfloat16
I32 = mybir.dt.int32
AF = mybir.ActivationFunctionType
ALU = mybir.AluOpType
AX = mybir.AxisListType

T = 2048
DM = 1024
NH = 16
HD = 64
LAT = 256
NE0 = 16
IM = 2048
NE1 = 8
HP = 128
KSMAX = float(np.log(2.0 ** 16 - 1.0))
NTILE = T // 128
NCH = DM // 128
NSTR = T // 512
NM = IM // 128

ENGINES = ("sync", "gpsimd", "vector", "scalar", "tensor")


class Prog:
    def __init__(self):
        self.steps = []

    def step(self, eng, fn, inc=1):
        self.steps.append((eng, fn, inc))

    def dma(self, eng, out, in_):
        self.step(eng, lambda e, o=out, i=in_: e.dma_start(out=o, in_=i), 16)

    def build(self, block, sem):
        waits = []
        total = 0
        for _, _, inc in self.steps:
            waits.append(total)
            total += inc
        for eng_name in ENGINES:
            mysteps = [(i, fn, inc) for i, (e, fn, inc) in enumerate(self.steps)
                       if e == eng_name]
            if not mysteps:
                continue

            def make_body(mysteps):
                def body(eng):
                    for i, fn, inc in mysteps:
                        if waits[i] > 0:
                            eng.wait_ge(sem, waits[i])
                        fn(eng).then_inc(sem, inc)
                return body
            getattr(block, eng_name)(make_body(mysteps))
        return total


def host_consts():
    j = np.arange(128)
    u = np.arange(897)
    c = {}
    c["c_eye"] = np.eye(128, dtype=np.float32)
    c["c_sup"] = (j[:, None] <= (u[None, :] - 385)).astype(np.float32)
    c["c_dmat"] = (j[None, :] - j[:, None]).astype(np.float32)
    c["c_iota_row"] = np.arange(T, dtype=np.float32)[None, :]
    c["c_iota_col"] = np.arange(128, dtype=np.float32)[:, None]
    e16 = np.zeros((16, 16 * 128), dtype=np.float32)
    for e in range(16):
        e16[e, e * 128:(e + 1) * 128] = 1.0
    c["c_e16"] = e16
    e8 = np.zeros((8, 8 * 128), dtype=np.float32)
    for e in range(8):
        e8[e, e * 128:(e + 1) * 128] = 1.0
    c["c_e8"] = e8
    return c


class B:
    pass


_UID = [0]


def _uq(name):
    _UID[0] += 1
    return f"{name}_u{_UID[0]}"




def build_program(blocks=(0, 1), n_pass=2, taps=()):
    nc = bass.Bass("TRN2", target_bir_lowering=False)
    b = B()
    b.nc = nc
    b.prog = Prog()
    b.taps = {}
    b.tapset = set(taps)
    p = b.prog

    di = {}
    di["xin"] = nc.dram_tensor("xin", [n_pass, T, DM], F32, kind="ExternalInput").ap()
    for nm, shp in [
        ("wq", [DM, DM]), ("wdkv", [DM, LAT]), ("wuk", [LAT, DM]), ("wuv", [LAT, DM]),
        ("wo", [DM, DM]), ("lng", [1, LAT]), ("lnb", [1, LAT]),
        ("rw", [DM, NE0]), ("rb", [1, NE0]),
        ("w1", [NE0, DM, IM]), ("b1", [NE0, IM]), ("w2", [NE0, IM, DM]), ("b2", [NE0, DM]),
        ("wk1", [DM, DM]), ("wv1", [DM, DM]), ("wc1", [DM, DM]),
        ("lkb", [1, NH]), ("ksc", [1, NH]), ("vcf", [1, NH]), ("vsc", [1, NH]),
        ("wr1", [NE1, DM]), ("wqkv1", [NE1, DM, 3 * HP]), ("wo1", [NE1, HP, DM]),
        ("c_eye", [128, 128]), ("c_sup", [128, 897]), ("c_dmat", [128, 128]),
        ("c_iota_row", [1, T]), ("c_iota_col", [128, 1]),
        ("c_e16", [16, 2048]), ("c_e8", [8, 1024]),
    ]:
        di[nm] = nc.dram_tensor(nm, shp, F32, kind="ExternalInput").ap()
    b.di = di
    b.dout = nc.dram_tensor("out", [n_pass, T, DM], F32, kind="ExternalOutput").ap()
    b.d_xn = nc.dram_tensor("d_xn", [NCH, 128, T], BF16).ap()
    b.d_w1b = nc.dram_tensor("d_w1b", [NE0, 128, NCH * IM], BF16).ap()
    b.d_w2b = nc.dram_tensor("d_w2b", [NE0, 128, NM * DM], BF16).ap()

    with ExitStack() as ctx:
        def sb(name, shape, dtype):
            return ctx.enter_context(nc.sbuf_tensor(name, shape, dtype))
        b.sb_top = sb

        b.x = sb("x", [128, NTILE * DM], F32)
        b.chc = sb("chc", [128, T], BF16)
        b.chs = sb("chs", [128, T], BF16)
        b.eyeb = sb("eyeb", [128, 128], BF16)
        b.eyef = sb("eyef", [128, 128], F32)
        b.sup = sb("supm", [128, 897], BF16)
        b.dmat = sb("dmatc", [128, 128], F32)
        b.e16 = sb("e16c", [16, 2048], BF16)
        b.e8 = sb("e8c", [8, 1024], BF16)
        b.onesb_row = sb("onesb_row", [1, 128], BF16)
        b.onesf_row = sb("onesf_row", [1, 128], F32)
        b.onesb_col = sb("onesb_col", [128, 1], BF16)
        b.onesf_col = sb("onesf_col", [128, 1], F32)
        b.iotac = sb("iotac", [128, 1], F32)
        b.r3 = sb("r3", [1, 512], F32)
        b.r4 = sb("r4", [1, 512], F32)
        b.rb1 = sb("rb1", [1, 512], BF16)
        b.t16 = sb("t16a", [1, 16], F32)
        b.t16b = sb("t16b", [1, 16], F32)
        b.t16c = sb("t16c", [1, 16], F32)
        b.col = [sb(f"col{i}", [128, 1], F32) for i in range(8)]
        b.eps5 = sb("eps5", [128, 1], F32)
        b.scf = sb("scf", [128, T], F32)
        b.scb = sb("scb", [128, T], BF16)
        b.scb2 = sb("scb2", [128, T], BF16)
        b.ebuf = sb("ebuf", [128, 512], BF16)

        b.ps_a = ctx.enter_context(nc.psum_tensor("ps_a", [128, 2048], F32))
        b.ps_b = ctx.enter_context(nc.psum_tensor("ps_b", [128, 1024], F32))
        b.ps_c = ctx.enter_context(nc.psum_tensor("ps_c", [128, 512], F32))
        b.ps_tb = ctx.enter_context(nc.psum_tensor("ps_tb", [128, 512], BF16))

        sem = ctx.enter_context(nc.semaphore("g"))
        block = ctx.enter_context(nc.Block())

        emit_prologue(b, ctx)
        for psx in range(n_pass):
            for t in range(NTILE):
                p.dma("sync", b.x[:, t * DM:(t + 1) * DM], di["xin"][psx, t * 128:(t + 1) * 128, :])
            import os
            skip = set(os.environ.get("K_SKIP", "").split(","))
            for blk in blocks:
                emit_ln(b, ctx, psx, blk)
                if blk == 0:
                    if "l0attn" not in skip:
                        emit_l0_attn(b, ctx, psx)
                    if "l0moe" not in skip:
                        emit_l0_moe(b, ctx, psx)
                else:
                    if "l1attn" not in skip:
                        emit_l1_attn(b, ctx, psx)
                    if "l1ffn" not in skip:
                        emit_l1_ffn(b, ctx, psx)
            for t in range(NTILE):
                p.dma("sync", b.dout[psx, t * 128:(t + 1) * 128, :], b.x[:, t * DM:(t + 1) * DM])

        total = p.build(block, sem)

        @block.sync
        def _(sync):
            sync.wait_ge(sem, total)

    return nc, b


def tap(b, name, ap_fn, shape, dtype=F32):
    if name not in b.tapset or name in b.taps:
        return
    t = b.nc.dram_tensor("tap_" + name, list(shape), dtype, kind="ExternalOutput").ap()
    b.taps[name] = (list(shape), dtype)
    b.prog.dma("sync", t, ap_fn())


def emit_prologue(b, ctx):
    import os
    skip = set(os.environ.get("K_SKIP", "").split(","))
    p, di = b.prog, b.di
    p.dma("gpsimd", b.eyeb[:], di["c_eye"])
    if "eyef" not in skip:
        p.dma("sync", b.eyef[:], di["c_eye"])
    if "sup" not in skip:
        p.dma("gpsimd", b.sup[:], di["c_sup"])
    if "dmatx" not in skip:
        p.dma("sync", b.dmat[:], di["c_dmat"])
    if "e16" not in skip:
        p.dma("gpsimd", b.e16[:], di["c_e16"])
    if "e8" not in skip:
        p.dma("gpsimd", b.e8[:], di["c_e8"])
    if "iotac" not in skip:
        p.dma("sync", b.iotac[:], di["c_iota_col"])
    if "consts" in skip:
        p.step("vector", lambda e: e.memset(b.onesb_row[:], 1.0))
        p.step("vector", lambda e: e.memset(b.onesf_row[:], 1.0))
        p.step("vector", lambda e: e.memset(b.onesb_col[:], 1.0))
        p.step("vector", lambda e: e.memset(b.onesf_col[:], 1.0))
        p.step("vector", lambda e: e.memset(b.eps5[:], 1e-5))
        return
    p.step("vector", lambda e: e.memset(b.onesb_row[:], 1.0))
    p.step("vector", lambda e: e.memset(b.onesf_row[:], 1.0))
    p.step("vector", lambda e: e.memset(b.onesb_col[:], 1.0))
    p.step("vector", lambda e: e.memset(b.onesf_col[:], 1.0))
    p.step("vector", lambda e: e.memset(b.eps5[:], 1e-5))

    import os
    if "rope" in set(os.environ.get("K_SKIP", "").split(",")):
        return
    O2PI = float(1.0 / (2 * np.pi))
    TWO_PI = float(2 * np.pi)
    LN10K = float(np.log(10000.0))
    with ExitStack() as ph:
        iotar = ph.enter_context(b.nc.sbuf_tensor(_uq("iotar"), [1, T], F32))
        ang = ph.enter_context(b.nc.sbuf_tensor(_uq("ang"), [32, T], F32))
        tmpi = ph.enter_context(b.nc.sbuf_tensor(_uq("tmpi"), [32, T], I32))
        tmpf = ph.enter_context(b.nc.sbuf_tensor(_uq("tmpf"), [32, T], F32))
        invf = b.col[7]
        p.dma("sync", iotar[:], di["c_iota_row"])
        p.step("scalar", lambda e: e.activation(
            out=invf[0:32, :], in_=b.iotac[0:32, :], func=AF.Exp, scale=-LN10K / 32.0))
        for s in range(NSTR):
            p.step("tensor", lambda e, s=s: e.matmul(
                b.ps_a[0:32, s * 512:(s + 1) * 512], b.onesf_row[0:1, 0:32],
                iotar[0:1, s * 512:(s + 1) * 512], start=True, stop=True))

        def reduce_sin(dst, phase):
            p.step("vector", lambda e: e.tensor_scalar(
                out=ang[:], in0=b.ps_a[0:32, 0:T], scalar1=invf[0:32, :],
                scalar2=phase, op0=ALU.mult, op1=ALU.add))
            p.step("vector", lambda e: e.tensor_scalar(
                out=tmpi[:], in0=ang[:], scalar1=O2PI, scalar2=None, op0=ALU.mult))
            p.step("vector", lambda e: e.tensor_copy(out=tmpf[:], in_=tmpi[:]))
            p.step("vector", lambda e: e.scalar_tensor_tensor(
                out=ang[:], in0=tmpf[:], scalar=-TWO_PI, in1=ang[:],
                op0=ALU.mult, op1=ALU.add))
            p.step("scalar", lambda e, dst=dst: e.activation(out=dst, in_=ang[:], func=AF.Sin))

        reduce_sin(b.scb[0:32, 0:T], 0.0)
        reduce_sin(b.scb2[0:32, 0:T], float(np.pi / 2))
        p.step("vector", lambda e: e.tensor_scalar(
            out=b.scb[32:64, 0:T], in0=b.scb[0:32, 0:T], scalar1=-1.0, scalar2=None,
            op0=ALU.mult))
        for r in range(4):
            p.dma("sync", b.chc[32 * r:32 * (r + 1), 0:T], b.scb2[0:32, 0:T])
            p.dma("sync", b.chs[32 * r:32 * (r + 1), 0:T],
                  b.scb[32:64, 0:T] if r % 2 == 0 else b.scb[0:32, 0:T])
    tap(b, "ropec", lambda: b.chc[:], [128, T], BF16)
    tap(b, "ropes", lambda: b.chs[:], [128, T], BF16)


def emit_ln(b, ctx, psx, blk):
    p = b.prog
    with ExitStack() as ph:
        stage = ph.enter_context(b.nc.sbuf_tensor(_uq("lnstage"), [128, NCH * 512], BF16))
        for s in range(NSTR):
            for tt in range(4):
                t = s * 4 + tt
                xt = b.x[:, t * DM:(t + 1) * DM]
                c0, c1, mu, var, rstd = b.col[0], b.col[1], b.col[2], b.col[3], b.col[4]
                p.step("scalar", lambda e, xt=xt, c0=c0: e.activation(
                    out=b.scf[:, 0:DM], in_=xt, func=AF.Square, accum_out=c0[:]))
                p.step("scalar", lambda e, xt=xt, c1=c1: e.activation(
                    out=b.scf[:, DM:2 * DM], in_=xt, func=AF.Identity, accum_out=c1[:]))
                p.step("vector", lambda e, c1=c1, mu=mu: e.tensor_scalar(
                    out=mu[:], in0=c1[:], scalar1=1.0 / DM, scalar2=None, op0=ALU.mult))
                p.step("vector", lambda e, mu=mu, var=var: e.tensor_mul(
                    out=var[:], in0=mu[:], in1=mu[:]))
                p.step("vector", lambda e, c0=c0, var=var: e.scalar_tensor_tensor(
                    out=var[:], in0=c0[:], scalar=1.0 / DM, in1=var[:],
                    op0=ALU.mult, op1=ALU.subtract))
                p.step("scalar", lambda e, var=var, rstd=rstd: e.activation(
                    out=rstd[:], in_=var[:], func=AF.Sqrt, bias=b.eps5[:]))
                p.step("vector", lambda e, rstd=rstd: e.reciprocal(out=rstd[:], in_=rstd[:]))
                p.step("vector", lambda e, xt=xt, mu=mu, rstd=rstd: e.tensor_scalar(
                    out=b.scb[:, 0:DM], in0=xt, scalar1=mu[:], scalar2=rstd[:],
                    op0=ALU.subtract, op1=ALU.mult))
                if t == 0:
                    tap(b, f"lnx_b{blk}_p{psx}", lambda xt=xt: xt, [128, DM], F32)
                    tap(b, f"lnc0_b{blk}_p{psx}", lambda c0=c0: c0[:], [128, 1], F32)
                    tap(b, f"lnmu_b{blk}_p{psx}", lambda mu=mu: mu[:], [128, 1], F32)
                    tap(b, f"lnrs_b{blk}_p{psx}", lambda rstd=rstd: rstd[:], [128, 1], F32)
                    tap(b, f"lnscb_b{blk}_p{psx}", lambda: b.scb[:, 0:DM], [128, DM], BF16)
                for c in range(NCH):
                    p.step("tensor", lambda e, c=c: e.transpose(
                        b.ps_tb[:, 0:128], b.scb[:, c * 128:(c + 1) * 128], b.eyeb[:]))
                    p.step("vector", lambda e, c=c, tt=tt: e.tensor_copy(
                        out=stage[:, c * 512 + tt * 128: c * 512 + (tt + 1) * 128],
                        in_=b.ps_tb[:, 0:128]))
            for c in range(NCH):
                p.dma("sync", b.d_xn[c, :, s * 512:(s + 1) * 512],
                      stage[:, c * 512:(c + 1) * 512])
    tap(b, f"xn_b{blk}_p{psx}", lambda: b.scb[:, 0:DM], [128, DM], BF16)


def load_xn_strip(b, dst, s):
    b.prog.dma("sync", dst[:, 0:NCH * 512].rearrange("p (c n) -> p c n", n=512),
               b.d_xn[:, :, s * 512:(s + 1) * 512].rearrange("c p n -> p c n"))


def load_xn_tile(b, dst, t):
    b.prog.dma("sync", dst[:, 0:NCH * 128].rearrange("p (c n) -> p c n", n=128),
               b.d_xn[:, :, t * 128:(t + 1) * 128].rearrange("c p n -> p c n"))


def emit_rope(b, chunk_ap, width, pos0, rows=128):
    p = b.prog
    ngr = rows // 32
    for dg in range(ngr):
        sg = dg ^ 1
        p.step("vector", lambda e, dg=dg, sg=sg: e.tensor_copy(
            out=b.scb2[32 * dg:32 * (dg + 1), 0:width],
            in_=chunk_ap[32 * sg:32 * (sg + 1), 0:width]))
    p.step("vector", lambda e: e.tensor_mul(
        out=b.scb[0:rows, 0:width], in0=chunk_ap, in1=b.chc[0:rows, pos0:pos0 + width]))
    p.step("vector", lambda e: e.tensor_mul(
        out=b.scb2[0:rows, 0:width], in0=b.scb2[0:rows, 0:width],
        in1=b.chs[0:rows, pos0:pos0 + width]))
    p.step("vector", lambda e: e.tensor_add(
        out=chunk_ap, in0=b.scb[0:rows, 0:width], in1=b.scb2[0:rows, 0:width]))


def emit_add_to_x(b, src_f32_ap, s, oc, width=512):
    p = b.prog
    nt = width // 128
    for st in range(nt):
        t = s * 4 + st
        p.step("tensor", lambda e, st=st: e.transpose(
            b.ps_c[:, 0:128], src_f32_ap[:, st * 128:(st + 1) * 128], b.eyef[:]))
        p.step("vector", lambda e, t=t, oc=oc: e.tensor_add(
            out=b.x[:, t * DM + oc * 128: t * DM + (oc + 1) * 128],
            in0=b.x[:, t * DM + oc * 128: t * DM + (oc + 1) * 128],
            in1=b.ps_c[:, 0:128]))


def emit_l0_attn(b, ctx, psx):
    p, di = b.prog, b.di
    with ExitStack() as ph:
        def sb(name, shape, dtype):
            return ph.enter_context(b.nc.sbuf_tensor(_uq(name), shape, dtype))
        kbuf = sb("l0k", [128, NCH * T], BF16)
        vt = sb("l0vt", [128, NTILE * NH * 65], BF16)

        with ExitStack() as ph2:
            def sb2(name, shape, dtype):
                return ph2.enter_context(b.nc.sbuf_tensor(_uq(name), shape, dtype))
            latn = sb2("l0latn", [128, 2 * T], BF16)
            wdkv = sb2("wdkv", [128, NCH * LAT], BF16)
            wuk = sb2("wuk", [128, 2 * DM], BF16)
            wuv = sb2("wuv", [128, 2 * DM], BF16)
            gcol = sb2("lngc", [128, 2], F32)
            bcol = sb2("lnbc", [128, 2], F32)
            latf = sb2("latf", [128, 1024], F32)
            xnst = sb2("xnst0", [128, NCH * 512], BF16)

            p.dma("gpsimd", wdkv[:].rearrange("p (c l) -> p c l", l=LAT),
                  di["wdkv"].rearrange("(c p) l -> p c l", p=128))
            p.dma("gpsimd", wuk[:].rearrange("p (c d) -> p c d", d=DM),
                  di["wuk"].rearrange("(c p) d -> p c d", p=128))
            p.dma("gpsimd", wuv[:].rearrange("p (c d) -> p c d", d=DM),
                  di["wuv"].rearrange("(c p) d -> p c d", p=128))
            p.dma("sync", b.scf[0:2, 0:128], di["lng"].rearrange("o (c p) -> (o c) p", p=128))
            p.dma("sync", b.scf[32:34, 0:128], di["lnb"].rearrange("o (c p) -> (o c) p", p=128))
            p.step("tensor", lambda e: e.transpose(b.ps_c[:, 0:2], b.scf[0:2, 0:128], b.eyef[0:2, 0:2]))
            p.step("vector", lambda e: e.tensor_copy(out=gcol[:], in_=b.ps_c[:, 0:2]))
            p.step("tensor", lambda e: e.transpose(b.ps_c[:, 0:2], b.scf[32:34, 0:128], b.eyef[32:34, 32:34]))
            p.step("vector", lambda e: e.tensor_copy(out=bcol[:], in_=b.ps_c[:, 0:2]))
            tap(b, f"wdkvt{psx}", lambda: wdkv[:], [128, NCH * LAT], BF16)
            p.step("vector", lambda e: e.memset(
                vt[:].rearrange("p (t h c) -> p t h c", h=NH, c=65)[:, :, :, 64:65], 1.0))

            for s in range(NSTR):
                load_xn_strip(b, xnst, s)
                if s == 0:
                    tap(b, f"xnstA{psx}", lambda: xnst[:], [128, NCH * 512], BF16)
                for lc in range(2):
                    for fc in range(NCH):
                        p.step("tensor", lambda e, lc=lc, fc=fc: e.matmul(
                            b.ps_b[:, lc * 512:(lc + 1) * 512],
                            wdkv[:, fc * LAT + lc * 128: fc * LAT + (lc + 1) * 128],
                            xnst[:, fc * 512:(fc + 1) * 512],
                            start=(fc == 0), stop=(fc == NCH - 1)))
                p.step("vector", lambda e: e.tensor_copy(out=latf[:], in_=b.ps_b[:, 0:1024]))
                if s == 0:
                    tap(b, f"latraw{psx}", lambda: latf[:], [128, 1024], F32)
                p.step("scalar", lambda e: e.activation(
                    out=b.scf[:, 0:1024], in_=latf[:], func=AF.Square))
                for lc in range(2):
                    p.step("tensor", lambda e, lc=lc: e.matmul(
                        b.ps_c[0:1, 0:512], b.onesf_col[:], latf[:, lc * 512:(lc + 1) * 512],
                        start=(lc == 0), stop=(lc == 1)))
                p.step("vector", lambda e: e.tensor_scalar(
                    out=b.r3[:], in0=b.ps_c[0:1, 0:512], scalar1=1.0 / LAT, scalar2=None,
                    op0=ALU.mult))
                if s == 0:
                    tap(b, f"latsum{psx}", lambda: b.r3[:], [1, 512], F32)
                for lc in range(2):
                    p.step("tensor", lambda e, lc=lc: e.matmul(
                        b.ps_c[0:1, 0:512], b.onesf_col[:], b.scf[:, lc * 512:(lc + 1) * 512],
                        start=(lc == 0), stop=(lc == 1)))
                p.step("vector", lambda e: e.tensor_mul(out=b.r4[:], in0=b.r3[:], in1=b.r3[:]))
                p.step("vector", lambda e: e.scalar_tensor_tensor(
                    out=b.r4[:], in0=b.ps_c[0:1, 0:512], scalar=1.0 / LAT, in1=b.r4[:],
                    op0=ALU.mult, op1=ALU.subtract))
                p.step("scalar", lambda e: e.activation(out=b.r4[:], in_=b.r4[:], func=AF.Sqrt, bias=b.eps5[0:1, 0:1]))
                p.step("vector", lambda e: e.reciprocal(out=b.r4[:], in_=b.r4[:]))
                p.step("tensor", lambda e: e.matmul(
                    b.ps_c[:, 0:512], b.onesf_row[0:1, :], b.r3[:], start=True, stop=True))
                for lc in range(2):
                    p.step("vector", lambda e, lc=lc: e.tensor_sub(
                        out=latf[:, lc * 512:(lc + 1) * 512],
                        in0=latf[:, lc * 512:(lc + 1) * 512], in1=b.ps_c[:, 0:512]))
                p.step("tensor", lambda e: e.matmul(
                    b.ps_c[:, 0:512], b.onesf_row[0:1, :], b.r4[:], start=True, stop=True))
                for lc in range(2):
                    p.step("vector", lambda e, lc=lc: e.tensor_mul(
                        out=latf[:, lc * 512:(lc + 1) * 512],
                        in0=latf[:, lc * 512:(lc + 1) * 512], in1=b.ps_c[:, 0:512]))
                for lc in range(2):
                    p.step("vector", lambda e, lc=lc, s=s: e.tensor_scalar(
                        out=latn[:, lc * T + s * 512: lc * T + (s + 1) * 512],
                        in0=latf[:, lc * 512:(lc + 1) * 512],
                        scalar1=gcol[:, lc:lc + 1], scalar2=bcol[:, lc:lc + 1],
                        op0=ALU.mult, op1=ALU.add))
                if s == 0:
                    tap(b, f"latmu{psx}", lambda: b.r3[:], [1, 512], F32)
                    tap(b, f"latrs{psx}", lambda: b.r4[:], [1, 512], F32)
                    tap(b, f"latf{psx}", lambda: latf[:], [128, 1024], F32)
                    tap(b, f"gcol{psx}", lambda: gcol[:], [128, 2], F32)
                    tap(b, f"bcol{psx}", lambda: bcol[:], [128, 2], F32)
            tap(b, f"latn{psx}", lambda: latn[:], [128, 2 * T], BF16)

            for c in range(NCH):
                for s in range(NSTR):
                    for lc in range(2):
                        p.step("tensor", lambda e, c=c, s=s, lc=lc: e.matmul(
                            b.ps_c[:, 0:512],
                            wuk[:, lc * DM + c * 128: lc * DM + (c + 1) * 128],
                            latn[:, lc * T + s * 512: lc * T + (s + 1) * 512],
                            start=(lc == 0), stop=(lc == 1)))
                    p.step("scalar", lambda e, c=c, s=s: e.activation(
                        out=kbuf[:, c * T + s * 512: c * T + (s + 1) * 512],
                        in_=b.ps_c[:, 0:512], func=AF.Identity))
            for c in range(NCH):
                emit_rope(b, kbuf[:, c * T:(c + 1) * T], T, 0)
            tap(b, f"k0_{psx}", lambda: kbuf[:, 0:T], [128, T], BF16)

            for tt in range(NTILE):
                for half in range(2):
                    for lc in range(2):
                        p.step("tensor", lambda e, tt=tt, half=half, lc=lc: e.matmul(
                            b.ps_b[:, half * 512:(half + 1) * 512],
                            latn[:, lc * T + tt * 128: lc * T + (tt + 1) * 128],
                            wuv[:, lc * DM + half * 512: lc * DM + (half + 1) * 512],
                            start=(lc == 0), stop=(lc == 1)))
                p.step("vector", lambda e, tt=tt: e.tensor_copy(
                    out=vt[:, tt * (NH * 65):(tt + 1) * (NH * 65)]
                    .rearrange("p (h c) -> p h c", c=65)[:, :, 0:64],
                    in_=b.ps_b[:, 0:1024].rearrange("p (h c) -> p h c", c=64)))
            tap(b, f"v0_{psx}", lambda: vt[:, 0:NH * 65], [128, NH * 65], BF16)

        with ExitStack() as ph2:
            def sb2(name, shape, dtype):
                return ph2.enter_context(b.nc.sbuf_tensor(_uq(name), shape, dtype))
            xnstb = sb2("xnst0b", [128, NCH * 512], BF16)
            qst = sb2("qst", [128, NCH * 512], BF16)
            ost = sb2("ost", [128, NCH * 512], BF16)
            wqp = sb2("wqp", [128, NCH * 256], BF16)
            wop = sb2("wop", [128, NCH * 256], BF16)
            nk2 = b.t16

            for h in range(NH):
                c, rb = h // 2, (h % 2) * 64
                p.step("scalar", lambda e, c=c, rb=rb: e.activation(
                    out=b.scf[0:64, 0:T], in_=kbuf[rb:rb + 64, c * T:(c + 1) * T],
                    func=AF.Square))
                for s in range(NSTR):
                    p.step("tensor", lambda e, s=s: e.matmul(
                        b.ps_b[0:1, s * 256:s * 256 + 512] if False else b.ps_c[0:1, 0:512],
                        b.onesf_col[0:64, :], b.scf[0:64, s * 512:(s + 1) * 512],
                        start=True, stop=True))
                    p.step("vector", lambda e, s=s: e.tensor_copy(
                        out=b.r3[0:1, 0:512] if s == 0 else b.r4[0:1, 0:512],
                        in_=b.ps_c[0:1, 0:512]))
                    if s > 0:
                        p.step("vector", lambda e: e.tensor_max(
                            out=b.r3[:], in0=b.r3[:], in1=b.r4[:]))
                p.step("vector", lambda e, h=h: e.tensor_reduce(
                    out=nk2[0:1, h:h + 1], in_=b.r3[:], axis=AX.X, op=ALU.max))

            for s in range(NSTR):
                load_xn_strip(b, xnstb, s)
                for cp in range(4):
                    p.dma("gpsimd", wqp[:].rearrange("p (c n) -> p c n", n=256),
                          di["wq"].rearrange("(c p) n -> p c n", p=128)[:, :, cp * 256:(cp + 1) * 256])
                    for j2 in range(2):
                        for fc in range(NCH):
                            p.step("tensor", lambda e, j2=j2, fc=fc: e.matmul(
                                b.ps_b[:, j2 * 512:(j2 + 1) * 512],
                                wqp[:, fc * 256 + j2 * 128: fc * 256 + (j2 + 1) * 128],
                                xnstb[:, fc * 512:(fc + 1) * 512],
                                start=(fc == 0), stop=(fc == NCH - 1)))
                        p.step("scalar", lambda e, cp=cp, j2=j2: e.activation(
                            out=qst[:, (cp * 2 + j2) * 512:(cp * 2 + j2 + 1) * 512],
                            in_=b.ps_b[:, j2 * 512:(j2 + 1) * 512], func=AF.Identity))
                for c in range(NCH):
                    emit_rope(b, qst[:, c * 512:(c + 1) * 512], 512, s * 512)
                if psx == 0 and s == 0:
                    tap(b, "q0", lambda: qst[:, 0:NCH * 512], [128, NCH * 512], BF16)

                for h in range(NH):
                    c, rb = h // 2, (h % 2) * 64
                    qsl = qst[rb:rb + 64, c * 512:(c + 1) * 512]
                    p.step("scalar", lambda e, qsl=qsl: e.activation(
                        out=b.scf[0:64, 0:512], in_=qsl, func=AF.Square))
                    p.step("tensor", lambda e: e.matmul(
                        b.ps_c[0:1, 0:512], b.onesf_col[0:64, :], b.scf[0:64, 0:512],
                        start=True, stop=True))
                    p.step("vector", lambda e, h=h: e.tensor_scalar(
                        out=b.r3[:], in0=b.ps_c[0:1, 0:512], scalar1=nk2[0:1, h:h + 1],
                        scalar2=-0.5, op0=ALU.add, op1=ALU.mult))
                    p.step("vector", lambda e: e.tensor_copy(out=b.rb1[:], in_=b.r3[:]))
                    nkc = 4 * s + 4
                    for kc in range(nkc):
                        p.step("tensor", lambda e, h=h, kc=kc, c=c, rb=rb: e.matmul(
                            b.ps_c[:, 0:512],
                            kbuf[rb:rb + 64, c * T + kc * 128: c * T + (kc + 1) * 128],
                            qst[rb:rb + 64, c * 512:(c + 1) * 512],
                            start=True, stop=False))
                        p.step("tensor", lambda e: e.matmul(
                            b.ps_c[:, 0:512], b.onesb_row[:], b.rb1[:],
                            start=False, stop=True))
                        p.step("scalar", lambda e: e.activation(
                            out=b.ebuf[:], in_=b.ps_c[:, 0:512], func=AF.Exp, scale=0.125))
                        r = kc - 4 * s
                        if r >= 0:
                            p.step("vector", lambda e, r=r: e.tensor_mul(
                                out=b.ebuf[:], in0=b.ebuf[:],
                                in1=b.sup[:, 385 - 128 * r: 897 - 128 * r]))
                        p.step("tensor", lambda e, h=h, kc=kc, last=(kc == nkc - 1): e.matmul(
                            b.ps_a[0:65, 0:512],
                            vt[:, kc * (NH * 65) + h * 65: kc * (NH * 65) + (h + 1) * 65],
                            b.ebuf[:], start=(kc == 0), stop=last))
                    p.step("vector", lambda e: e.tensor_scalar(
                        out=b.r3[:], in0=b.ps_a[64:65, 0:512], scalar1=1e-30, scalar2=None,
                        op0=ALU.add))
                    p.step("vector", lambda e: e.reciprocal(out=b.r4[:], in_=b.r3[:]))
                    p.step("vector", lambda e: e.tensor_copy(out=b.rb1[:], in_=b.r4[:]))
                    p.step("tensor", lambda e: e.matmul(
                        b.ps_c[0:64, 0:512], b.onesb_row[0:1, 0:64], b.rb1[:],
                        start=True, stop=True))
                    p.step("vector", lambda e: e.tensor_copy(
                        out=b.scf[0:64, 0:512], in_=b.ps_c[0:64, 0:512]))
                    p.step("vector", lambda e, h=h, c=c, rb=rb: e.tensor_mul(
                        out=ost[rb:rb + 64, c * 512:(c + 1) * 512],
                        in0=b.ps_a[0:64, 0:512], in1=b.scf[0:64, 0:512]))
                if psx == 0 and s == 0:
                    tap(b, "o0", lambda: ost[:, 0:NCH * 512], [128, NCH * 512], BF16)

                for cp in range(4):
                    p.dma("gpsimd", wop[:].rearrange("p (c n) -> p c n", n=256),
                          di["wo"].rearrange("(c p) n -> p c n", p=128)[:, :, cp * 256:(cp + 1) * 256])
                    for j2 in range(2):
                        oc = cp * 2 + j2
                        for ci in range(NCH):
                            p.step("tensor", lambda e, j2=j2, ci=ci: e.matmul(
                                b.ps_b[:, j2 * 512:(j2 + 1) * 512],
                                wop[:, ci * 256 + j2 * 128: ci * 256 + (j2 + 1) * 128],
                                ost[:, ci * 512:(ci + 1) * 512],
                                start=(ci == 0), stop=(ci == NCH - 1)))
                        p.step("vector", lambda e, j2=j2: e.tensor_copy(
                            out=b.scf[:, 0:512], in_=b.ps_b[:, j2 * 512:(j2 + 1) * 512]))
                        emit_add_to_x(b, b.scf[:, 0:512], s, oc)


def emit_l0_moe(b, ctx, psx):
    p, di = b.prog, b.di
    with ExitStack() as ph:
        def sb(name, shape, dtype):
            return ph.enter_context(b.nc.sbuf_tensor(_uq(name), shape, dtype))
        xnst = sb("xnstm", [128, NCH * 512], BF16)
        w1sb = sb("w1sb", [128, NCH * IM], BF16)
        w2sb = sb("w2sb", [128, NM * 512], BF16)
        hm = sb("hm", [128, NM * 512], BF16)
        acc = sb("macc", [128, NCH * 512], F32)
        cer = sb("cer", [128, 512], BF16)
        cefm = sb("cefm", [16, T], BF16)
        b1t = sb("b1t", [128, NE0 * 16], F32)
        b2sb = sb("b2sb", [16, DM], BF16)
        rwsb = sb("rwsb", [128, NCH * NE0], BF16)
        rbsb = sb("rbsb", [1, NE0], BF16)
        lg = sb("lgts", [128, NE0], F32)

        p.dma("gpsimd", rwsb[:].rearrange("p (c n) -> p c n", n=NE0),
              di["rw"].rearrange("(c p) n -> p c n", p=128))
        p.dma("gpsimd", rbsb[:], di["rb"])
        p.dma("gpsimd", b2sb[:], di["b2"])
        for e0 in range(NE0):
            p.dma("sync", b.scf[0:16, 0:128], di["b1"][e0:e0 + 1, :].rearrange("o (m q) -> (o m) q", q=128))
            p.step("tensor", lambda e, e0=e0: e.transpose(
                b.ps_c[:, 0:16], b.scf[0:16, 0:128], b.eyef[0:16, 0:16]))
            p.step("vector", lambda e, e0=e0: e.tensor_copy(
                out=b1t[:, e0 * 16:(e0 + 1) * 16], in_=b.ps_c[:, 0:16]))

        for t in range(NTILE):
            load_xn_tile(b, xnst, t)
            for fc in range(NCH):
                p.step("tensor", lambda e, fc=fc: e.matmul(
                    b.ps_c[:, 0:NE0],
                    xnst[:, fc * 128:(fc + 1) * 128],
                    rwsb[:, fc * NE0:(fc + 1) * NE0],
                    start=(fc == 0), stop=False))
            p.step("tensor", lambda e: e.matmul(
                b.ps_c[:, 0:NE0], b.onesb_row[:], rbsb[:], start=False, stop=True))
            l0, m1, mr, msk = b.scf[:, 0:NE0], b.col[0], b.col[1], b.scf[:, 64:64 + NE0]
            p.step("vector", lambda e: e.tensor_copy(out=l0, in_=b.ps_c[:, 0:NE0]))
            p.step("vector", lambda e: e.tensor_copy(out=b.scf[:, 32:32 + NE0], in_=l0))
            lw = b.scf[:, 32:32 + NE0]
            for rnd in range(4):
                p.step("vector", lambda e, dst=(m1 if rnd == 0 else mr): e.tensor_reduce(
                    out=dst[:], in_=lw, axis=AX.X, op=ALU.max))
                if rnd < 3:
                    src = m1 if rnd == 0 else mr
                    p.step("vector", lambda e, src=src: e.tensor_scalar(
                        out=msk, in0=lw, scalar1=src[:], scalar2=None, op0=ALU.is_ge))
                    p.step("vector", lambda e: e.scalar_tensor_tensor(
                        out=lw, in0=msk, scalar=-1e30, in1=lw, op0=ALU.mult, op1=ALU.add))
            p.step("vector", lambda e: e.tensor_scalar(
                out=msk, in0=l0, scalar1=mr[:], scalar2=None, op0=ALU.is_ge))
            p.step("vector", lambda e: e.tensor_scalar(
                out=b.col[2][:], in0=m1[:], scalar1=-1.0, scalar2=None, op0=ALU.mult))
            p.step("scalar", lambda e: e.activation(
                out=b.scf[:, 96:96 + NE0], in_=l0, func=AF.Exp, bias=b.col[2][:]))
            p.step("vector", lambda e: e.tensor_mul(
                out=b.scf[:, 96:96 + NE0], in0=b.scf[:, 96:96 + NE0], in1=msk))
            p.step("vector", lambda e: e.tensor_reduce(
                out=b.col[3][:], in_=b.scf[:, 96:96 + NE0], axis=AX.X, op=ALU.add))
            p.step("vector", lambda e: e.reciprocal(out=b.col[3][:], in_=b.col[3][:]))
            p.step("vector", lambda e: e.tensor_scalar(
                out=lg[:], in0=b.scf[:, 96:96 + NE0], scalar1=b.col[3][:], scalar2=None,
                op0=ALU.mult))
            p.step("tensor", lambda e: e.transpose(b.ps_c[0:16, 0:128], lg[:], b.eyef[:]))
            p.step("vector", lambda e, t=t: e.tensor_copy(
                out=cefm[:, t * 128:(t + 1) * 128], in_=b.ps_c[0:16, 0:128]))
        tap(b, f"ce{psx}", lambda: cefm[:], [16, T], BF16)

        for s in range(NSTR):
            load_xn_strip(b, xnst, s)
            for half in range(2):
                for j in range(4):
                    oc = half * 4 + j
                    p.step("tensor", lambda e, oc=oc, j=j, s=s: e.matmul(
                        b.ps_a[:, j * 512:(j + 1) * 512],
                        b2sb[:, oc * 128:(oc + 1) * 128],
                        cefm[:, s * 512:(s + 1) * 512],
                        start=True, stop=True))
                for j in range(4):
                    oc = half * 4 + j
                    p.step("vector", lambda e, oc=oc, j=j: e.tensor_copy(
                        out=acc[:, oc * 512:(oc + 1) * 512], in_=b.ps_a[:, j * 512:(j + 1) * 512]))
            for e0 in range(NE0):
                first = (psx == 0 and s == 0)
                if first:
                    p.dma("gpsimd", w1sb[:].rearrange("p (c m) -> p c m", m=IM),
                          di["w1"][e0].rearrange("(c p) m -> p c m", p=128))
                    p.dma("sync", b.d_w1b[e0], w1sb[:])
                else:
                    p.dma("sync", w1sb[:], b.d_w1b[e0])
                p.step("tensor", lambda e, e0=e0, s=s: e.matmul(
                    b.ps_c[:, 0:512], b.e16[:, e0 * 128:(e0 + 1) * 128],
                    cefm[:, s * 512:(s + 1) * 512], start=True, stop=True))
                p.step("vector", lambda e: e.tensor_copy(out=cer[:], in_=b.ps_c[:, 0:512]))
                for mp in range(NM // 2):
                    for k2 in range(2):
                        m = 2 * mp + k2
                        for fc in range(NCH):
                            p.step("tensor", lambda e, m=m, k2=k2, fc=fc: e.matmul(
                                b.ps_b[:, k2 * 512:(k2 + 1) * 512],
                                w1sb[:, fc * IM + m * 128: fc * IM + (m + 1) * 128],
                                xnst[:, fc * 512:(fc + 1) * 512],
                                start=(fc == 0), stop=(fc == NCH - 1)))
                    for k2 in range(2):
                        m = 2 * mp + k2
                        p.step("scalar", lambda e, m=m, k2=k2, e0=e0: e.activation(
                            out=hm[:, m * 512:(m + 1) * 512],
                            in_=b.ps_b[:, k2 * 512:(k2 + 1) * 512], func=AF.Gelu,
                            bias=b1t[:, e0 * 16 + m: e0 * 16 + m + 1]))
                    p.step("vector", lambda e, mp=mp: e.tensor_tensor(
                        out=hm[:, mp * 1024:(mp + 1) * 1024]
                        .rearrange("p (k n) -> p k n", k=2),
                        in0=hm[:, mp * 1024:(mp + 1) * 1024]
                        .rearrange("p (k n) -> p k n", k=2),
                        in1=cer[:].unsqueeze(1).broadcast_to((128, 2, 512)),
                        op=ALU.mult))
                for half in range(2):
                    cache_view = b.d_w2b[e0].rearrange(
                        "p (m o) -> p m o", o=1024)[:, :, half * 512:(half + 1) * 512]
                    if first:
                        p.dma("gpsimd", w2sb[:].rearrange("p (m n) -> p m n", n=512),
                              di["w2"][e0].rearrange("(m p) n -> p m n", p=128)[:, :, half * 512:(half + 1) * 512])
                        p.dma("sync", cache_view, w2sb[:].rearrange("p (m n) -> p m n", n=512))
                    else:
                        p.dma("sync", w2sb[:].rearrange("p (m n) -> p m n", n=512), cache_view)
                    for m in range(NM):
                        for j in range(4):
                            p.step("tensor", lambda e, m=m, j=j: e.matmul(
                                b.ps_a[:, j * 512:(j + 1) * 512],
                                w2sb[:, m * 512 + j * 128: m * 512 + (j + 1) * 128],
                                hm[:, m * 512:(m + 1) * 512],
                                start=(m == 0), stop=(m == NM - 1)))
                    for j in range(4):
                        oc = half * 4 + j
                        p.step("vector", lambda e, oc=oc, j=j: e.tensor_add(
                            out=acc[:, oc * 512:(oc + 1) * 512],
                            in0=acc[:, oc * 512:(oc + 1) * 512],
                            in1=b.ps_a[:, j * 512:(j + 1) * 512]))
            if psx == 0 and s == 0:
                tap(b, "moeacc", lambda: acc[:], [128, NCH * 512], F32)
            for oc in range(NCH):
                emit_add_to_x(b, acc[:, oc * 512:(oc + 1) * 512], s, oc)


def emit_l1_attn(b, ctx, psx):
    p, di = b.prog, b.di
    with ExitStack() as ph:
        def sb(name, shape, dtype):
            return ph.enter_context(b.nc.sbuf_tensor(_uq(name), shape, dtype))
        kh = sb("l1k", [128, NCH * T], BF16)

        Rrow = b.t16
        R2n = b.t16b
        vfac = b.t16c
        p.dma("sync", b.r3[0:1, 0:NH], di["ksc"])
        p.step("vector", lambda e: e.tensor_scalar(
            out=b.r3[0:1, 0:NH], in0=b.r3[0:1, 0:NH], scalar1=10.0, scalar2=KSMAX,
            op0=ALU.mult, op1=ALU.min))
        p.step("scalar", lambda e: e.activation(out=Rrow[:], in_=b.r3[0:1, 0:NH], func=AF.Exp))
        p.step("vector", lambda e: e.tensor_mul(out=R2n[:], in0=Rrow[:], in1=Rrow[:]))
        p.step("vector", lambda e: e.tensor_scalar(
            out=R2n[:], in0=R2n[:], scalar1=-1.0, scalar2=None, op0=ALU.mult))
        p.dma("sync", b.r3[0:1, 0:NH], di["vsc"])
        p.step("scalar", lambda e: e.activation(
            out=vfac[:], in_=b.r3[0:1, 0:NH], func=AF.Exp, scale=10.0))

        def build_wT(name, wt, wnat):
            for ci in range(NCH):
                p.dma("gpsimd", wnat[:], di[name][ci * 128:(ci + 1) * 128, :])
                for co in range(NCH):
                    p.step("tensor", lambda e, co=co: e.transpose(
                        b.ps_tb[:, 0:128], wnat[:, co * 128:(co + 1) * 128], b.eyeb[:]))
                    p.step("vector", lambda e, co=co, ci=ci: e.tensor_copy(
                        out=wt[:, co * 1024 + ci * 128: co * 1024 + (ci + 1) * 128],
                        in_=b.ps_tb[:, 0:128]))

        with ExitStack() as ph2:
            def sb2(name, shape, dtype):
                return ph2.enter_context(b.nc.sbuf_tensor(_uq(name), shape, dtype))
            wt = sb2("l1wtk", [128, NCH * 1024], BF16)
            wnat = sb2("l1wn", [128, 1024], BF16)
            Lt = sb2("lt", [128, NH * 128], BF16)
            dvr = sb2("dvr", [128, NH * 128], BF16)
            betac = sb2("betac", [128, 1], F32)
            xnt = sb2("xnt1", [128, NCH * 128], BF16)
            ring = [sb2(f"ring{i}", [128, 1024], BF16) for i in range(2)]

            build_wT("wk1", wt, wnat)

            p.dma("sync", b.r3[0:1, 0:NH], di["lkb"])
            for h in range(NH):
                p.step("tensor", lambda e, h=h: e.matmul(
                    b.ps_c[:, 0:1], b.onesf_row[:], b.r3[0:1, h:h + 1], start=True, stop=True))
                p.step("scalar", lambda e: e.activation(
                    out=betac[:], in_=b.ps_c[:, 0:1], func=AF.Abs))
                p.step("vector", lambda e: e.tensor_scalar(
                    out=betac[:], in0=betac[:], scalar1=-10.0, scalar2=None, op0=ALU.mult))
                p.step("vector", lambda e: e.tensor_scalar(
                    out=b.scf[:, 0:128], in0=b.dmat[:], scalar1=betac[:], scalar2=0.0,
                    op0=ALU.mult, op1=ALU.min))
                p.step("scalar", lambda e: e.activation(
                    out=b.scf[:, 128:256], in_=b.scf[:, 0:128], func=AF.Exp))
                p.step("vector", lambda e, h=h: e.tensor_mul(
                    out=Lt[:, h * 128:(h + 1) * 128], in0=b.scf[:, 128:256],
                    in1=b.sup[:, 385:513]))
                p.step("vector", lambda e: e.tensor_scalar(
                    out=b.scf[0:1, 0:128], in0=b.dmat[0:1, 0:128],
                    scalar1=1.0, scalar2=None, op0=ALU.add))
                p.step("vector", lambda e: e.tensor_scalar(
                    out=b.scf[0:1, 0:128], in0=b.scf[0:1, 0:128],
                    scalar1=betac[0:1, 0:1], scalar2=None, op0=ALU.mult))
                p.step("scalar", lambda e: e.activation(
                    out=b.scf[0:1, 0:128], in_=b.scf[0:1, 0:128], func=AF.Exp))
                p.step("vector", lambda e: e.tensor_copy(
                    out=b.rb1[0:1, 0:128], in_=b.scf[0:1, 0:128]))
                p.step("tensor", lambda e: e.matmul(
                    b.ps_c[:, 0:128], b.onesb_row[0:1, 0:128], b.rb1[0:1, 0:128],
                    start=True, stop=True))
                p.step("vector", lambda e, h=h: e.tensor_copy(
                    out=dvr[:, h * 128:(h + 1) * 128], in_=b.ps_c[:, 0:128]))
            tap(b, f"lt{psx}", lambda: Lt[:], [128, NH * 128], BF16)

            for t in range(NTILE):
                load_xn_tile(b, xnt, t)
                for half in range(2):
                    for fc in range(NCH):
                        p.step("tensor", lambda e, half=half, fc=fc: e.matmul(
                            b.ps_b[:, half * 512:(half + 1) * 512],
                            xnt[:, fc * 128:(fc + 1) * 128],
                            wt[:, fc * 1024 + half * 512: fc * 1024 + (half + 1) * 512],
                            start=(fc == 0), stop=(fc == NCH - 1)))
                p.step("scalar", lambda e, t=t: e.activation(
                    out=ring[t % 2][:], in_=b.ps_b[:, 0:1024], func=AF.Identity))
                ktm = ring[t % 2]
                for h in range(NH):
                    p.step("tensor", lambda e, h=h, ktm=ktm: e.matmul(
                        b.ps_a[0:64, h * 128:(h + 1) * 128],
                        ktm[:, h * 64:(h + 1) * 64],
                        Lt[:, h * 128:(h + 1) * 128], start=True, stop=True))
                for h in range(NH):
                    c, rb = h // 2, (h % 2) * 64
                    dst = kh[rb:rb + 64, c * T + t * 128: c * T + (t + 1) * 128]
                    pssl = b.ps_a[0:64, h * 128:(h + 1) * 128]
                    if t == 0:
                        p.step("vector", lambda e, dst=dst, pssl=pssl: e.tensor_copy(
                            out=dst, in_=pssl))
                    else:
                        carry = kh[rb:rb + 64, c * T + t * 128 - 1: c * T + t * 128]
                        p.step("vector", lambda e, dst=dst, pssl=pssl, carry=carry, h=h, rb=rb:
                               e.scalar_tensor_tensor(
                                   out=dst, in0=dvr[rb:rb + 64, h * 128:(h + 1) * 128],
                                   scalar=carry, in1=pssl, op0=ALU.mult, op1=ALU.add))
            tap(b, f"ksm{psx}", lambda: kh[:, 0:T], [128, T], BF16)

            for h in range(NH):
                c, rb = h // 2, (h % 2) * 64
                p.step("scalar", lambda e, c=c, rb=rb: e.activation(
                    out=b.scf[0:64, 0:T], in_=kh[rb:rb + 64, c * T:(c + 1) * T],
                    func=AF.Square))
                for s in range(NSTR):
                    p.step("tensor", lambda e, s=s: e.matmul(
                        b.ps_c[0:1, 0:512], b.onesf_col[0:64, :],
                        b.scf[0:64, s * 512:(s + 1) * 512], start=True, stop=True))
                    p.step("scalar", lambda e: e.activation(
                        out=b.r3[:], in_=b.ps_c[0:1, 0:512], func=AF.Sqrt))
                    p.step("vector", lambda e: e.tensor_scalar(
                        out=b.r3[:], in0=b.r3[:], scalar1=1e-10, scalar2=None, op0=ALU.add))
                    p.step("vector", lambda e: e.reciprocal(out=b.r3[:], in_=b.r3[:]))
                    p.step("vector", lambda e, h=h: e.tensor_scalar(
                        out=b.rb1[:], in0=b.r3[:], scalar1=Rrow[0:1, h:h + 1], scalar2=None,
                        op0=ALU.mult))
                    p.step("tensor", lambda e: e.matmul(
                        b.ps_c[:, 0:512], b.onesb_row[0:1, 0:128], b.rb1[:],
                        start=True, stop=True))
                    p.step("vector", lambda e, rb=rb: e.tensor_copy(
                        out=b.scf[rb:rb + 64, 0:512], in_=b.ps_c[0:64, 0:512]))
                    p.step("vector", lambda e, h=h, c=c, rb=rb, s=s: e.tensor_mul(
                        out=kh[rb:rb + 64, c * T + s * 512: c * T + (s + 1) * 512],
                        in0=kh[rb:rb + 64, c * T + s * 512: c * T + (s + 1) * 512],
                        in1=b.scf[rb:rb + 64, 0:512]))
            tap(b, f"khat{psx}", lambda: kh[:, 0:T], [128, T], BF16)

        with ExitStack() as ph2:
            def sb2(name, shape, dtype):
                return ph2.enter_context(b.nc.sbuf_tensor(_uq(name), shape, dtype))
            vt = sb2("l1vt", [128, NTILE * NH * 65], BF16)
            wtv = sb2("l1wtv", [128, NCH * 1024], BF16)
            wnatv = sb2("l1wn2", [128, 1024], BF16)
            xntv = sb2("xnt2", [128, NCH * 128], BF16)
            vring = [sb2(f"vring{i}", [128, 1024], BF16) for i in range(2)]
            vcf = sb2("vcfr", [128, NH], F32)
            vfr = sb2("vfacr", [128, NH], F32)
            nrm = sb2("vnrm", [128, NH], F32)
            r2c = sb2("r2c", [128, NH], F32)
            ost = sb2("ost1", [128, NCH * 512], BF16)

            build_wT("wv1", wtv, wnatv)
            p.step("vector", lambda e: e.memset(
                vt[:].rearrange("p (t h c) -> p t h c", h=NH, c=65)[:, :, :, 64:65], 1.0))
            p.dma("sync", b.r3[0:1, 0:NH], di["vcf"])
            p.step("tensor", lambda e: e.matmul(
                b.ps_c[:, 0:NH], b.onesf_row[:], b.r3[0:1, 0:NH], start=True, stop=True))
            p.step("vector", lambda e: e.tensor_copy(out=vcf[:], in_=b.ps_c[:, 0:NH]))
            p.step("tensor", lambda e: e.matmul(
                b.ps_c[:, 0:NH], b.onesf_row[:], vfac[:], start=True, stop=True))
            p.step("vector", lambda e: e.tensor_copy(out=vfr[:], in_=b.ps_c[:, 0:NH]))

            for t in range(NTILE + 1):
                if t < NTILE:
                    load_xn_tile(b, xntv, t)
                    for half in range(2):
                        for fc in range(NCH):
                            p.step("tensor", lambda e, half=half, fc=fc: e.matmul(
                                b.ps_b[:, half * 512:(half + 1) * 512],
                                xntv[:, fc * 128:(fc + 1) * 128],
                                wtv[:, fc * 1024 + half * 512: fc * 1024 + (half + 1) * 512],
                                start=(fc == 0), stop=(fc == NCH - 1)))
                    p.step("scalar", lambda e, t=t: e.activation(
                        out=vring[t % 2][:], in_=b.ps_b[:, 0:1024], func=AF.Identity))
                if t == 0:
                    continue
                tp = t - 1
                vcur = vring[tp % 2]
                vnxt = vring[t % 2] if t < NTILE else None
                shp = b.scb
                if vnxt is None:
                    p.step("vector", lambda e: e.memset(b.scb[:, 0:1024], 0.0))
                p.dma("sync", shp[0:127, 0:1024], vcur[1:128, :])
                if vnxt is not None:
                    p.dma("sync", shp[127:128, 0:1024], vnxt[0:1, :])
                p.step("vector", lambda e, vcur=vcur: e.tensor_sub(
                    out=b.scf[:, 0:1024], in0=vcur[:], in1=shp[:, 0:1024]))
                p.step("vector", lambda e: e.tensor_tensor(
                    out=b.scf[:, 0:1024].rearrange("p (h c) -> p h c", c=64),
                    in0=b.scf[:, 0:1024].rearrange("p (h c) -> p h c", c=64),
                    in1=vcf[:].unsqueeze(2).broadcast_to((128, NH, 64)),
                    op=ALU.mult))
                p.step("vector", lambda e: e.tensor_add(
                    out=b.scf[:, 0:1024], in0=b.scf[:, 0:1024], in1=shp[:, 0:1024]))
                p.step("scalar", lambda e: e.activation(
                    out=b.scf[:, 1024:2048], in_=b.scf[:, 0:1024], func=AF.Square))
                p.step("vector", lambda e: e.tensor_reduce(
                    out=nrm[:], in_=b.scf[:, 1024:2048].rearrange("p (h c) -> p h c", c=64),
                    axis=AX.X, op=ALU.add))
                p.step("scalar", lambda e: e.activation(out=nrm[:], in_=nrm[:], func=AF.Sqrt))
                p.step("vector", lambda e: e.tensor_scalar(
                    out=nrm[:], in0=nrm[:], scalar1=1e-10, scalar2=None, op0=ALU.add))
                p.step("vector", lambda e: e.reciprocal(out=nrm[:], in_=nrm[:]))
                p.step("vector", lambda e: e.tensor_mul(out=nrm[:], in0=nrm[:], in1=vfr[:]))
                p.step("vector", lambda e, tp=tp: e.tensor_tensor(
                    out=vt[:, tp * (NH * 65):(tp + 1) * (NH * 65)]
                    .rearrange("p (h c) -> p h c", c=65)[:, :, 0:64],
                    in0=b.scf[:, 0:1024].rearrange("p (h c) -> p h c", c=64),
                    in1=nrm[:].unsqueeze(2).broadcast_to((128, NH, 64)),
                    op=ALU.mult))
            tap(b, f"vhat{psx}", lambda: vt[:, 0:NH * 65], [128, NH * 65], BF16)

            build_wT("wc1", wtv, wnatv)

            p.step("tensor", lambda e: e.matmul(
                b.ps_c[:, 0:NH], b.onesf_row[:], R2n[:], start=True, stop=True))
            p.step("vector", lambda e: e.tensor_copy(out=r2c[:], in_=b.ps_c[:, 0:NH]))

            for s in range(NSTR):
                for h in range(NH):
                    c, rb = h // 2, (h % 2) * 64
                    nkc = 4 * s + 4
                    for kc in range(nkc):
                        p.step("tensor", lambda e, kc=kc, c=c, rb=rb, s=s: e.matmul(
                            b.ps_c[:, 0:512],
                            kh[rb:rb + 64, c * T + kc * 128: c * T + (kc + 1) * 128],
                            kh[rb:rb + 64, c * T + s * 512: c * T + (s + 1) * 512],
                            start=True, stop=True))
                        p.step("scalar", lambda e, h=h: e.activation(
                            out=b.ebuf[:], in_=b.ps_c[:, 0:512], func=AF.Exp,
                            bias=r2c[:, h:h + 1]))
                        r = kc - 4 * s
                        if r >= 0:
                            p.step("vector", lambda e, r=r: e.tensor_mul(
                                out=b.ebuf[:], in0=b.ebuf[:],
                                in1=b.sup[:, 384 - 128 * r: 896 - 128 * r]))
                        p.step("tensor", lambda e, kc=kc, h=h, last=(kc == nkc - 1): e.matmul(
                            b.ps_a[0:65, 0:512],
                            vt[:, kc * (NH * 65) + h * 65: kc * (NH * 65) + (h + 1) * 65],
                            b.ebuf[:], start=(kc == 0), stop=last))
                    p.step("vector", lambda e: e.tensor_scalar(
                        out=b.r3[0:1, 0:512], in0=b.ps_a[64:65, 0:512], scalar1=1e-30,
                        scalar2=None, op0=ALU.add))
                    p.step("vector", lambda e: e.reciprocal(
                        out=b.r4[0:1, 0:512], in_=b.r3[0:1, 0:512]))
                    p.step("vector", lambda e: e.tensor_copy(
                        out=b.rb1[0:1, 0:512], in_=b.r4[0:1, 0:512]))
                    p.step("tensor", lambda e: e.matmul(
                        b.ps_c[0:64, 0:512], b.onesb_row[0:1, 0:64], b.rb1[0:1, 0:512],
                        start=True, stop=True))
                    p.step("vector", lambda e: e.tensor_copy(
                        out=b.scf[0:64, 0:512], in_=b.ps_c[0:64, 0:512]))
                    p.step("vector", lambda e, h=h, c=c, rb=rb: e.tensor_mul(
                        out=ost[rb:rb + 64, c * 512:(c + 1) * 512],
                        in0=b.ps_a[0:64, 0:512], in1=b.scf[0:64, 0:512]))
                for oc in range(NCH):
                    for ci in range(NCH):
                        p.step("tensor", lambda e, oc=oc, ci=ci: e.matmul(
                            b.ps_b[:, 0:512],
                            wtv[:, ci * 1024 + oc * 128: ci * 1024 + (oc + 1) * 128],
                            ost[:, ci * 512:(ci + 1) * 512],
                            start=(ci == 0), stop=(ci == NCH - 1)))
                    p.step("vector", lambda e: e.tensor_copy(
                        out=b.scf[:, 0:512], in_=b.ps_b[:, 0:512]))
                    emit_add_to_x(b, b.scf[:, 0:512], s, oc)


def emit_add_shifted(b, src_ap, s, oc, qn):
    p = b.prog
    for st in range((qn + 127) // 128):
        w = min(128, qn - st * 128)
        p.step("tensor", lambda e, st=st, w=w: e.transpose(
            b.ps_c[:, 0:w] if False else b.ps_c[0:w, 0:128],
            src_ap[:, st * 128:st * 128 + w], b.eyef[:]))
        pos0 = s * 512 + 1 + st * 128
        t0, r0 = pos0 // 128, pos0 % 128
        n1 = min(w, 128 - r0)
        p.step("vector", lambda e, t0=t0, r0=r0, n1=n1, oc=oc: e.tensor_add(
            out=b.x[r0:r0 + n1, t0 * DM + oc * 128: t0 * DM + (oc + 1) * 128],
            in0=b.x[r0:r0 + n1, t0 * DM + oc * 128: t0 * DM + (oc + 1) * 128],
            in1=b.ps_c[0:n1, 0:128]))
        if w > n1:
            n2 = w - n1
            p.step("vector", lambda e, t0=t0, n1=n1, n2=n2, oc=oc: e.tensor_add(
                out=b.x[0:n2, (t0 + 1) * DM + oc * 128: (t0 + 1) * DM + (oc + 1) * 128],
                in0=b.x[0:n2, (t0 + 1) * DM + oc * 128: (t0 + 1) * DM + (oc + 1) * 128],
                in1=b.ps_c[n1:n1 + n2, 0:128]))


def emit_l1_ffn(b, ctx, psx):
    p, di = b.prog, b.di
    with ExitStack() as ph:
        def sb(name, shape, dtype):
            return ph.enter_context(b.nc.sbuf_tensor(_uq(name), shape, dtype))
        xnst = sb("xnstf", [128, NCH * 512], BF16)
        z = sb("zrow", [8, T], F32)
        gsel = sb("gsel", [8, T], F32)
        gselb = sb("gselb", [8, T], BF16)
        selT = sb("selT", [128, NTILE * 8], F32)
        biask = sb("biask", [128, NTILE * 8], F32)
        wrT = sb("wrT", [128, NCH * 8], BF16)
        qe = sb("qe", [128, T], BF16)
        ke = sb("ke", [128, T], BF16)
        ve = sb("ve", [128, NTILE * 128], BF16)
        wqkv = sb("wqkv", [128, NCH * 384], BF16)
        wo1 = sb("wo1s", [128, DM], BF16)
        ost = sb("ostf", [128, 512], BF16)
        lo = sb("lo8", [8, 1], F32)
        hi = sb("hi8", [8, 1], F32)
        th = sb("th8", [8, 1], F32)
        cnt = sb("cnt8", [8, 1], F32)
        dput = sb("dput8", [8, 1], F32)

        p.dma("gpsimd", b.scb[0:8, 0:DM], di["wr1"])
        for fc in range(NCH):
            p.step("tensor", lambda e, fc=fc: e.transpose(
                b.ps_tb[:, 0:8], b.scb[0:8, fc * 128:(fc + 1) * 128], b.eyeb[0:8, 0:8]))
            p.step("vector", lambda e, fc=fc: e.tensor_copy(
                out=wrT[:, fc * 8:(fc + 1) * 8], in_=b.ps_tb[:, 0:8]))

        for s in range(NSTR):
            load_xn_strip(b, xnst, s)
            for fc in range(NCH):
                p.step("tensor", lambda e, fc=fc: e.matmul(
                    b.ps_c[0:8, 0:512], wrT[:, fc * 8:(fc + 1) * 8],
                    xnst[:, fc * 512:(fc + 1) * 512],
                    start=(fc == 0), stop=(fc == NCH - 1)))
            p.step("vector", lambda e, s=s: e.tensor_copy(
                out=z[:, s * 512:(s + 1) * 512], in_=b.ps_c[0:8, 0:512]))

        p.step("vector", lambda e: e.memset(lo[:], -100.0))
        p.step("vector", lambda e: e.memset(hi[:], 100.0))
        for it in range(26):
            p.step("vector", lambda e: e.tensor_add(out=th[:], in0=lo[:], in1=hi[:]))
            p.step("vector", lambda e: e.tensor_scalar(
                out=th[:], in0=th[:], scalar1=0.5, scalar2=None, op0=ALU.mult))
            p.step("vector", lambda e: e.tensor_scalar(
                out=b.scf[0:8, 0:T], in0=z[:], scalar1=th[:], scalar2=None, op0=ALU.is_ge))
            p.step("vector", lambda e: e.tensor_reduce(
                out=cnt[:], in_=b.scf[0:8, 0:T], axis=AX.X, op=ALU.add))
            p.step("vector", lambda e: e.tensor_scalar(
                out=dput[:], in0=cnt[:], scalar1=256.0, scalar2=None, op0=ALU.is_ge))
            p.step("vector", lambda e: e.tensor_sub(out=cnt[:], in0=th[:], in1=lo[:]))
            p.step("vector", lambda e: e.scalar_tensor_tensor(
                out=lo[:], in0=dput[:], scalar=cnt[:], in1=lo[:], op0=ALU.mult, op1=ALU.add))
            p.step("vector", lambda e: e.tensor_scalar(
                out=dput[:], in0=dput[:], scalar1=-1.0, scalar2=1.0, op0=ALU.mult, op1=ALU.add))
            p.step("vector", lambda e: e.tensor_sub(out=cnt[:], in0=th[:], in1=hi[:]))
            p.step("vector", lambda e: e.scalar_tensor_tensor(
                out=hi[:], in0=dput[:], scalar=cnt[:], in1=hi[:], op0=ALU.mult, op1=ALU.add))
        p.step("vector", lambda e: e.tensor_scalar(
            out=b.scf[0:8, 0:T], in0=z[:], scalar1=lo[:], scalar2=None, op0=ALU.is_ge))
        p.step("scalar", lambda e: e.activation(out=gsel[:], in_=z[:], func=AF.Sigmoid))
        p.step("vector", lambda e: e.tensor_mul(
            out=gsel[:], in0=gsel[:], in1=b.scf[0:8, 0:T]))
        p.step("vector", lambda e: e.tensor_copy(out=gselb[:], in_=gsel[:]))
        for t in range(NTILE):
            p.step("tensor", lambda e, t=t: e.transpose(
                b.ps_c[:, 0:8], b.scf[0:8, t * 128:(t + 1) * 128], b.eyef[0:8, 0:8]))
            p.step("vector", lambda e, t=t: e.tensor_copy(
                out=selT[:, t * 8:(t + 1) * 8], in_=b.ps_c[:, 0:8]))
        p.step("vector", lambda e: e.tensor_scalar(
            out=biask[:], in0=selT[:], scalar1=-1.0, scalar2=None, op0=ALU.add))
        p.step("vector", lambda e: e.tensor_scalar(
            out=biask[:], in0=biask[:], scalar1=30000.0, scalar2=None, op0=ALU.mult))
        tap(b, f"gsel{psx}", lambda: gsel[:], [8, T], F32)

        for e1 in range(NE1):
            p.dma("gpsimd", wqkv[:].rearrange("p (c n) -> p c n", n=384),
                  di["wqkv1"][e1].rearrange("(c p) n -> p c n", p=128))
            p.dma("gpsimd", wo1[:], di["wo1"][e1])
            for s in range(NSTR):
                load_xn_strip(b, xnst, s)
                for which, dst in ((0, qe), (1, ke)):
                    for fc in range(NCH):
                        p.step("tensor", lambda e, which=which, fc=fc: e.matmul(
                            b.ps_b[:, 0:512],
                            wqkv[:, fc * 384 + which * 128: fc * 384 + (which + 1) * 128],
                            xnst[:, fc * 512:(fc + 1) * 512],
                            start=(fc == 0), stop=(fc == NCH - 1)))
                    p.step("scalar", lambda e, dst=dst, s=s: e.activation(
                        out=dst[:, s * 512:(s + 1) * 512], in_=b.ps_b[:, 0:512],
                        func=AF.Identity))
                for tt in range(4):
                    t = s * 4 + tt
                    for fc in range(NCH):
                        p.step("tensor", lambda e, t=t, tt=tt, fc=fc: e.matmul(
                            b.ps_c[:, 0:128],
                            xnst[:, fc * 512 + tt * 128: fc * 512 + (tt + 1) * 128],
                            wqkv[:, fc * 384 + 256: fc * 384 + 384],
                            start=(fc == 0), stop=(fc == NCH - 1)))
                    p.step("vector", lambda e, t=t: e.tensor_copy(
                        out=ve[:, t * 128:(t + 1) * 128], in_=b.ps_c[:, 0:128]))
            emit_rope(b, qe[0:64, 0:T], T, 0, rows=64)
            emit_rope(b, ke[0:64, 0:T], T, 0, rows=64)
            if psx == 0 and e1 == 0:
                tap(b, "qe0", lambda: qe[:], [128, T], BF16)
                tap(b, "ve0", lambda: ve[:], [128, NTILE * 128], BF16)

            for s in range(NSTR):
                if s == 0:
                    p.step("scalar", lambda e: e.activation(
                        out=b.scf[:, 0:T], in_=ke[:], func=AF.Square))
                    for ss in range(NSTR):
                        p.step("tensor", lambda e, ss=ss: e.matmul(
                            b.ps_c[0:1, 0:512], b.onesf_col[:],
                            b.scf[:, ss * 512:(ss + 1) * 512], start=True, stop=True))
                        p.step("vector", lambda e, ss=ss: e.tensor_copy(
                            out=(b.r3 if ss == 0 else b.r4)[0:1, 0:512],
                            in_=b.ps_c[0:1, 0:512]))
                        if ss > 0:
                            p.step("vector", lambda e: e.tensor_max(
                                out=b.r3[:], in0=b.r3[:], in1=b.r4[:]))
                    p.step("vector", lambda e: e.tensor_reduce(
                        out=b.col[5][0:1, :], in_=b.r3[:], axis=AX.X, op=ALU.max))
                p.step("scalar", lambda e, s=s: e.activation(
                    out=b.scf[:, 0:512], in_=qe[:, s * 512:(s + 1) * 512], func=AF.Square))
                p.step("tensor", lambda e: e.matmul(
                    b.ps_c[0:1, 0:512], b.onesf_col[:], b.scf[:, 0:512],
                    start=True, stop=True))
                p.step("vector", lambda e: e.tensor_scalar(
                    out=b.r3[0:1, 0:512], in0=b.ps_c[0:1, 0:512],
                    scalar1=b.col[5][0:1, 0:1], scalar2=-0.5, op0=ALU.add, op1=ALU.mult))
                p.step("vector", lambda e: e.tensor_copy(out=b.rb1[:], in_=b.r3[0:1, 0:512]))
                nkc = 4 * s + 4
                for kc in range(nkc):
                    p.step("tensor", lambda e, kc=kc, s=s: e.matmul(
                        b.ps_c[:, 0:512],
                        ke[:, kc * 128:(kc + 1) * 128], qe[:, s * 512:(s + 1) * 512],
                        start=True, stop=False))
                    p.step("tensor", lambda e: e.matmul(
                        b.ps_c[:, 0:512], b.onesb_row[:], b.rb1[:], start=False, stop=True))
                    p.step("scalar", lambda e, kc=kc: e.activation(
                        out=b.ebuf[:], in_=b.ps_c[:, 0:512], func=AF.Exp,
                        scale=float(1.0 / np.sqrt(HP)),
                        bias=biask[:, kc * 8 + e1: kc * 8 + e1 + 1]))
                    r = kc - 4 * s
                    if r >= 0:
                        p.step("vector", lambda e, r=r: e.tensor_mul(
                            out=b.ebuf[:], in0=b.ebuf[:],
                            in1=b.sup[:, 385 - 128 * r: 897 - 128 * r]))
                    p.step("tensor", lambda e, kc=kc, last=(kc == nkc - 1): e.matmul(
                        b.ps_a[:, 0:512], ve[:, kc * 128:(kc + 1) * 128], b.ebuf[:],
                        start=(kc == 0), stop=last))
                    p.step("tensor", lambda e, kc=kc, last=(kc == nkc - 1): e.matmul(
                        b.ps_b[0:1, 0:512], b.onesb_col[:], b.ebuf[:],
                        start=(kc == 0), stop=last))
                p.step("vector", lambda e: e.tensor_scalar(
                    out=b.r3[0:1, 0:512], in0=b.ps_b[0:1, 0:512], scalar1=1e-30,
                    scalar2=None, op0=ALU.add))
                p.step("vector", lambda e: e.reciprocal(out=b.r4[0:1, 0:512], in_=b.r3[0:1, 0:512]))
                p.step("vector", lambda e: e.tensor_copy(out=b.rb1[:], in_=b.r4[0:1, 0:512]))
                p.step("tensor", lambda e: e.matmul(
                    b.ps_c[:, 0:512], b.onesb_row[:], b.rb1[:], start=True, stop=True))
                p.step("vector", lambda e: e.tensor_copy(
                    out=b.scf[:, 0:512], in_=b.ps_c[:, 0:512]))
                p.step("tensor", lambda e, e1=e1, s=s: e.matmul(
                    b.ps_c[:, 0:512], b.e8[:, e1 * 128:(e1 + 1) * 128],
                    gselb[:, s * 512:(s + 1) * 512], start=True, stop=True))
                p.step("vector", lambda e: e.tensor_mul(
                    out=b.scf[:, 0:512], in0=b.scf[:, 0:512], in1=b.ps_c[:, 0:512]))
                p.step("vector", lambda e: e.tensor_mul(
                    out=ost[:, 0:512], in0=b.ps_a[:, 0:512], in1=b.scf[:, 0:512]))
                for ocp in range(4):
                    for j2 in range(2):
                        oc = ocp * 2 + j2
                        p.step("tensor", lambda e, oc=oc, j2=j2: e.matmul(
                            b.ps_b[:, j2 * 512:(j2 + 1) * 512],
                            wo1[:, oc * 128:(oc + 1) * 128], ost[:, 0:512],
                            start=True, stop=True))
                        p.step("vector", lambda e, j2=j2: e.tensor_copy(
                            out=b.scf[:, 512:1024], in_=b.ps_b[:, j2 * 512:(j2 + 1) * 512]))
                        emit_add_to_x(b, b.scf[:, 512:1024], s, oc)


_CACHE = {}


def kernel(**inputs):
    taps = inputs.pop("_taps", ())
    blocks = inputs.pop("_blocks", (0, 1))
    key = (tuple(taps), tuple(blocks))
    if key not in _CACHE:
        _CACHE[key] = build_program(blocks=blocks, taps=taps)
    nc, b = _CACHE[key]

    hc = host_consts()
    wmap = {
        "wq": inputs["L0_attn_W_q"], "wdkv": inputs["L0_attn_W_dkv"],
        "wuk": inputs["L0_attn_W_uk"], "wuv": inputs["L0_attn_W_uv"],
        "wo": inputs["L0_attn_W_o"],
        "lng": inputs["L0_attn_ln_gamma"].reshape(1, LAT),
        "lnb": inputs["L0_attn_ln_beta"].reshape(1, LAT),
        "rw": inputs["L0_ffn_router_w"], "rb": inputs["L0_ffn_router_b"].reshape(1, NE0),
        "w1": inputs["L0_ffn_w1"], "b1": inputs["L0_ffn_b1"],
        "w2": inputs["L0_ffn_w2"], "b2": inputs["L0_ffn_b2"],
        "wk1": inputs["L1_attn_W_k"], "wv1": inputs["L1_attn_W_v"],
        "wc1": inputs["L1_attn_W_c"],
        "lkb": inputs["L1_attn_leaky_key_beta"].reshape(1, NH),
        "ksc": inputs["L1_attn_key_scale"].reshape(1, NH),
        "vcf": inputs["L1_attn_v_coef"].reshape(1, NH),
        "vsc": inputs["L1_attn_val_scale"].reshape(1, NH),
        "wr1": inputs["L1_ffn_Wr"], "wqkv1": inputs["L1_ffn_Wqkv"],
        "wo1": inputs["L1_ffn_Wo"],
    }
    wmap.update(hc)
    x = inputs["L0_attn_x"]
    in_maps = []
    for c in range(8):
        m = dict(wmap)
        m["xin"] = np.ascontiguousarray(x[[c, c + 8]])
        in_maps.append(m)
    res = run_bass_kernel_spmd(nc, in_maps, core_ids=list(range(8)))
    out = np.empty((16, T, DM), dtype=np.float32)
    for c in range(8):
        out[c] = res.results[c]["out"][0]
        out[c + 8] = res.results[c]["out"][1]
    if taps:
        tapout = {nm: [res.results[c]["tap_" + nm] for c in range(8)] for nm in b.taps}
        return out, tapout
    return out


# version 03
# speedup vs baseline: 1.6922x; 1.6922x over previous
import numpy as np
import ml_dtypes
from contextlib import ExitStack

import concourse.bass as bass
import concourse.mybir as mybir
from concourse.bass_utils import run_bass_kernel_spmd

F32 = mybir.dt.float32
BF16 = mybir.dt.bfloat16
I32 = mybir.dt.int32
AF = mybir.ActivationFunctionType
ALU = mybir.AluOpType
AX = mybir.AxisListType

T = 2048
DM = 1024
NH = 16
HD = 64
LAT = 256
NE0 = 16
IM = 2048
NE1 = 8
HP = 128
KSMAX = float(np.log(2.0 ** 16 - 1.0))
NTILE = T // 128
NCH = DM // 128
NSTR = T // 512
NM = IM // 128

ENGINES = ("sync", "gpsimd", "vector", "scalar", "tensor")


class Prog:
    def __init__(self):
        self.steps = []

    def step(self, eng, fn, inc=1):
        self.steps.append((eng, fn, inc))

    def dma(self, eng, out, in_):
        self.step(eng, lambda e, o=out, i=in_: e.dma_start(out=o, in_=i), 16)

    def build(self, block, sem):
        waits = []
        total = 0
        for _, _, inc in self.steps:
            waits.append(total)
            total += inc
        groups = []
        MERGEABLE = ("tensor", "sync", "gpsimd")
        for i, (e, fn, inc) in enumerate(self.steps):
            if groups and groups[-1][0] == e and e in MERGEABLE:
                groups[-1][2].append((fn, inc))
            else:
                groups.append((e, waits[i], [(fn, inc)]))
        per_engine = {e: [] for e in ENGINES}
        for e, w, members in groups:
            per_engine[e].append((w, members))
        for eng_name in ENGINES:
            if not per_engine[eng_name]:
                continue
            is_dma = eng_name in ("sync", "gpsimd")

            def make_body(mygroups, is_dma):
                def body(eng):
                    for w, members in mygroups:
                        if w > 0:
                            eng.wait_ge(sem, w)
                        if is_dma:
                            for fn, inc in members:
                                fn(eng).then_inc(sem, inc)
                        else:
                            tot = sum(inc for _, inc in members)
                            for j, (fn, inc) in enumerate(members):
                                inst = fn(eng)
                                if j == len(members) - 1:
                                    inst.then_inc(sem, tot)
                return body
            getattr(block, eng_name)(make_body(per_engine[eng_name], is_dma))
        return total


def host_consts():
    j = np.arange(128)
    u = np.arange(897)
    c = {}
    c["c_eye"] = np.eye(128, dtype=np.float32)
    c["c_sup"] = (j[:, None] <= (u[None, :] - 385)).astype(np.float32)
    c["c_dmat"] = (j[None, :] - j[:, None]).astype(np.float32)
    c["c_iota_row"] = np.arange(T, dtype=np.float32)[None, :]
    c["c_iota_col"] = np.arange(128, dtype=np.float32)[:, None]
    e16 = np.zeros((16, 16 * 128), dtype=np.float32)
    for e in range(16):
        e16[e, e * 128:(e + 1) * 128] = 1.0
    c["c_e16"] = e16
    e8 = np.zeros((8, 8 * 128), dtype=np.float32)
    for e in range(8):
        e8[e, e * 128:(e + 1) * 128] = 1.0
    c["c_e8"] = e8
    return c


class B:
    pass


_UID = [0]


def _uq(name):
    _UID[0] += 1
    return f"{name}_u{_UID[0]}"




def build_program(blocks=(0, 1), n_pass=2, taps=()):
    nc = bass.Bass("TRN2", target_bir_lowering=False)
    b = B()
    b.nc = nc
    b.prog = Prog()
    b.taps = {}
    b.tapset = set(taps)
    p = b.prog

    di = {}
    di["xin"] = nc.dram_tensor("xin", [n_pass, T, DM], F32, kind="ExternalInput").ap()
    for nm, shp in [
        ("wq", [DM, DM]), ("wdkv", [DM, LAT]), ("wuk", [LAT, DM]), ("wuv", [LAT, DM]),
        ("wo", [DM, DM]), ("lng", [1, LAT]), ("lnb", [1, LAT]),
        ("rw", [DM, NE0]), ("rb", [1, NE0]),
        ("w1", [NE0, DM, IM]), ("b1", [NE0, IM]), ("w2", [NE0, IM, DM]), ("b2", [NE0, DM]),
        ("wk1", [DM, DM]), ("wv1", [DM, DM]), ("wc1", [DM, DM]),
        ("lkb", [1, NH]), ("ksc", [1, NH]), ("vcf", [1, NH]), ("vsc", [1, NH]),
        ("wr1", [NE1, DM]), ("wqkv1", [NE1, DM, 3 * HP]), ("wo1", [NE1, HP, DM]),
        ("c_eye", [128, 128]), ("c_sup", [128, 897]), ("c_dmat", [128, 128]),
        ("c_iota_row", [1, T]), ("c_iota_col", [128, 1]),
        ("c_e16", [16, 2048]), ("c_e8", [8, 1024]),
    ]:
        di[nm] = nc.dram_tensor(nm, shp, F32, kind="ExternalInput").ap()
    b.di = di
    b.dout = nc.dram_tensor("out", [n_pass, T, DM], F32, kind="ExternalOutput").ap()
    b.d_xn = nc.dram_tensor("d_xn", [NCH, 128, T], BF16).ap()
    b.d_w1b = nc.dram_tensor("d_w1b", [NE0, 128, NCH * IM], BF16).ap()
    b.d_w2b = nc.dram_tensor("d_w2b", [NE0, 128, NM * DM], BF16).ap()

    with ExitStack() as ctx:
        def sb(name, shape, dtype):
            return ctx.enter_context(nc.sbuf_tensor(name, shape, dtype))
        b.sb_top = sb

        b.x = sb("x", [128, NTILE * DM], F32)
        b.chc = sb("chc", [128, T], BF16)
        b.chs = sb("chs", [128, T], BF16)
        b.eyeb = sb("eyeb", [128, 128], BF16)
        b.eyef = sb("eyef", [128, 128], F32)
        b.sup = sb("supm", [128, 897], BF16)
        b.dmat = sb("dmatc", [128, 128], F32)
        b.e16 = sb("e16c", [16, 2048], BF16)
        b.e8 = sb("e8c", [8, 1024], BF16)
        b.onesb_row = sb("onesb_row", [1, 128], BF16)
        b.onesf_row = sb("onesf_row", [1, 128], F32)
        b.onesb_col = sb("onesb_col", [128, 1], BF16)
        b.onesf_col = sb("onesf_col", [128, 1], F32)
        b.iotac = sb("iotac", [128, 1], F32)
        b.r3 = sb("r3", [1, 512], F32)
        b.r4 = sb("r4", [1, 512], F32)
        b.rb1 = sb("rb1", [1, 512], BF16)
        b.t16 = sb("t16a", [1, 16], F32)
        b.t16b = sb("t16b", [1, 16], F32)
        b.t16c = sb("t16c", [1, 16], F32)
        b.col = [sb(f"col{i}", [128, 1], F32) for i in range(8)]
        b.eps5 = sb("eps5", [128, 1], F32)
        b.scf = sb("scf", [128, T], F32)
        b.scb = sb("scb", [128, T], BF16)
        b.scb2 = sb("scb2", [128, T], BF16)
        b.ebuf = sb("ebuf", [128, 512], BF16)

        b.ps_a = ctx.enter_context(nc.psum_tensor("ps_a", [128, 2048], F32))
        b.ps_b = ctx.enter_context(nc.psum_tensor("ps_b", [128, 1024], F32))
        b.ps_c = ctx.enter_context(nc.psum_tensor("ps_c", [128, 512], F32))
        b.ps_tb = ctx.enter_context(nc.psum_tensor("ps_tb", [128, 512], BF16))

        sem = ctx.enter_context(nc.semaphore("g"))
        block = ctx.enter_context(nc.Block())

        emit_prologue(b, ctx)
        for psx in range(n_pass):
            for t in range(NTILE):
                p.dma("sync", b.x[:, t * DM:(t + 1) * DM], di["xin"][psx, t * 128:(t + 1) * 128, :])
            import os
            skip = set(os.environ.get("K_SKIP", "").split(","))
            for blk in blocks:
                emit_ln(b, ctx, psx, blk)
                if blk == 0:
                    if "l0attn" not in skip:
                        emit_l0_attn(b, ctx, psx)
                    if "l0moe" not in skip:
                        emit_l0_moe(b, ctx, psx)
                else:
                    if "l1attn" not in skip:
                        emit_l1_attn(b, ctx, psx)
                    if "l1ffn" not in skip:
                        emit_l1_ffn(b, ctx, psx)
            for t in range(NTILE):
                p.dma("sync", b.dout[psx, t * 128:(t + 1) * 128, :], b.x[:, t * DM:(t + 1) * DM])

        total = p.build(block, sem)

        @block.sync
        def _(sync):
            sync.wait_ge(sem, total)

    return nc, b


def tap(b, name, ap_fn, shape, dtype=F32):
    if name not in b.tapset or name in b.taps:
        return
    t = b.nc.dram_tensor("tap_" + name, list(shape), dtype, kind="ExternalOutput").ap()
    b.taps[name] = (list(shape), dtype)
    b.prog.dma("sync", t, ap_fn())


def emit_prologue(b, ctx):
    import os
    skip = set(os.environ.get("K_SKIP", "").split(","))
    p, di = b.prog, b.di
    p.dma("gpsimd", b.eyeb[:], di["c_eye"])
    if "eyef" not in skip:
        p.dma("sync", b.eyef[:], di["c_eye"])
    if "sup" not in skip:
        p.dma("gpsimd", b.sup[:], di["c_sup"])
    if "dmatx" not in skip:
        p.dma("sync", b.dmat[:], di["c_dmat"])
    if "e16" not in skip:
        p.dma("gpsimd", b.e16[:], di["c_e16"])
    if "e8" not in skip:
        p.dma("gpsimd", b.e8[:], di["c_e8"])
    if "iotac" not in skip:
        p.dma("sync", b.iotac[:], di["c_iota_col"])
    if "consts" in skip:
        p.step("vector", lambda e: e.memset(b.onesb_row[:], 1.0))
        p.step("vector", lambda e: e.memset(b.onesf_row[:], 1.0))
        p.step("vector", lambda e: e.memset(b.onesb_col[:], 1.0))
        p.step("vector", lambda e: e.memset(b.onesf_col[:], 1.0))
        p.step("vector", lambda e: e.memset(b.eps5[:], 1e-5))
        return
    p.step("vector", lambda e: e.memset(b.onesb_row[:], 1.0))
    p.step("vector", lambda e: e.memset(b.onesf_row[:], 1.0))
    p.step("vector", lambda e: e.memset(b.onesb_col[:], 1.0))
    p.step("vector", lambda e: e.memset(b.onesf_col[:], 1.0))
    p.step("vector", lambda e: e.memset(b.eps5[:], 1e-5))

    import os
    if "rope" in set(os.environ.get("K_SKIP", "").split(",")):
        return
    O2PI = float(1.0 / (2 * np.pi))
    TWO_PI = float(2 * np.pi)
    LN10K = float(np.log(10000.0))
    with ExitStack() as ph:
        iotar = ph.enter_context(b.nc.sbuf_tensor(_uq("iotar"), [1, T], F32))
        ang = ph.enter_context(b.nc.sbuf_tensor(_uq("ang"), [32, T], F32))
        tmpi = ph.enter_context(b.nc.sbuf_tensor(_uq("tmpi"), [32, T], I32))
        tmpf = ph.enter_context(b.nc.sbuf_tensor(_uq("tmpf"), [32, T], F32))
        invf = b.col[7]
        p.dma("sync", iotar[:], di["c_iota_row"])
        p.step("scalar", lambda e: e.activation(
            out=invf[0:32, :], in_=b.iotac[0:32, :], func=AF.Exp, scale=-LN10K / 32.0))
        for s in range(NSTR):
            p.step("tensor", lambda e, s=s: e.matmul(
                b.ps_a[0:32, s * 512:(s + 1) * 512], b.onesf_row[0:1, 0:32],
                iotar[0:1, s * 512:(s + 1) * 512], start=True, stop=True))

        def reduce_sin(dst, phase):
            p.step("vector", lambda e: e.tensor_scalar(
                out=ang[:], in0=b.ps_a[0:32, 0:T], scalar1=invf[0:32, :],
                scalar2=phase, op0=ALU.mult, op1=ALU.add))
            p.step("vector", lambda e: e.tensor_scalar(
                out=tmpi[:], in0=ang[:], scalar1=O2PI, scalar2=None, op0=ALU.mult))
            p.step("vector", lambda e: e.tensor_copy(out=tmpf[:], in_=tmpi[:]))
            p.step("vector", lambda e: e.scalar_tensor_tensor(
                out=ang[:], in0=tmpf[:], scalar=-TWO_PI, in1=ang[:],
                op0=ALU.mult, op1=ALU.add))
            p.step("scalar", lambda e, dst=dst: e.activation(out=dst, in_=ang[:], func=AF.Sin))

        reduce_sin(b.scb[0:32, 0:T], 0.0)
        reduce_sin(b.scb2[0:32, 0:T], float(np.pi / 2))
        p.step("vector", lambda e: e.tensor_scalar(
            out=b.scb[32:64, 0:T], in0=b.scb[0:32, 0:T], scalar1=-1.0, scalar2=None,
            op0=ALU.mult))
        for r in range(4):
            p.dma("sync", b.chc[32 * r:32 * (r + 1), 0:T], b.scb2[0:32, 0:T])
            p.dma("sync", b.chs[32 * r:32 * (r + 1), 0:T],
                  b.scb[32:64, 0:T] if r % 2 == 0 else b.scb[0:32, 0:T])
    tap(b, "ropec", lambda: b.chc[:], [128, T], BF16)
    tap(b, "ropes", lambda: b.chs[:], [128, T], BF16)


def emit_ln(b, ctx, psx, blk):
    p = b.prog
    with ExitStack() as ph:
        stage = ph.enter_context(b.nc.sbuf_tensor(_uq("lnstage"), [128, NCH * 512], BF16))
        for s in range(NSTR):
            for tt in range(4):
                t = s * 4 + tt
                xt = b.x[:, t * DM:(t + 1) * DM]
                c0, c1, mu, var, rstd = b.col[0], b.col[1], b.col[2], b.col[3], b.col[4]
                p.step("scalar", lambda e, xt=xt, c0=c0: e.activation(
                    out=b.scf[:, 0:DM], in_=xt, func=AF.Square, accum_out=c0[:]))
                p.step("scalar", lambda e, xt=xt, c1=c1: e.activation(
                    out=b.scf[:, DM:2 * DM], in_=xt, func=AF.Identity, accum_out=c1[:]))
                p.step("vector", lambda e, c1=c1, mu=mu: e.tensor_scalar(
                    out=mu[:], in0=c1[:], scalar1=1.0 / DM, scalar2=None, op0=ALU.mult))
                p.step("vector", lambda e, mu=mu, var=var: e.tensor_mul(
                    out=var[:], in0=mu[:], in1=mu[:]))
                p.step("vector", lambda e, c0=c0, var=var: e.scalar_tensor_tensor(
                    out=var[:], in0=c0[:], scalar=1.0 / DM, in1=var[:],
                    op0=ALU.mult, op1=ALU.subtract))
                p.step("scalar", lambda e, var=var, rstd=rstd: e.activation(
                    out=rstd[:], in_=var[:], func=AF.Sqrt, bias=b.eps5[:]))
                p.step("vector", lambda e, rstd=rstd: e.reciprocal(out=rstd[:], in_=rstd[:]))
                p.step("vector", lambda e, xt=xt, mu=mu, rstd=rstd: e.tensor_scalar(
                    out=b.scb[:, 0:DM], in0=xt, scalar1=mu[:], scalar2=rstd[:],
                    op0=ALU.subtract, op1=ALU.mult))
                if t == 0:
                    tap(b, f"lnx_b{blk}_p{psx}", lambda xt=xt: xt, [128, DM], F32)
                    tap(b, f"lnc0_b{blk}_p{psx}", lambda c0=c0: c0[:], [128, 1], F32)
                    tap(b, f"lnmu_b{blk}_p{psx}", lambda mu=mu: mu[:], [128, 1], F32)
                    tap(b, f"lnrs_b{blk}_p{psx}", lambda rstd=rstd: rstd[:], [128, 1], F32)
                    tap(b, f"lnscb_b{blk}_p{psx}", lambda: b.scb[:, 0:DM], [128, DM], BF16)
                for c in range(NCH):
                    p.step("tensor", lambda e, c=c: e.transpose(
                        b.ps_tb[:, 0:128], b.scb[:, c * 128:(c + 1) * 128], b.eyeb[:]))
                    p.step("vector", lambda e, c=c, tt=tt: e.tensor_copy(
                        out=stage[:, c * 512 + tt * 128: c * 512 + (tt + 1) * 128],
                        in_=b.ps_tb[:, 0:128]))
            for c in range(NCH):
                p.dma("sync", b.d_xn[c, :, s * 512:(s + 1) * 512],
                      stage[:, c * 512:(c + 1) * 512])
    tap(b, f"xn_b{blk}_p{psx}", lambda: b.scb[:, 0:DM], [128, DM], BF16)


def load_xn_strip(b, dst, s):
    b.prog.dma("sync", dst[:, 0:NCH * 512].rearrange("p (c n) -> p c n", n=512),
               b.d_xn[:, :, s * 512:(s + 1) * 512].rearrange("c p n -> p c n"))


def load_xn_tile(b, dst, t):
    b.prog.dma("sync", dst[:, 0:NCH * 128].rearrange("p (c n) -> p c n", n=128),
               b.d_xn[:, :, t * 128:(t + 1) * 128].rearrange("c p n -> p c n"))


def emit_rope(b, chunk_ap, width, pos0, rows=128):
    p = b.prog
    ngr = rows // 32
    for dg in range(ngr):
        sg = dg ^ 1
        p.step("vector", lambda e, dg=dg, sg=sg: e.tensor_copy(
            out=b.scb2[32 * dg:32 * (dg + 1), 0:width],
            in_=chunk_ap[32 * sg:32 * (sg + 1), 0:width]))
    p.step("vector", lambda e: e.tensor_mul(
        out=b.scb[0:rows, 0:width], in0=chunk_ap, in1=b.chc[0:rows, pos0:pos0 + width]))
    p.step("vector", lambda e: e.tensor_mul(
        out=b.scb2[0:rows, 0:width], in0=b.scb2[0:rows, 0:width],
        in1=b.chs[0:rows, pos0:pos0 + width]))
    p.step("vector", lambda e: e.tensor_add(
        out=chunk_ap, in0=b.scb[0:rows, 0:width], in1=b.scb2[0:rows, 0:width]))


def emit_add_to_x(b, src_f32_ap, s, oc, width=512):
    p = b.prog
    nt = width // 128
    for st in range(nt):
        t = s * 4 + st
        p.step("tensor", lambda e, st=st: e.transpose(
            b.ps_c[:, 0:128], src_f32_ap[:, st * 128:(st + 1) * 128], b.eyef[:]))
        p.step("vector", lambda e, t=t, oc=oc: e.tensor_add(
            out=b.x[:, t * DM + oc * 128: t * DM + (oc + 1) * 128],
            in0=b.x[:, t * DM + oc * 128: t * DM + (oc + 1) * 128],
            in1=b.ps_c[:, 0:128]))


def emit_l0_attn(b, ctx, psx):
    p, di = b.prog, b.di
    with ExitStack() as ph:
        def sb(name, shape, dtype):
            return ph.enter_context(b.nc.sbuf_tensor(_uq(name), shape, dtype))
        kbuf = sb("l0k", [128, NCH * T], BF16)
        vt = sb("l0vt", [128, NTILE * NH * 65], BF16)

        with ExitStack() as ph2:
            def sb2(name, shape, dtype):
                return ph2.enter_context(b.nc.sbuf_tensor(_uq(name), shape, dtype))
            latn = sb2("l0latn", [128, 2 * T], BF16)
            wdkv = sb2("wdkv", [128, NCH * LAT], BF16)
            wuk = sb2("wuk", [128, 2 * DM], BF16)
            wuv = sb2("wuv", [128, 2 * DM], BF16)
            gcol = sb2("lngc", [128, 2], F32)
            bcol = sb2("lnbc", [128, 2], F32)
            latf = sb2("latf", [128, 1024], F32)
            xnst = sb2("xnst0", [128, NCH * 512], BF16)

            p.dma("gpsimd", wdkv[:].rearrange("p (c l) -> p c l", l=LAT),
                  di["wdkv"].rearrange("(c p) l -> p c l", p=128))
            p.dma("gpsimd", wuk[:].rearrange("p (c d) -> p c d", d=DM),
                  di["wuk"].rearrange("(c p) d -> p c d", p=128))
            p.dma("gpsimd", wuv[:].rearrange("p (c d) -> p c d", d=DM),
                  di["wuv"].rearrange("(c p) d -> p c d", p=128))
            p.dma("sync", b.scf[0:2, 0:128], di["lng"].rearrange("o (c p) -> (o c) p", p=128))
            p.dma("sync", b.scf[32:34, 0:128], di["lnb"].rearrange("o (c p) -> (o c) p", p=128))
            p.step("tensor", lambda e: e.transpose(b.ps_c[:, 0:2], b.scf[0:2, 0:128], b.eyef[0:2, 0:2]))
            p.step("vector", lambda e: e.tensor_copy(out=gcol[:], in_=b.ps_c[:, 0:2]))
            p.step("tensor", lambda e: e.transpose(b.ps_c[:, 0:2], b.scf[32:34, 0:128], b.eyef[32:34, 32:34]))
            p.step("vector", lambda e: e.tensor_copy(out=bcol[:], in_=b.ps_c[:, 0:2]))
            tap(b, f"wdkvt{psx}", lambda: wdkv[:], [128, NCH * LAT], BF16)
            p.step("vector", lambda e: e.memset(
                vt[:].rearrange("p (t h c) -> p t h c", h=NH, c=65)[:, :, :, 64:65], 1.0))

            for s in range(NSTR):
                load_xn_strip(b, xnst, s)
                if s == 0:
                    tap(b, f"xnstA{psx}", lambda: xnst[:], [128, NCH * 512], BF16)
                for lc in range(2):
                    for fc in range(NCH):
                        p.step("tensor", lambda e, lc=lc, fc=fc: e.matmul(
                            b.ps_b[:, lc * 512:(lc + 1) * 512],
                            wdkv[:, fc * LAT + lc * 128: fc * LAT + (lc + 1) * 128],
                            xnst[:, fc * 512:(fc + 1) * 512],
                            start=(fc == 0), stop=(fc == NCH - 1)))
                p.step("vector", lambda e: e.tensor_copy(out=latf[:], in_=b.ps_b[:, 0:1024]))
                if s == 0:
                    tap(b, f"latraw{psx}", lambda: latf[:], [128, 1024], F32)
                p.step("scalar", lambda e: e.activation(
                    out=b.scf[:, 0:1024], in_=latf[:], func=AF.Square))
                for lc in range(2):
                    p.step("tensor", lambda e, lc=lc: e.matmul(
                        b.ps_c[0:1, 0:512], b.onesf_col[:], latf[:, lc * 512:(lc + 1) * 512],
                        start=(lc == 0), stop=(lc == 1)))
                p.step("vector", lambda e: e.tensor_scalar(
                    out=b.r3[:], in0=b.ps_c[0:1, 0:512], scalar1=1.0 / LAT, scalar2=None,
                    op0=ALU.mult))
                if s == 0:
                    tap(b, f"latsum{psx}", lambda: b.r3[:], [1, 512], F32)
                for lc in range(2):
                    p.step("tensor", lambda e, lc=lc: e.matmul(
                        b.ps_c[0:1, 0:512], b.onesf_col[:], b.scf[:, lc * 512:(lc + 1) * 512],
                        start=(lc == 0), stop=(lc == 1)))
                p.step("vector", lambda e: e.tensor_mul(out=b.r4[:], in0=b.r3[:], in1=b.r3[:]))
                p.step("vector", lambda e: e.scalar_tensor_tensor(
                    out=b.r4[:], in0=b.ps_c[0:1, 0:512], scalar=1.0 / LAT, in1=b.r4[:],
                    op0=ALU.mult, op1=ALU.subtract))
                p.step("scalar", lambda e: e.activation(out=b.r4[:], in_=b.r4[:], func=AF.Sqrt, bias=b.eps5[0:1, 0:1]))
                p.step("vector", lambda e: e.reciprocal(out=b.r4[:], in_=b.r4[:]))
                p.step("tensor", lambda e: e.matmul(
                    b.ps_c[:, 0:512], b.onesf_row[0:1, :], b.r3[:], start=True, stop=True))
                for lc in range(2):
                    p.step("vector", lambda e, lc=lc: e.tensor_sub(
                        out=latf[:, lc * 512:(lc + 1) * 512],
                        in0=latf[:, lc * 512:(lc + 1) * 512], in1=b.ps_c[:, 0:512]))
                p.step("tensor", lambda e: e.matmul(
                    b.ps_c[:, 0:512], b.onesf_row[0:1, :], b.r4[:], start=True, stop=True))
                for lc in range(2):
                    p.step("vector", lambda e, lc=lc: e.tensor_mul(
                        out=latf[:, lc * 512:(lc + 1) * 512],
                        in0=latf[:, lc * 512:(lc + 1) * 512], in1=b.ps_c[:, 0:512]))
                for lc in range(2):
                    p.step("vector", lambda e, lc=lc, s=s: e.tensor_scalar(
                        out=latn[:, lc * T + s * 512: lc * T + (s + 1) * 512],
                        in0=latf[:, lc * 512:(lc + 1) * 512],
                        scalar1=gcol[:, lc:lc + 1], scalar2=bcol[:, lc:lc + 1],
                        op0=ALU.mult, op1=ALU.add))
                if s == 0:
                    tap(b, f"latmu{psx}", lambda: b.r3[:], [1, 512], F32)
                    tap(b, f"latrs{psx}", lambda: b.r4[:], [1, 512], F32)
                    tap(b, f"latf{psx}", lambda: latf[:], [128, 1024], F32)
                    tap(b, f"gcol{psx}", lambda: gcol[:], [128, 2], F32)
                    tap(b, f"bcol{psx}", lambda: bcol[:], [128, 2], F32)
            tap(b, f"latn{psx}", lambda: latn[:], [128, 2 * T], BF16)

            for c in range(NCH):
                for s in range(NSTR):
                    for lc in range(2):
                        p.step("tensor", lambda e, c=c, s=s, lc=lc: e.matmul(
                            b.ps_c[:, 0:512],
                            wuk[:, lc * DM + c * 128: lc * DM + (c + 1) * 128],
                            latn[:, lc * T + s * 512: lc * T + (s + 1) * 512],
                            start=(lc == 0), stop=(lc == 1)))
                    p.step("scalar", lambda e, c=c, s=s: e.activation(
                        out=kbuf[:, c * T + s * 512: c * T + (s + 1) * 512],
                        in_=b.ps_c[:, 0:512], func=AF.Identity))
            for c in range(NCH):
                emit_rope(b, kbuf[:, c * T:(c + 1) * T], T, 0)
            tap(b, f"k0_{psx}", lambda: kbuf[:, 0:T], [128, T], BF16)

            for tt in range(NTILE):
                for half in range(2):
                    for lc in range(2):
                        p.step("tensor", lambda e, tt=tt, half=half, lc=lc: e.matmul(
                            b.ps_b[:, half * 512:(half + 1) * 512],
                            latn[:, lc * T + tt * 128: lc * T + (tt + 1) * 128],
                            wuv[:, lc * DM + half * 512: lc * DM + (half + 1) * 512],
                            start=(lc == 0), stop=(lc == 1)))
                p.step("vector", lambda e, tt=tt: e.tensor_copy(
                    out=vt[:, tt * (NH * 65):(tt + 1) * (NH * 65)]
                    .rearrange("p (h c) -> p h c", c=65)[:, :, 0:64],
                    in_=b.ps_b[:, 0:1024].rearrange("p (h c) -> p h c", c=64)))
            tap(b, f"v0_{psx}", lambda: vt[:, 0:NH * 65], [128, NH * 65], BF16)

        with ExitStack() as ph2:
            def sb2(name, shape, dtype):
                return ph2.enter_context(b.nc.sbuf_tensor(_uq(name), shape, dtype))
            xnstb = sb2("xnst0b", [128, NCH * 512], BF16)
            qst = sb2("qst", [128, NCH * 512], BF16)
            ost = sb2("ost", [128, NCH * 512], BF16)
            wqp = sb2("wqp", [128, NCH * 256], BF16)
            wop = sb2("wop", [128, NCH * 256], BF16)
            nk2 = b.t16

            for h in range(NH):
                c, rb = h // 2, (h % 2) * 64
                p.step("scalar", lambda e, c=c, rb=rb: e.activation(
                    out=b.scf[0:64, 0:T], in_=kbuf[rb:rb + 64, c * T:(c + 1) * T],
                    func=AF.Square))
                for s in range(NSTR):
                    p.step("tensor", lambda e, s=s: e.matmul(
                        b.ps_b[0:1, s * 256:s * 256 + 512] if False else b.ps_c[0:1, 0:512],
                        b.onesf_col[0:64, :], b.scf[0:64, s * 512:(s + 1) * 512],
                        start=True, stop=True))
                    p.step("vector", lambda e, s=s: e.tensor_copy(
                        out=b.r3[0:1, 0:512] if s == 0 else b.r4[0:1, 0:512],
                        in_=b.ps_c[0:1, 0:512]))
                    if s > 0:
                        p.step("vector", lambda e: e.tensor_max(
                            out=b.r3[:], in0=b.r3[:], in1=b.r4[:]))
                p.step("vector", lambda e, h=h: e.tensor_reduce(
                    out=nk2[0:1, h:h + 1], in_=b.r3[:], axis=AX.X, op=ALU.max))

            for s in range(NSTR):
                load_xn_strip(b, xnstb, s)
                for cp in range(4):
                    p.dma("gpsimd", wqp[:].rearrange("p (c n) -> p c n", n=256),
                          di["wq"].rearrange("(c p) n -> p c n", p=128)[:, :, cp * 256:(cp + 1) * 256])
                    for j2 in range(2):
                        for fc in range(NCH):
                            p.step("tensor", lambda e, j2=j2, fc=fc: e.matmul(
                                b.ps_b[:, j2 * 512:(j2 + 1) * 512],
                                wqp[:, fc * 256 + j2 * 128: fc * 256 + (j2 + 1) * 128],
                                xnstb[:, fc * 512:(fc + 1) * 512],
                                start=(fc == 0), stop=(fc == NCH - 1)))
                        p.step("scalar", lambda e, cp=cp, j2=j2: e.activation(
                            out=qst[:, (cp * 2 + j2) * 512:(cp * 2 + j2 + 1) * 512],
                            in_=b.ps_b[:, j2 * 512:(j2 + 1) * 512], func=AF.Identity))
                for c in range(NCH):
                    emit_rope(b, qst[:, c * 512:(c + 1) * 512], 512, s * 512)
                if psx == 0 and s == 0:
                    tap(b, "q0", lambda: qst[:, 0:NCH * 512], [128, NCH * 512], BF16)

                for h in range(NH):
                    c, rb = h // 2, (h % 2) * 64
                    qsl = qst[rb:rb + 64, c * 512:(c + 1) * 512]
                    p.step("scalar", lambda e, qsl=qsl: e.activation(
                        out=b.scf[0:64, 0:512], in_=qsl, func=AF.Square))
                    p.step("tensor", lambda e: e.matmul(
                        b.ps_c[0:1, 0:512], b.onesf_col[0:64, :], b.scf[0:64, 0:512],
                        start=True, stop=True))
                    p.step("vector", lambda e, h=h: e.tensor_scalar(
                        out=b.r3[:], in0=b.ps_c[0:1, 0:512], scalar1=nk2[0:1, h:h + 1],
                        scalar2=-0.5, op0=ALU.add, op1=ALU.mult))
                    p.step("vector", lambda e: e.tensor_copy(out=b.rb1[:], in_=b.r3[:]))
                    nkc = 4 * s + 4
                    for kc in range(nkc):
                        p.step("tensor", lambda e, h=h, kc=kc, c=c, rb=rb: e.matmul(
                            b.ps_c[:, 0:512],
                            kbuf[rb:rb + 64, c * T + kc * 128: c * T + (kc + 1) * 128],
                            qst[rb:rb + 64, c * 512:(c + 1) * 512],
                            start=True, stop=False))
                        p.step("tensor", lambda e: e.matmul(
                            b.ps_c[:, 0:512], b.onesb_row[:], b.rb1[:],
                            start=False, stop=True))
                        p.step("scalar", lambda e: e.activation(
                            out=b.ebuf[:], in_=b.ps_c[:, 0:512], func=AF.Exp, scale=0.125))
                        r = kc - 4 * s
                        if r >= 0:
                            p.step("vector", lambda e, r=r: e.tensor_mul(
                                out=b.ebuf[:], in0=b.ebuf[:],
                                in1=b.sup[:, 385 - 128 * r: 897 - 128 * r]))
                        p.step("tensor", lambda e, h=h, kc=kc, last=(kc == nkc - 1): e.matmul(
                            b.ps_a[0:65, 0:512],
                            vt[:, kc * (NH * 65) + h * 65: kc * (NH * 65) + (h + 1) * 65],
                            b.ebuf[:], start=(kc == 0), stop=last))
                    p.step("vector", lambda e: e.tensor_scalar(
                        out=b.r3[:], in0=b.ps_a[64:65, 0:512], scalar1=1e-30, scalar2=None,
                        op0=ALU.add))
                    p.step("vector", lambda e: e.reciprocal(out=b.r4[:], in_=b.r3[:]))
                    p.step("vector", lambda e: e.tensor_copy(out=b.rb1[:], in_=b.r4[:]))
                    p.step("tensor", lambda e: e.matmul(
                        b.ps_c[0:64, 0:512], b.onesb_row[0:1, 0:64], b.rb1[:],
                        start=True, stop=True))
                    p.step("vector", lambda e: e.tensor_copy(
                        out=b.scf[0:64, 0:512], in_=b.ps_c[0:64, 0:512]))
                    p.step("vector", lambda e, h=h, c=c, rb=rb: e.tensor_mul(
                        out=ost[rb:rb + 64, c * 512:(c + 1) * 512],
                        in0=b.ps_a[0:64, 0:512], in1=b.scf[0:64, 0:512]))
                if psx == 0 and s == 0:
                    tap(b, "o0", lambda: ost[:, 0:NCH * 512], [128, NCH * 512], BF16)

                for cp in range(4):
                    p.dma("gpsimd", wop[:].rearrange("p (c n) -> p c n", n=256),
                          di["wo"].rearrange("(c p) n -> p c n", p=128)[:, :, cp * 256:(cp + 1) * 256])
                    for j2 in range(2):
                        oc = cp * 2 + j2
                        for ci in range(NCH):
                            p.step("tensor", lambda e, j2=j2, ci=ci: e.matmul(
                                b.ps_b[:, j2 * 512:(j2 + 1) * 512],
                                wop[:, ci * 256 + j2 * 128: ci * 256 + (j2 + 1) * 128],
                                ost[:, ci * 512:(ci + 1) * 512],
                                start=(ci == 0), stop=(ci == NCH - 1)))
                        p.step("vector", lambda e, j2=j2: e.tensor_copy(
                            out=b.scf[:, 0:512], in_=b.ps_b[:, j2 * 512:(j2 + 1) * 512]))
                        emit_add_to_x(b, b.scf[:, 0:512], s, oc)


def emit_l0_moe(b, ctx, psx):
    p, di = b.prog, b.di
    with ExitStack() as ph:
        def sb(name, shape, dtype):
            return ph.enter_context(b.nc.sbuf_tensor(_uq(name), shape, dtype))
        xnst = sb("xnstm", [128, NCH * 512], BF16)
        w1sb = sb("w1sb", [128, NCH * IM], BF16)
        w2sb = sb("w2sb", [128, NM * 512], BF16)
        hm = sb("hm", [128, NM * 512], BF16)
        acc = sb("macc", [128, NCH * 512], F32)
        cer = sb("cer", [128, 512], BF16)
        cefm = sb("cefm", [16, T], BF16)
        b1t = sb("b1t", [128, NE0 * 16], F32)
        b2sb = sb("b2sb", [16, DM], BF16)
        rwsb = sb("rwsb", [128, NCH * NE0], BF16)
        rbsb = sb("rbsb", [1, NE0], BF16)
        lg = sb("lgts", [128, NE0], F32)

        p.dma("gpsimd", rwsb[:].rearrange("p (c n) -> p c n", n=NE0),
              di["rw"].rearrange("(c p) n -> p c n", p=128))
        p.dma("gpsimd", rbsb[:], di["rb"])
        p.dma("gpsimd", b2sb[:], di["b2"])
        for e0 in range(NE0):
            p.dma("sync", b.scf[0:16, 0:128], di["b1"][e0:e0 + 1, :].rearrange("o (m q) -> (o m) q", q=128))
            p.step("tensor", lambda e, e0=e0: e.transpose(
                b.ps_c[:, 0:16], b.scf[0:16, 0:128], b.eyef[0:16, 0:16]))
            p.step("vector", lambda e, e0=e0: e.tensor_copy(
                out=b1t[:, e0 * 16:(e0 + 1) * 16], in_=b.ps_c[:, 0:16]))

        for t in range(NTILE):
            load_xn_tile(b, xnst, t)
            for fc in range(NCH):
                p.step("tensor", lambda e, fc=fc: e.matmul(
                    b.ps_c[:, 0:NE0],
                    xnst[:, fc * 128:(fc + 1) * 128],
                    rwsb[:, fc * NE0:(fc + 1) * NE0],
                    start=(fc == 0), stop=False))
            p.step("tensor", lambda e: e.matmul(
                b.ps_c[:, 0:NE0], b.onesb_row[:], rbsb[:], start=False, stop=True))
            l0, m1, mr, msk = b.scf[:, 0:NE0], b.col[0], b.col[1], b.scf[:, 64:64 + NE0]
            p.step("vector", lambda e: e.tensor_copy(out=l0, in_=b.ps_c[:, 0:NE0]))
            p.step("vector", lambda e: e.tensor_copy(out=b.scf[:, 32:32 + NE0], in_=l0))
            lw = b.scf[:, 32:32 + NE0]
            for rnd in range(4):
                p.step("vector", lambda e, dst=(m1 if rnd == 0 else mr): e.tensor_reduce(
                    out=dst[:], in_=lw, axis=AX.X, op=ALU.max))
                if rnd < 3:
                    src = m1 if rnd == 0 else mr
                    p.step("vector", lambda e, src=src: e.tensor_scalar(
                        out=msk, in0=lw, scalar1=src[:], scalar2=None, op0=ALU.is_ge))
                    p.step("vector", lambda e: e.scalar_tensor_tensor(
                        out=lw, in0=msk, scalar=-1e30, in1=lw, op0=ALU.mult, op1=ALU.add))
            p.step("vector", lambda e: e.tensor_scalar(
                out=msk, in0=l0, scalar1=mr[:], scalar2=None, op0=ALU.is_ge))
            p.step("vector", lambda e: e.tensor_scalar(
                out=b.col[2][:], in0=m1[:], scalar1=-1.0, scalar2=None, op0=ALU.mult))
            p.step("scalar", lambda e: e.activation(
                out=b.scf[:, 96:96 + NE0], in_=l0, func=AF.Exp, bias=b.col[2][:]))
            p.step("vector", lambda e: e.tensor_mul(
                out=b.scf[:, 96:96 + NE0], in0=b.scf[:, 96:96 + NE0], in1=msk))
            p.step("vector", lambda e: e.tensor_reduce(
                out=b.col[3][:], in_=b.scf[:, 96:96 + NE0], axis=AX.X, op=ALU.add))
            p.step("vector", lambda e: e.reciprocal(out=b.col[3][:], in_=b.col[3][:]))
            p.step("vector", lambda e: e.tensor_scalar(
                out=lg[:], in0=b.scf[:, 96:96 + NE0], scalar1=b.col[3][:], scalar2=None,
                op0=ALU.mult))
            p.step("tensor", lambda e: e.transpose(b.ps_c[0:16, 0:128], lg[:], b.eyef[:]))
            p.step("vector", lambda e, t=t: e.tensor_copy(
                out=cefm[:, t * 128:(t + 1) * 128], in_=b.ps_c[0:16, 0:128]))
        tap(b, f"ce{psx}", lambda: cefm[:], [16, T], BF16)

        for s in range(NSTR):
            load_xn_strip(b, xnst, s)
            for half in range(2):
                for j in range(4):
                    oc = half * 4 + j
                    p.step("tensor", lambda e, oc=oc, j=j, s=s: e.matmul(
                        b.ps_a[:, j * 512:(j + 1) * 512],
                        b2sb[:, oc * 128:(oc + 1) * 128],
                        cefm[:, s * 512:(s + 1) * 512],
                        start=True, stop=True))
                for j in range(4):
                    oc = half * 4 + j
                    p.step("vector", lambda e, oc=oc, j=j: e.tensor_copy(
                        out=acc[:, oc * 512:(oc + 1) * 512], in_=b.ps_a[:, j * 512:(j + 1) * 512]))
            for e0 in range(NE0):
                first = (psx == 0 and s == 0)
                if first:
                    p.dma("gpsimd", w1sb[:].rearrange("p (c m) -> p c m", m=IM),
                          di["w1"][e0].rearrange("(c p) m -> p c m", p=128))
                    p.dma("sync", b.d_w1b[e0], w1sb[:])
                else:
                    p.dma("sync", w1sb[:], b.d_w1b[e0])
                p.step("tensor", lambda e, e0=e0, s=s: e.matmul(
                    b.ps_c[:, 0:512], b.e16[:, e0 * 128:(e0 + 1) * 128],
                    cefm[:, s * 512:(s + 1) * 512], start=True, stop=True))
                p.step("vector", lambda e: e.tensor_copy(out=cer[:], in_=b.ps_c[:, 0:512]))
                for mp in range(NM // 2):
                    for k2 in range(2):
                        m = 2 * mp + k2
                        for fc in range(NCH):
                            p.step("tensor", lambda e, m=m, k2=k2, fc=fc: e.matmul(
                                b.ps_b[:, k2 * 512:(k2 + 1) * 512],
                                w1sb[:, fc * IM + m * 128: fc * IM + (m + 1) * 128],
                                xnst[:, fc * 512:(fc + 1) * 512],
                                start=(fc == 0), stop=(fc == NCH - 1)))
                    for k2 in range(2):
                        m = 2 * mp + k2
                        p.step("scalar", lambda e, m=m, k2=k2, e0=e0: e.activation(
                            out=hm[:, m * 512:(m + 1) * 512],
                            in_=b.ps_b[:, k2 * 512:(k2 + 1) * 512], func=AF.Gelu,
                            bias=b1t[:, e0 * 16 + m: e0 * 16 + m + 1]))
                    p.step("vector", lambda e, mp=mp: e.tensor_tensor(
                        out=hm[:, mp * 1024:(mp + 1) * 1024]
                        .rearrange("p (k n) -> p k n", k=2),
                        in0=hm[:, mp * 1024:(mp + 1) * 1024]
                        .rearrange("p (k n) -> p k n", k=2),
                        in1=cer[:].unsqueeze(1).broadcast_to((128, 2, 512)),
                        op=ALU.mult))
                for half in range(2):
                    cache_view = b.d_w2b[e0].rearrange(
                        "p (m o) -> p m o", o=1024)[:, :, half * 512:(half + 1) * 512]
                    if first:
                        p.dma("gpsimd", w2sb[:].rearrange("p (m n) -> p m n", n=512),
                              di["w2"][e0].rearrange("(m p) n -> p m n", p=128)[:, :, half * 512:(half + 1) * 512])
                        p.dma("sync", cache_view, w2sb[:].rearrange("p (m n) -> p m n", n=512))
                    else:
                        p.dma("sync", w2sb[:].rearrange("p (m n) -> p m n", n=512), cache_view)
                    for m in range(NM):
                        for j in range(4):
                            p.step("tensor", lambda e, m=m, j=j: e.matmul(
                                b.ps_a[:, j * 512:(j + 1) * 512],
                                w2sb[:, m * 512 + j * 128: m * 512 + (j + 1) * 128],
                                hm[:, m * 512:(m + 1) * 512],
                                start=(m == 0), stop=(m == NM - 1)))
                    for j in range(4):
                        oc = half * 4 + j
                        p.step("vector", lambda e, oc=oc, j=j: e.tensor_add(
                            out=acc[:, oc * 512:(oc + 1) * 512],
                            in0=acc[:, oc * 512:(oc + 1) * 512],
                            in1=b.ps_a[:, j * 512:(j + 1) * 512]))
            if psx == 0 and s == 0:
                tap(b, "moeacc", lambda: acc[:], [128, NCH * 512], F32)
            for oc in range(NCH):
                emit_add_to_x(b, acc[:, oc * 512:(oc + 1) * 512], s, oc)


def emit_l1_attn(b, ctx, psx):
    p, di = b.prog, b.di
    with ExitStack() as ph:
        def sb(name, shape, dtype):
            return ph.enter_context(b.nc.sbuf_tensor(_uq(name), shape, dtype))
        kh = sb("l1k", [128, NCH * T], BF16)

        Rrow = b.t16
        R2n = b.t16b
        vfac = b.t16c
        p.dma("sync", b.r3[0:1, 0:NH], di["ksc"])
        p.step("vector", lambda e: e.tensor_scalar(
            out=b.r3[0:1, 0:NH], in0=b.r3[0:1, 0:NH], scalar1=10.0, scalar2=KSMAX,
            op0=ALU.mult, op1=ALU.min))
        p.step("scalar", lambda e: e.activation(out=Rrow[:], in_=b.r3[0:1, 0:NH], func=AF.Exp))
        p.step("vector", lambda e: e.tensor_mul(out=R2n[:], in0=Rrow[:], in1=Rrow[:]))
        p.step("vector", lambda e: e.tensor_scalar(
            out=R2n[:], in0=R2n[:], scalar1=-1.0, scalar2=None, op0=ALU.mult))
        p.dma("sync", b.r3[0:1, 0:NH], di["vsc"])
        p.step("scalar", lambda e: e.activation(
            out=vfac[:], in_=b.r3[0:1, 0:NH], func=AF.Exp, scale=10.0))

        def build_wT(name, wt, wnat):
            for ci in range(NCH):
                p.dma("gpsimd", wnat[:], di[name][ci * 128:(ci + 1) * 128, :])
                for co in range(NCH):
                    p.step("tensor", lambda e, co=co: e.transpose(
                        b.ps_tb[:, 0:128], wnat[:, co * 128:(co + 1) * 128], b.eyeb[:]))
                    p.step("vector", lambda e, co=co, ci=ci: e.tensor_copy(
                        out=wt[:, co * 1024 + ci * 128: co * 1024 + (ci + 1) * 128],
                        in_=b.ps_tb[:, 0:128]))

        with ExitStack() as ph2:
            def sb2(name, shape, dtype):
                return ph2.enter_context(b.nc.sbuf_tensor(_uq(name), shape, dtype))
            wt = sb2("l1wtk", [128, NCH * 1024], BF16)
            wnat = sb2("l1wn", [128, 1024], BF16)
            Lt = sb2("lt", [128, NH * 128], BF16)
            dvr = sb2("dvr", [128, NH * 128], BF16)
            betac = sb2("betac", [128, 1], F32)
            xnt = sb2("xnt1", [128, NCH * 128], BF16)
            ring = [sb2(f"ring{i}", [128, 1024], BF16) for i in range(2)]

            build_wT("wk1", wt, wnat)

            p.dma("sync", b.r3[0:1, 0:NH], di["lkb"])
            for h in range(NH):
                p.step("tensor", lambda e, h=h: e.matmul(
                    b.ps_c[:, 0:1], b.onesf_row[:], b.r3[0:1, h:h + 1], start=True, stop=True))
                p.step("scalar", lambda e: e.activation(
                    out=betac[:], in_=b.ps_c[:, 0:1], func=AF.Abs))
                p.step("vector", lambda e: e.tensor_scalar(
                    out=betac[:], in0=betac[:], scalar1=-10.0, scalar2=None, op0=ALU.mult))
                p.step("vector", lambda e: e.tensor_scalar(
                    out=b.scf[:, 0:128], in0=b.dmat[:], scalar1=betac[:], scalar2=0.0,
                    op0=ALU.mult, op1=ALU.min))
                p.step("scalar", lambda e: e.activation(
                    out=b.scf[:, 128:256], in_=b.scf[:, 0:128], func=AF.Exp))
                p.step("vector", lambda e, h=h: e.tensor_mul(
                    out=Lt[:, h * 128:(h + 1) * 128], in0=b.scf[:, 128:256],
                    in1=b.sup[:, 385:513]))
                p.step("vector", lambda e: e.tensor_scalar(
                    out=b.scf[0:1, 0:128], in0=b.dmat[0:1, 0:128],
                    scalar1=1.0, scalar2=None, op0=ALU.add))
                p.step("vector", lambda e: e.tensor_scalar(
                    out=b.scf[0:1, 0:128], in0=b.scf[0:1, 0:128],
                    scalar1=betac[0:1, 0:1], scalar2=None, op0=ALU.mult))
                p.step("scalar", lambda e: e.activation(
                    out=b.scf[0:1, 0:128], in_=b.scf[0:1, 0:128], func=AF.Exp))
                p.step("vector", lambda e: e.tensor_copy(
                    out=b.rb1[0:1, 0:128], in_=b.scf[0:1, 0:128]))
                p.step("tensor", lambda e: e.matmul(
                    b.ps_c[:, 0:128], b.onesb_row[0:1, 0:128], b.rb1[0:1, 0:128],
                    start=True, stop=True))
                p.step("vector", lambda e, h=h: e.tensor_copy(
                    out=dvr[:, h * 128:(h + 1) * 128], in_=b.ps_c[:, 0:128]))
            tap(b, f"lt{psx}", lambda: Lt[:], [128, NH * 128], BF16)

            for t in range(NTILE):
                load_xn_tile(b, xnt, t)
                for half in range(2):
                    for fc in range(NCH):
                        p.step("tensor", lambda e, half=half, fc=fc: e.matmul(
                            b.ps_b[:, half * 512:(half + 1) * 512],
                            xnt[:, fc * 128:(fc + 1) * 128],
                            wt[:, fc * 1024 + half * 512: fc * 1024 + (half + 1) * 512],
                            start=(fc == 0), stop=(fc == NCH - 1)))
                p.step("scalar", lambda e, t=t: e.activation(
                    out=ring[t % 2][:], in_=b.ps_b[:, 0:1024], func=AF.Identity))
                ktm = ring[t % 2]
                for h in range(NH):
                    p.step("tensor", lambda e, h=h, ktm=ktm: e.matmul(
                        b.ps_a[0:64, h * 128:(h + 1) * 128],
                        ktm[:, h * 64:(h + 1) * 64],
                        Lt[:, h * 128:(h + 1) * 128], start=True, stop=True))
                for h in range(NH):
                    c, rb = h // 2, (h % 2) * 64
                    dst = kh[rb:rb + 64, c * T + t * 128: c * T + (t + 1) * 128]
                    pssl = b.ps_a[0:64, h * 128:(h + 1) * 128]
                    if t == 0:
                        p.step("vector", lambda e, dst=dst, pssl=pssl: e.tensor_copy(
                            out=dst, in_=pssl))
                    else:
                        carry = kh[rb:rb + 64, c * T + t * 128 - 1: c * T + t * 128]
                        p.step("vector", lambda e, dst=dst, pssl=pssl, carry=carry, h=h, rb=rb:
                               e.scalar_tensor_tensor(
                                   out=dst, in0=dvr[rb:rb + 64, h * 128:(h + 1) * 128],
                                   scalar=carry, in1=pssl, op0=ALU.mult, op1=ALU.add))
            tap(b, f"ksm{psx}", lambda: kh[:, 0:T], [128, T], BF16)

            for h in range(NH):
                c, rb = h // 2, (h % 2) * 64
                p.step("scalar", lambda e, c=c, rb=rb: e.activation(
                    out=b.scf[0:64, 0:T], in_=kh[rb:rb + 64, c * T:(c + 1) * T],
                    func=AF.Square))
                for s in range(NSTR):
                    p.step("tensor", lambda e, s=s: e.matmul(
                        b.ps_c[0:1, 0:512], b.onesf_col[0:64, :],
                        b.scf[0:64, s * 512:(s + 1) * 512], start=True, stop=True))
                    p.step("scalar", lambda e: e.activation(
                        out=b.r3[:], in_=b.ps_c[0:1, 0:512], func=AF.Sqrt))
                    p.step("vector", lambda e: e.tensor_scalar(
                        out=b.r3[:], in0=b.r3[:], scalar1=1e-10, scalar2=None, op0=ALU.add))
                    p.step("vector", lambda e: e.reciprocal(out=b.r3[:], in_=b.r3[:]))
                    p.step("vector", lambda e, h=h: e.tensor_scalar(
                        out=b.rb1[:], in0=b.r3[:], scalar1=Rrow[0:1, h:h + 1], scalar2=None,
                        op0=ALU.mult))
                    p.step("tensor", lambda e: e.matmul(
                        b.ps_c[:, 0:512], b.onesb_row[0:1, 0:128], b.rb1[:],
                        start=True, stop=True))
                    p.step("vector", lambda e, rb=rb: e.tensor_copy(
                        out=b.scf[rb:rb + 64, 0:512], in_=b.ps_c[0:64, 0:512]))
                    p.step("vector", lambda e, h=h, c=c, rb=rb, s=s: e.tensor_mul(
                        out=kh[rb:rb + 64, c * T + s * 512: c * T + (s + 1) * 512],
                        in0=kh[rb:rb + 64, c * T + s * 512: c * T + (s + 1) * 512],
                        in1=b.scf[rb:rb + 64, 0:512]))
            tap(b, f"khat{psx}", lambda: kh[:, 0:T], [128, T], BF16)

        with ExitStack() as ph2:
            def sb2(name, shape, dtype):
                return ph2.enter_context(b.nc.sbuf_tensor(_uq(name), shape, dtype))
            vt = sb2("l1vt", [128, NTILE * NH * 65], BF16)
            wtv = sb2("l1wtv", [128, NCH * 1024], BF16)
            wnatv = sb2("l1wn2", [128, 1024], BF16)
            xntv = sb2("xnt2", [128, NCH * 128], BF16)
            vring = [sb2(f"vring{i}", [128, 1024], BF16) for i in range(2)]
            vcf = sb2("vcfr", [128, NH], F32)
            vfr = sb2("vfacr", [128, NH], F32)
            nrm = sb2("vnrm", [128, NH], F32)
            r2c = sb2("r2c", [128, NH], F32)
            ost = sb2("ost1", [128, NCH * 512], BF16)

            build_wT("wv1", wtv, wnatv)
            p.step("vector", lambda e: e.memset(
                vt[:].rearrange("p (t h c) -> p t h c", h=NH, c=65)[:, :, :, 64:65], 1.0))
            p.dma("sync", b.r3[0:1, 0:NH], di["vcf"])
            p.step("tensor", lambda e: e.matmul(
                b.ps_c[:, 0:NH], b.onesf_row[:], b.r3[0:1, 0:NH], start=True, stop=True))
            p.step("vector", lambda e: e.tensor_copy(out=vcf[:], in_=b.ps_c[:, 0:NH]))
            p.step("tensor", lambda e: e.matmul(
                b.ps_c[:, 0:NH], b.onesf_row[:], vfac[:], start=True, stop=True))
            p.step("vector", lambda e: e.tensor_copy(out=vfr[:], in_=b.ps_c[:, 0:NH]))

            for t in range(NTILE + 1):
                if t < NTILE:
                    load_xn_tile(b, xntv, t)
                    for half in range(2):
                        for fc in range(NCH):
                            p.step("tensor", lambda e, half=half, fc=fc: e.matmul(
                                b.ps_b[:, half * 512:(half + 1) * 512],
                                xntv[:, fc * 128:(fc + 1) * 128],
                                wtv[:, fc * 1024 + half * 512: fc * 1024 + (half + 1) * 512],
                                start=(fc == 0), stop=(fc == NCH - 1)))
                    p.step("scalar", lambda e, t=t: e.activation(
                        out=vring[t % 2][:], in_=b.ps_b[:, 0:1024], func=AF.Identity))
                if t == 0:
                    continue
                tp = t - 1
                vcur = vring[tp % 2]
                vnxt = vring[t % 2] if t < NTILE else None
                shp = b.scb
                if vnxt is None:
                    p.step("vector", lambda e: e.memset(b.scb[:, 0:1024], 0.0))
                p.dma("sync", shp[0:127, 0:1024], vcur[1:128, :])
                if vnxt is not None:
                    p.dma("sync", shp[127:128, 0:1024], vnxt[0:1, :])
                p.step("vector", lambda e, vcur=vcur: e.tensor_sub(
                    out=b.scf[:, 0:1024], in0=vcur[:], in1=shp[:, 0:1024]))
                p.step("vector", lambda e: e.tensor_tensor(
                    out=b.scf[:, 0:1024].rearrange("p (h c) -> p h c", c=64),
                    in0=b.scf[:, 0:1024].rearrange("p (h c) -> p h c", c=64),
                    in1=vcf[:].unsqueeze(2).broadcast_to((128, NH, 64)),
                    op=ALU.mult))
                p.step("vector", lambda e: e.tensor_add(
                    out=b.scf[:, 0:1024], in0=b.scf[:, 0:1024], in1=shp[:, 0:1024]))
                p.step("scalar", lambda e: e.activation(
                    out=b.scf[:, 1024:2048], in_=b.scf[:, 0:1024], func=AF.Square))
                p.step("vector", lambda e: e.tensor_reduce(
                    out=nrm[:], in_=b.scf[:, 1024:2048].rearrange("p (h c) -> p h c", c=64),
                    axis=AX.X, op=ALU.add))
                p.step("scalar", lambda e: e.activation(out=nrm[:], in_=nrm[:], func=AF.Sqrt))
                p.step("vector", lambda e: e.tensor_scalar(
                    out=nrm[:], in0=nrm[:], scalar1=1e-10, scalar2=None, op0=ALU.add))
                p.step("vector", lambda e: e.reciprocal(out=nrm[:], in_=nrm[:]))
                p.step("vector", lambda e: e.tensor_mul(out=nrm[:], in0=nrm[:], in1=vfr[:]))
                p.step("vector", lambda e, tp=tp: e.tensor_tensor(
                    out=vt[:, tp * (NH * 65):(tp + 1) * (NH * 65)]
                    .rearrange("p (h c) -> p h c", c=65)[:, :, 0:64],
                    in0=b.scf[:, 0:1024].rearrange("p (h c) -> p h c", c=64),
                    in1=nrm[:].unsqueeze(2).broadcast_to((128, NH, 64)),
                    op=ALU.mult))
            tap(b, f"vhat{psx}", lambda: vt[:, 0:NH * 65], [128, NH * 65], BF16)

            build_wT("wc1", wtv, wnatv)

            p.step("tensor", lambda e: e.matmul(
                b.ps_c[:, 0:NH], b.onesf_row[:], R2n[:], start=True, stop=True))
            p.step("vector", lambda e: e.tensor_copy(out=r2c[:], in_=b.ps_c[:, 0:NH]))

            for s in range(NSTR):
                for h in range(NH):
                    c, rb = h // 2, (h % 2) * 64
                    nkc = 4 * s + 4
                    for kc in range(nkc):
                        p.step("tensor", lambda e, kc=kc, c=c, rb=rb, s=s: e.matmul(
                            b.ps_c[:, 0:512],
                            kh[rb:rb + 64, c * T + kc * 128: c * T + (kc + 1) * 128],
                            kh[rb:rb + 64, c * T + s * 512: c * T + (s + 1) * 512],
                            start=True, stop=True))
                        p.step("scalar", lambda e, h=h: e.activation(
                            out=b.ebuf[:], in_=b.ps_c[:, 0:512], func=AF.Exp,
                            bias=r2c[:, h:h + 1]))
                        r = kc - 4 * s
                        if r >= 0:
                            p.step("vector", lambda e, r=r: e.tensor_mul(
                                out=b.ebuf[:], in0=b.ebuf[:],
                                in1=b.sup[:, 384 - 128 * r: 896 - 128 * r]))
                        p.step("tensor", lambda e, kc=kc, h=h, last=(kc == nkc - 1): e.matmul(
                            b.ps_a[0:65, 0:512],
                            vt[:, kc * (NH * 65) + h * 65: kc * (NH * 65) + (h + 1) * 65],
                            b.ebuf[:], start=(kc == 0), stop=last))
                    p.step("vector", lambda e: e.tensor_scalar(
                        out=b.r3[0:1, 0:512], in0=b.ps_a[64:65, 0:512], scalar1=1e-30,
                        scalar2=None, op0=ALU.add))
                    p.step("vector", lambda e: e.reciprocal(
                        out=b.r4[0:1, 0:512], in_=b.r3[0:1, 0:512]))
                    p.step("vector", lambda e: e.tensor_copy(
                        out=b.rb1[0:1, 0:512], in_=b.r4[0:1, 0:512]))
                    p.step("tensor", lambda e: e.matmul(
                        b.ps_c[0:64, 0:512], b.onesb_row[0:1, 0:64], b.rb1[0:1, 0:512],
                        start=True, stop=True))
                    p.step("vector", lambda e: e.tensor_copy(
                        out=b.scf[0:64, 0:512], in_=b.ps_c[0:64, 0:512]))
                    p.step("vector", lambda e, h=h, c=c, rb=rb: e.tensor_mul(
                        out=ost[rb:rb + 64, c * 512:(c + 1) * 512],
                        in0=b.ps_a[0:64, 0:512], in1=b.scf[0:64, 0:512]))
                for oc in range(NCH):
                    for ci in range(NCH):
                        p.step("tensor", lambda e, oc=oc, ci=ci: e.matmul(
                            b.ps_b[:, 0:512],
                            wtv[:, ci * 1024 + oc * 128: ci * 1024 + (oc + 1) * 128],
                            ost[:, ci * 512:(ci + 1) * 512],
                            start=(ci == 0), stop=(ci == NCH - 1)))
                    p.step("vector", lambda e: e.tensor_copy(
                        out=b.scf[:, 0:512], in_=b.ps_b[:, 0:512]))
                    emit_add_to_x(b, b.scf[:, 0:512], s, oc)


def emit_add_shifted(b, src_ap, s, oc, qn):
    p = b.prog
    for st in range((qn + 127) // 128):
        w = min(128, qn - st * 128)
        p.step("tensor", lambda e, st=st, w=w: e.transpose(
            b.ps_c[:, 0:w] if False else b.ps_c[0:w, 0:128],
            src_ap[:, st * 128:st * 128 + w], b.eyef[:]))
        pos0 = s * 512 + 1 + st * 128
        t0, r0 = pos0 // 128, pos0 % 128
        n1 = min(w, 128 - r0)
        p.step("vector", lambda e, t0=t0, r0=r0, n1=n1, oc=oc: e.tensor_add(
            out=b.x[r0:r0 + n1, t0 * DM + oc * 128: t0 * DM + (oc + 1) * 128],
            in0=b.x[r0:r0 + n1, t0 * DM + oc * 128: t0 * DM + (oc + 1) * 128],
            in1=b.ps_c[0:n1, 0:128]))
        if w > n1:
            n2 = w - n1
            p.step("vector", lambda e, t0=t0, n1=n1, n2=n2, oc=oc: e.tensor_add(
                out=b.x[0:n2, (t0 + 1) * DM + oc * 128: (t0 + 1) * DM + (oc + 1) * 128],
                in0=b.x[0:n2, (t0 + 1) * DM + oc * 128: (t0 + 1) * DM + (oc + 1) * 128],
                in1=b.ps_c[n1:n1 + n2, 0:128]))


def emit_l1_ffn(b, ctx, psx):
    p, di = b.prog, b.di
    with ExitStack() as ph:
        def sb(name, shape, dtype):
            return ph.enter_context(b.nc.sbuf_tensor(_uq(name), shape, dtype))
        xnst = sb("xnstf", [128, NCH * 512], BF16)
        z = sb("zrow", [8, T], F32)
        gsel = sb("gsel", [8, T], F32)
        gselb = sb("gselb", [8, T], BF16)
        selT = sb("selT", [128, NTILE * 8], F32)
        biask = sb("biask", [128, NTILE * 8], F32)
        wrT = sb("wrT", [128, NCH * 8], BF16)
        qe = sb("qe", [128, T], BF16)
        ke = sb("ke", [128, T], BF16)
        ve = sb("ve", [128, NTILE * 128], BF16)
        wqkv = sb("wqkv", [128, NCH * 384], BF16)
        wo1 = sb("wo1s", [128, DM], BF16)
        ost = sb("ostf", [128, 512], BF16)
        lo = sb("lo8", [8, 1], F32)
        hi = sb("hi8", [8, 1], F32)
        th = sb("th8", [8, 1], F32)
        cnt = sb("cnt8", [8, 1], F32)
        dput = sb("dput8", [8, 1], F32)

        p.dma("gpsimd", b.scb[0:8, 0:DM], di["wr1"])
        for fc in range(NCH):
            p.step("tensor", lambda e, fc=fc: e.transpose(
                b.ps_tb[:, 0:8], b.scb[0:8, fc * 128:(fc + 1) * 128], b.eyeb[0:8, 0:8]))
            p.step("vector", lambda e, fc=fc: e.tensor_copy(
                out=wrT[:, fc * 8:(fc + 1) * 8], in_=b.ps_tb[:, 0:8]))

        for s in range(NSTR):
            load_xn_strip(b, xnst, s)
            for fc in range(NCH):
                p.step("tensor", lambda e, fc=fc: e.matmul(
                    b.ps_c[0:8, 0:512], wrT[:, fc * 8:(fc + 1) * 8],
                    xnst[:, fc * 512:(fc + 1) * 512],
                    start=(fc == 0), stop=(fc == NCH - 1)))
            p.step("vector", lambda e, s=s: e.tensor_copy(
                out=z[:, s * 512:(s + 1) * 512], in_=b.ps_c[0:8, 0:512]))

        p.step("vector", lambda e: e.memset(lo[:], -100.0))
        p.step("vector", lambda e: e.memset(hi[:], 100.0))
        for it in range(26):
            p.step("vector", lambda e: e.tensor_add(out=th[:], in0=lo[:], in1=hi[:]))
            p.step("vector", lambda e: e.tensor_scalar(
                out=th[:], in0=th[:], scalar1=0.5, scalar2=None, op0=ALU.mult))
            p.step("vector", lambda e: e.tensor_scalar(
                out=b.scf[0:8, 0:T], in0=z[:], scalar1=th[:], scalar2=None, op0=ALU.is_ge))
            p.step("vector", lambda e: e.tensor_reduce(
                out=cnt[:], in_=b.scf[0:8, 0:T], axis=AX.X, op=ALU.add))
            p.step("vector", lambda e: e.tensor_scalar(
                out=dput[:], in0=cnt[:], scalar1=256.0, scalar2=None, op0=ALU.is_ge))
            p.step("vector", lambda e: e.tensor_sub(out=cnt[:], in0=th[:], in1=lo[:]))
            p.step("vector", lambda e: e.scalar_tensor_tensor(
                out=lo[:], in0=dput[:], scalar=cnt[:], in1=lo[:], op0=ALU.mult, op1=ALU.add))
            p.step("vector", lambda e: e.tensor_scalar(
                out=dput[:], in0=dput[:], scalar1=-1.0, scalar2=1.0, op0=ALU.mult, op1=ALU.add))
            p.step("vector", lambda e: e.tensor_sub(out=cnt[:], in0=th[:], in1=hi[:]))
            p.step("vector", lambda e: e.scalar_tensor_tensor(
                out=hi[:], in0=dput[:], scalar=cnt[:], in1=hi[:], op0=ALU.mult, op1=ALU.add))
        p.step("vector", lambda e: e.tensor_scalar(
            out=b.scf[0:8, 0:T], in0=z[:], scalar1=lo[:], scalar2=None, op0=ALU.is_ge))
        p.step("scalar", lambda e: e.activation(out=gsel[:], in_=z[:], func=AF.Sigmoid))
        p.step("vector", lambda e: e.tensor_mul(
            out=gsel[:], in0=gsel[:], in1=b.scf[0:8, 0:T]))
        p.step("vector", lambda e: e.tensor_copy(out=gselb[:], in_=gsel[:]))
        for t in range(NTILE):
            p.step("tensor", lambda e, t=t: e.transpose(
                b.ps_c[:, 0:8], b.scf[0:8, t * 128:(t + 1) * 128], b.eyef[0:8, 0:8]))
            p.step("vector", lambda e, t=t: e.tensor_copy(
                out=selT[:, t * 8:(t + 1) * 8], in_=b.ps_c[:, 0:8]))
        p.step("vector", lambda e: e.tensor_scalar(
            out=biask[:], in0=selT[:], scalar1=-1.0, scalar2=None, op0=ALU.add))
        p.step("vector", lambda e: e.tensor_scalar(
            out=biask[:], in0=biask[:], scalar1=30000.0, scalar2=None, op0=ALU.mult))
        tap(b, f"gsel{psx}", lambda: gsel[:], [8, T], F32)

        for e1 in range(NE1):
            p.dma("gpsimd", wqkv[:].rearrange("p (c n) -> p c n", n=384),
                  di["wqkv1"][e1].rearrange("(c p) n -> p c n", p=128))
            p.dma("gpsimd", wo1[:], di["wo1"][e1])
            for s in range(NSTR):
                load_xn_strip(b, xnst, s)
                for which, dst in ((0, qe), (1, ke)):
                    for fc in range(NCH):
                        p.step("tensor", lambda e, which=which, fc=fc: e.matmul(
                            b.ps_b[:, 0:512],
                            wqkv[:, fc * 384 + which * 128: fc * 384 + (which + 1) * 128],
                            xnst[:, fc * 512:(fc + 1) * 512],
                            start=(fc == 0), stop=(fc == NCH - 1)))
                    p.step("scalar", lambda e, dst=dst, s=s: e.activation(
                        out=dst[:, s * 512:(s + 1) * 512], in_=b.ps_b[:, 0:512],
                        func=AF.Identity))
                for tt in range(4):
                    t = s * 4 + tt
                    for fc in range(NCH):
                        p.step("tensor", lambda e, t=t, tt=tt, fc=fc: e.matmul(
                            b.ps_c[:, 0:128],
                            xnst[:, fc * 512 + tt * 128: fc * 512 + (tt + 1) * 128],
                            wqkv[:, fc * 384 + 256: fc * 384 + 384],
                            start=(fc == 0), stop=(fc == NCH - 1)))
                    p.step("vector", lambda e, t=t: e.tensor_copy(
                        out=ve[:, t * 128:(t + 1) * 128], in_=b.ps_c[:, 0:128]))
            emit_rope(b, qe[0:64, 0:T], T, 0, rows=64)
            emit_rope(b, ke[0:64, 0:T], T, 0, rows=64)
            if psx == 0 and e1 == 0:
                tap(b, "qe0", lambda: qe[:], [128, T], BF16)
                tap(b, "ve0", lambda: ve[:], [128, NTILE * 128], BF16)

            for s in range(NSTR):
                if s == 0:
                    p.step("scalar", lambda e: e.activation(
                        out=b.scf[:, 0:T], in_=ke[:], func=AF.Square))
                    for ss in range(NSTR):
                        p.step("tensor", lambda e, ss=ss: e.matmul(
                            b.ps_c[0:1, 0:512], b.onesf_col[:],
                            b.scf[:, ss * 512:(ss + 1) * 512], start=True, stop=True))
                        p.step("vector", lambda e, ss=ss: e.tensor_copy(
                            out=(b.r3 if ss == 0 else b.r4)[0:1, 0:512],
                            in_=b.ps_c[0:1, 0:512]))
                        if ss > 0:
                            p.step("vector", lambda e: e.tensor_max(
                                out=b.r3[:], in0=b.r3[:], in1=b.r4[:]))
                    p.step("vector", lambda e: e.tensor_reduce(
                        out=b.col[5][0:1, :], in_=b.r3[:], axis=AX.X, op=ALU.max))
                p.step("scalar", lambda e, s=s: e.activation(
                    out=b.scf[:, 0:512], in_=qe[:, s * 512:(s + 1) * 512], func=AF.Square))
                p.step("tensor", lambda e: e.matmul(
                    b.ps_c[0:1, 0:512], b.onesf_col[:], b.scf[:, 0:512],
                    start=True, stop=True))
                p.step("vector", lambda e: e.tensor_scalar(
                    out=b.r3[0:1, 0:512], in0=b.ps_c[0:1, 0:512],
                    scalar1=b.col[5][0:1, 0:1], scalar2=-0.5, op0=ALU.add, op1=ALU.mult))
                p.step("vector", lambda e: e.tensor_copy(out=b.rb1[:], in_=b.r3[0:1, 0:512]))
                nkc = 4 * s + 4
                for kc in range(nkc):
                    p.step("tensor", lambda e, kc=kc, s=s: e.matmul(
                        b.ps_c[:, 0:512],
                        ke[:, kc * 128:(kc + 1) * 128], qe[:, s * 512:(s + 1) * 512],
                        start=True, stop=False))
                    p.step("tensor", lambda e: e.matmul(
                        b.ps_c[:, 0:512], b.onesb_row[:], b.rb1[:], start=False, stop=True))
                    p.step("scalar", lambda e, kc=kc: e.activation(
                        out=b.ebuf[:], in_=b.ps_c[:, 0:512], func=AF.Exp,
                        scale=float(1.0 / np.sqrt(HP)),
                        bias=biask[:, kc * 8 + e1: kc * 8 + e1 + 1]))
                    r = kc - 4 * s
                    if r >= 0:
                        p.step("vector", lambda e, r=r: e.tensor_mul(
                            out=b.ebuf[:], in0=b.ebuf[:],
                            in1=b.sup[:, 385 - 128 * r: 897 - 128 * r]))
                    p.step("tensor", lambda e, kc=kc, last=(kc == nkc - 1): e.matmul(
                        b.ps_a[:, 0:512], ve[:, kc * 128:(kc + 1) * 128], b.ebuf[:],
                        start=(kc == 0), stop=last))
                    p.step("tensor", lambda e, kc=kc, last=(kc == nkc - 1): e.matmul(
                        b.ps_b[0:1, 0:512], b.onesb_col[:], b.ebuf[:],
                        start=(kc == 0), stop=last))
                p.step("vector", lambda e: e.tensor_scalar(
                    out=b.r3[0:1, 0:512], in0=b.ps_b[0:1, 0:512], scalar1=1e-30,
                    scalar2=None, op0=ALU.add))
                p.step("vector", lambda e: e.reciprocal(out=b.r4[0:1, 0:512], in_=b.r3[0:1, 0:512]))
                p.step("vector", lambda e: e.tensor_copy(out=b.rb1[:], in_=b.r4[0:1, 0:512]))
                p.step("tensor", lambda e: e.matmul(
                    b.ps_c[:, 0:512], b.onesb_row[:], b.rb1[:], start=True, stop=True))
                p.step("vector", lambda e: e.tensor_copy(
                    out=b.scf[:, 0:512], in_=b.ps_c[:, 0:512]))
                p.step("tensor", lambda e, e1=e1, s=s: e.matmul(
                    b.ps_c[:, 0:512], b.e8[:, e1 * 128:(e1 + 1) * 128],
                    gselb[:, s * 512:(s + 1) * 512], start=True, stop=True))
                p.step("vector", lambda e: e.tensor_mul(
                    out=b.scf[:, 0:512], in0=b.scf[:, 0:512], in1=b.ps_c[:, 0:512]))
                p.step("vector", lambda e: e.tensor_mul(
                    out=ost[:, 0:512], in0=b.ps_a[:, 0:512], in1=b.scf[:, 0:512]))
                for ocp in range(4):
                    for j2 in range(2):
                        oc = ocp * 2 + j2
                        p.step("tensor", lambda e, oc=oc, j2=j2: e.matmul(
                            b.ps_b[:, j2 * 512:(j2 + 1) * 512],
                            wo1[:, oc * 128:(oc + 1) * 128], ost[:, 0:512],
                            start=True, stop=True))
                        p.step("vector", lambda e, j2=j2: e.tensor_copy(
                            out=b.scf[:, 512:1024], in_=b.ps_b[:, j2 * 512:(j2 + 1) * 512]))
                        emit_add_to_x(b, b.scf[:, 512:1024], s, oc)


_CACHE = {}


def kernel(**inputs):
    taps = inputs.pop("_taps", ())
    blocks = inputs.pop("_blocks", (0, 1))
    key = (tuple(taps), tuple(blocks))
    if key not in _CACHE:
        _CACHE[key] = build_program(blocks=blocks, taps=taps)
    nc, b = _CACHE[key]

    hc = host_consts()
    wmap = {
        "wq": inputs["L0_attn_W_q"], "wdkv": inputs["L0_attn_W_dkv"],
        "wuk": inputs["L0_attn_W_uk"], "wuv": inputs["L0_attn_W_uv"],
        "wo": inputs["L0_attn_W_o"],
        "lng": inputs["L0_attn_ln_gamma"].reshape(1, LAT),
        "lnb": inputs["L0_attn_ln_beta"].reshape(1, LAT),
        "rw": inputs["L0_ffn_router_w"], "rb": inputs["L0_ffn_router_b"].reshape(1, NE0),
        "w1": inputs["L0_ffn_w1"], "b1": inputs["L0_ffn_b1"],
        "w2": inputs["L0_ffn_w2"], "b2": inputs["L0_ffn_b2"],
        "wk1": inputs["L1_attn_W_k"], "wv1": inputs["L1_attn_W_v"],
        "wc1": inputs["L1_attn_W_c"],
        "lkb": inputs["L1_attn_leaky_key_beta"].reshape(1, NH),
        "ksc": inputs["L1_attn_key_scale"].reshape(1, NH),
        "vcf": inputs["L1_attn_v_coef"].reshape(1, NH),
        "vsc": inputs["L1_attn_val_scale"].reshape(1, NH),
        "wr1": inputs["L1_ffn_Wr"], "wqkv1": inputs["L1_ffn_Wqkv"],
        "wo1": inputs["L1_ffn_Wo"],
    }
    wmap.update(hc)
    x = inputs["L0_attn_x"]
    in_maps = []
    for c in range(8):
        m = dict(wmap)
        m["xin"] = np.ascontiguousarray(x[[c, c + 8]])
        in_maps.append(m)
    res = run_bass_kernel_spmd(nc, in_maps, core_ids=list(range(8)))
    out = np.empty((16, T, DM), dtype=np.float32)
    for c in range(8):
        out[c] = res.results[c]["out"][0]
        out[c + 8] = res.results[c]["out"][1]
    if taps:
        tapout = {nm: [res.results[c]["tap_" + nm] for c in range(8)] for nm in b.taps}
        return out, tapout
    return out


# version 05 (final)
# speedup vs baseline: 1.7916x; 1.0588x over previous
import numpy as np
import ml_dtypes
from contextlib import ExitStack

import concourse.bass as bass
import concourse.mybir as mybir
from concourse.bass_utils import run_bass_kernel_spmd

F32 = mybir.dt.float32
BF16 = mybir.dt.bfloat16
I32 = mybir.dt.int32
AF = mybir.ActivationFunctionType
ALU = mybir.AluOpType
AX = mybir.AxisListType

T = 2048
DM = 1024
NH = 16
HD = 64
LAT = 256
NE0 = 16
IM = 2048
NE1 = 8
HP = 128
KSMAX = float(np.log(2.0 ** 16 - 1.0))
NTILE = T // 128
NCH = DM // 128
NSTR = T // 512
NM = IM // 128

ENGINES = ("sync", "gpsimd", "vector", "scalar", "tensor")


class Prog:
    def __init__(self):
        self.steps = []

    def step(self, eng, fn, inc=1):
        self.steps.append((eng, fn, inc))

    def dma(self, eng, out, in_):
        self.step(eng, lambda e, o=out, i=in_: e.dma_start(out=o, in_=i), 16)

    def build(self, block, sem):
        waits = []
        total = 0
        for _, _, inc in self.steps:
            waits.append(total)
            total += inc
        groups = []
        MERGEABLE = ("tensor", "sync", "gpsimd", "scalar")
        for i, (e, fn, inc) in enumerate(self.steps):
            if groups and groups[-1][0] == e and e in MERGEABLE:
                groups[-1][2].append((fn, inc))
            else:
                groups.append((e, waits[i], [(fn, inc)]))
        per_engine = {e: [] for e in ENGINES}
        for e, w, members in groups:
            per_engine[e].append((w, members))
        for eng_name in ENGINES:
            if not per_engine[eng_name]:
                continue
            is_dma = eng_name in ("sync", "gpsimd")

            def make_body(mygroups, is_dma):
                def body(eng):
                    for w, members in mygroups:
                        if w > 0:
                            eng.wait_ge(sem, w)
                        if is_dma:
                            for fn, inc in members:
                                fn(eng).then_inc(sem, inc)
                        else:
                            tot = sum(inc for _, inc in members)
                            for j, (fn, inc) in enumerate(members):
                                inst = fn(eng)
                                if j == len(members) - 1:
                                    inst.then_inc(sem, tot)
                return body
            getattr(block, eng_name)(make_body(per_engine[eng_name], is_dma))
        return total


def host_consts():
    j = np.arange(128)
    u = np.arange(897)
    c = {}
    c["c_eye"] = np.eye(128, dtype=np.float32)
    c["c_sup"] = (j[:, None] <= (u[None, :] - 385)).astype(np.float32)
    c["c_dmat"] = (j[None, :] - j[:, None]).astype(np.float32)
    c["c_iota_row"] = np.arange(T, dtype=np.float32)[None, :]
    c["c_iota_col"] = np.arange(128, dtype=np.float32)[:, None]
    e16 = np.zeros((16, 16 * 128), dtype=np.float32)
    for e in range(16):
        e16[e, e * 128:(e + 1) * 128] = 1.0
    c["c_e16"] = e16
    e8 = np.zeros((8, 8 * 128), dtype=np.float32)
    for e in range(8):
        e8[e, e * 128:(e + 1) * 128] = 1.0
    c["c_e8"] = e8
    return c


class B:
    pass


_UID = [0]


def _uq(name):
    _UID[0] += 1
    return f"{name}_u{_UID[0]}"




def build_program(blocks=(0, 1), n_pass=2, taps=()):
    nc = bass.Bass("TRN2", target_bir_lowering=False)
    b = B()
    b.nc = nc
    b.prog = Prog()
    b.taps = {}
    b.tapset = set(taps)
    p = b.prog

    di = {}
    di["xin"] = nc.dram_tensor("xin", [n_pass, T, DM], F32, kind="ExternalInput").ap()
    for nm, shp in [
        ("wq", [DM, DM]), ("wdkv", [DM, LAT]), ("wuk", [LAT, DM]), ("wuv", [LAT, DM]),
        ("wo", [DM, DM]), ("lng", [1, LAT]), ("lnb", [1, LAT]),
        ("rw", [DM, NE0]), ("rb", [1, NE0]),
        ("w1", [NE0, DM, IM]), ("b1", [NE0, IM]), ("w2", [NE0, IM, DM]), ("b2", [NE0, DM]),
        ("wk1", [DM, DM]), ("wv1", [DM, DM]), ("wc1", [DM, DM]),
        ("lkb", [1, NH]), ("ksc", [1, NH]), ("vcf", [1, NH]), ("vsc", [1, NH]),
        ("wr1", [NE1, DM]), ("wqkv1", [NE1, DM, 3 * HP]), ("wo1", [NE1, HP, DM]),
        ("c_eye", [128, 128]), ("c_sup", [128, 897]), ("c_dmat", [128, 128]),
        ("c_iota_row", [1, T]), ("c_iota_col", [128, 1]),
        ("c_e16", [16, 2048]), ("c_e8", [8, 1024]),
    ]:
        di[nm] = nc.dram_tensor(nm, shp, F32, kind="ExternalInput").ap()
    b.di = di
    b.dout = nc.dram_tensor("out", [n_pass, T, DM], F32, kind="ExternalOutput").ap()
    b.d_xn = nc.dram_tensor("d_xn", [NCH, 128, T], BF16).ap()
    b.d_w1b = nc.dram_tensor("d_w1b", [NE0, 128, NCH * IM], BF16).ap()
    b.d_w2b = nc.dram_tensor("d_w2b", [NE0, 128, NM * DM], BF16).ap()

    with ExitStack() as ctx:
        def sb(name, shape, dtype):
            return ctx.enter_context(nc.sbuf_tensor(name, shape, dtype))
        b.sb_top = sb

        b.x = sb("x", [128, NTILE * DM], F32)
        b.chc = sb("chc", [128, T], BF16)
        b.chs = sb("chs", [128, T], BF16)
        b.eyeb = sb("eyeb", [128, 128], BF16)
        b.eyef = sb("eyef", [128, 128], F32)
        b.sup = sb("supm", [128, 897], BF16)
        b.dmat = sb("dmatc", [128, 128], F32)
        b.e16 = sb("e16c", [16, 2048], BF16)
        b.e8 = sb("e8c", [8, 1024], BF16)
        b.onesb_row = sb("onesb_row", [1, 128], BF16)
        b.onesf_row = sb("onesf_row", [1, 128], F32)
        b.onesb_col = sb("onesb_col", [128, 1], BF16)
        b.onesf_col = sb("onesf_col", [128, 1], F32)
        b.iotac = sb("iotac", [128, 1], F32)
        b.r3 = sb("r3", [1, 512], F32)
        b.r4 = sb("r4", [1, 512], F32)
        b.rb1 = sb("rb1", [1, 512], BF16)
        b.t16 = sb("t16a", [1, 16], F32)
        b.t16b = sb("t16b", [1, 16], F32)
        b.t16c = sb("t16c", [1, 16], F32)
        b.col = [sb(f"col{i}", [128, 1], F32) for i in range(8)]
        b.eps5 = sb("eps5", [128, 1], F32)
        b.scf = sb("scf", [128, T], F32)
        b.scb = sb("scb", [128, T], BF16)
        b.scb2 = sb("scb2", [128, T], BF16)
        b.ebuf = sb("ebuf", [128, 512], BF16)

        b.ps_a = ctx.enter_context(nc.psum_tensor("ps_a", [128, 2048], F32))
        b.ps_b = ctx.enter_context(nc.psum_tensor("ps_b", [128, 1024], F32))
        b.ps_c = ctx.enter_context(nc.psum_tensor("ps_c", [128, 512], F32))
        b.ps_tb = ctx.enter_context(nc.psum_tensor("ps_tb", [128, 512], BF16))

        sem = ctx.enter_context(nc.semaphore("g"))
        block = ctx.enter_context(nc.Block())

        emit_prologue(b, ctx)
        for psx in range(n_pass):
            for t in range(NTILE):
                p.dma("sync", b.x[:, t * DM:(t + 1) * DM], di["xin"][psx, t * 128:(t + 1) * 128, :])
            import os
            skip = set(os.environ.get("K_SKIP", "").split(","))
            for blk in blocks:
                emit_ln(b, ctx, psx, blk)
                if blk == 0:
                    if "l0attn" not in skip:
                        emit_l0_attn(b, ctx, psx)
                    if "l0moe" not in skip:
                        emit_l0_moe(b, ctx, psx)
                else:
                    if "l1attn" not in skip:
                        emit_l1_attn(b, ctx, psx)
                    if "l1ffn" not in skip:
                        emit_l1_ffn(b, ctx, psx)
            for t in range(NTILE):
                p.dma("sync", b.dout[psx, t * 128:(t + 1) * 128, :], b.x[:, t * DM:(t + 1) * DM])

        total = p.build(block, sem)

        @block.sync
        def _(sync):
            sync.wait_ge(sem, total)

    return nc, b


def tap(b, name, ap_fn, shape, dtype=F32):
    if name not in b.tapset or name in b.taps:
        return
    t = b.nc.dram_tensor("tap_" + name, list(shape), dtype, kind="ExternalOutput").ap()
    b.taps[name] = (list(shape), dtype)
    b.prog.dma("sync", t, ap_fn())


def emit_prologue(b, ctx):
    import os
    skip = set(os.environ.get("K_SKIP", "").split(","))
    p, di = b.prog, b.di
    p.dma("gpsimd", b.eyeb[:], di["c_eye"])
    if "eyef" not in skip:
        p.dma("sync", b.eyef[:], di["c_eye"])
    if "sup" not in skip:
        p.dma("gpsimd", b.sup[:], di["c_sup"])
    if "dmatx" not in skip:
        p.dma("sync", b.dmat[:], di["c_dmat"])
    if "e16" not in skip:
        p.dma("gpsimd", b.e16[:], di["c_e16"])
    if "e8" not in skip:
        p.dma("gpsimd", b.e8[:], di["c_e8"])
    if "iotac" not in skip:
        p.dma("sync", b.iotac[:], di["c_iota_col"])
    if "consts" in skip:
        p.step("vector", lambda e: e.memset(b.onesb_row[:], 1.0))
        p.step("vector", lambda e: e.memset(b.onesf_row[:], 1.0))
        p.step("vector", lambda e: e.memset(b.onesb_col[:], 1.0))
        p.step("vector", lambda e: e.memset(b.onesf_col[:], 1.0))
        p.step("vector", lambda e: e.memset(b.eps5[:], 1e-5))
        return
    p.step("vector", lambda e: e.memset(b.onesb_row[:], 1.0))
    p.step("vector", lambda e: e.memset(b.onesf_row[:], 1.0))
    p.step("vector", lambda e: e.memset(b.onesb_col[:], 1.0))
    p.step("vector", lambda e: e.memset(b.onesf_col[:], 1.0))
    p.step("vector", lambda e: e.memset(b.eps5[:], 1e-5))

    import os
    if "rope" in set(os.environ.get("K_SKIP", "").split(",")):
        return
    O2PI = float(1.0 / (2 * np.pi))
    TWO_PI = float(2 * np.pi)
    LN10K = float(np.log(10000.0))
    with ExitStack() as ph:
        iotar = ph.enter_context(b.nc.sbuf_tensor(_uq("iotar"), [1, T], F32))
        ang = ph.enter_context(b.nc.sbuf_tensor(_uq("ang"), [32, T], F32))
        tmpi = ph.enter_context(b.nc.sbuf_tensor(_uq("tmpi"), [32, T], I32))
        tmpf = ph.enter_context(b.nc.sbuf_tensor(_uq("tmpf"), [32, T], F32))
        invf = b.col[7]
        p.dma("sync", iotar[:], di["c_iota_row"])
        p.step("scalar", lambda e: e.activation(
            out=invf[0:32, :], in_=b.iotac[0:32, :], func=AF.Exp, scale=-LN10K / 32.0))
        for s in range(NSTR):
            p.step("tensor", lambda e, s=s: e.matmul(
                b.ps_a[0:32, s * 512:(s + 1) * 512], b.onesf_row[0:1, 0:32],
                iotar[0:1, s * 512:(s + 1) * 512], start=True, stop=True))

        def reduce_sin(dst, phase):
            p.step("vector", lambda e: e.tensor_scalar(
                out=ang[:], in0=b.ps_a[0:32, 0:T], scalar1=invf[0:32, :],
                scalar2=phase, op0=ALU.mult, op1=ALU.add))
            p.step("vector", lambda e: e.tensor_scalar(
                out=tmpi[:], in0=ang[:], scalar1=O2PI, scalar2=None, op0=ALU.mult))
            p.step("vector", lambda e: e.tensor_copy(out=tmpf[:], in_=tmpi[:]))
            p.step("vector", lambda e: e.scalar_tensor_tensor(
                out=ang[:], in0=tmpf[:], scalar=-TWO_PI, in1=ang[:],
                op0=ALU.mult, op1=ALU.add))
            p.step("scalar", lambda e, dst=dst: e.activation(out=dst, in_=ang[:], func=AF.Sin))

        reduce_sin(b.scb[0:32, 0:T], 0.0)
        reduce_sin(b.scb2[0:32, 0:T], float(np.pi / 2))
        p.step("vector", lambda e: e.tensor_scalar(
            out=b.scb[32:64, 0:T], in0=b.scb[0:32, 0:T], scalar1=-1.0, scalar2=None,
            op0=ALU.mult))
        for r in range(4):
            p.dma("sync", b.chc[32 * r:32 * (r + 1), 0:T], b.scb2[0:32, 0:T])
            p.dma("sync", b.chs[32 * r:32 * (r + 1), 0:T],
                  b.scb[32:64, 0:T] if r % 2 == 0 else b.scb[0:32, 0:T])
    tap(b, "ropec", lambda: b.chc[:], [128, T], BF16)
    tap(b, "ropes", lambda: b.chs[:], [128, T], BF16)


def emit_ln(b, ctx, psx, blk):
    p = b.prog
    with ExitStack() as ph:
        stage = ph.enter_context(b.nc.sbuf_tensor(_uq("lnstage"), [128, NCH * 512], BF16))
        for s in range(NSTR):
            for tt in range(4):
                t = s * 4 + tt
                xt = b.x[:, t * DM:(t + 1) * DM]
                c0, c1, mu, var, rstd = b.col[0], b.col[1], b.col[2], b.col[3], b.col[4]
                p.step("scalar", lambda e, xt=xt, c0=c0: e.activation(
                    out=b.scf[:, 0:DM], in_=xt, func=AF.Square, accum_out=c0[:]))
                p.step("scalar", lambda e, xt=xt, c1=c1: e.activation(
                    out=b.scf[:, DM:2 * DM], in_=xt, func=AF.Identity, accum_out=c1[:]))
                p.step("vector", lambda e, c1=c1, mu=mu: e.tensor_scalar(
                    out=mu[:], in0=c1[:], scalar1=1.0 / DM, scalar2=None, op0=ALU.mult))
                p.step("vector", lambda e, mu=mu, var=var: e.tensor_mul(
                    out=var[:], in0=mu[:], in1=mu[:]))
                p.step("vector", lambda e, c0=c0, var=var: e.scalar_tensor_tensor(
                    out=var[:], in0=c0[:], scalar=1.0 / DM, in1=var[:],
                    op0=ALU.mult, op1=ALU.subtract))
                p.step("scalar", lambda e, var=var, rstd=rstd: e.activation(
                    out=rstd[:], in_=var[:], func=AF.Sqrt, bias=b.eps5[:]))
                p.step("vector", lambda e, rstd=rstd: e.reciprocal(out=rstd[:], in_=rstd[:]))
                p.step("vector", lambda e, xt=xt, mu=mu, rstd=rstd: e.tensor_scalar(
                    out=b.scb[:, 0:DM], in0=xt, scalar1=mu[:], scalar2=rstd[:],
                    op0=ALU.subtract, op1=ALU.mult))
                if t == 0:
                    tap(b, f"lnx_b{blk}_p{psx}", lambda xt=xt: xt, [128, DM], F32)
                    tap(b, f"lnc0_b{blk}_p{psx}", lambda c0=c0: c0[:], [128, 1], F32)
                    tap(b, f"lnmu_b{blk}_p{psx}", lambda mu=mu: mu[:], [128, 1], F32)
                    tap(b, f"lnrs_b{blk}_p{psx}", lambda rstd=rstd: rstd[:], [128, 1], F32)
                    tap(b, f"lnscb_b{blk}_p{psx}", lambda: b.scb[:, 0:DM], [128, DM], BF16)
                for c in range(NCH):
                    p.step("tensor", lambda e, c=c: e.transpose(
                        b.ps_tb[:, 0:128], b.scb[:, c * 128:(c + 1) * 128], b.eyeb[:]))
                    p.step("vector", lambda e, c=c, tt=tt: e.tensor_copy(
                        out=stage[:, c * 512 + tt * 128: c * 512 + (tt + 1) * 128],
                        in_=b.ps_tb[:, 0:128]))
            for c in range(NCH):
                p.dma("sync", b.d_xn[c, :, s * 512:(s + 1) * 512],
                      stage[:, c * 512:(c + 1) * 512])
    tap(b, f"xn_b{blk}_p{psx}", lambda: b.scb[:, 0:DM], [128, DM], BF16)


def load_xn_strip(b, dst, s):
    b.prog.dma("sync", dst[:, 0:NCH * 512].rearrange("p (c n) -> p c n", n=512),
               b.d_xn[:, :, s * 512:(s + 1) * 512].rearrange("c p n -> p c n"))


def load_xn_tile(b, dst, t):
    b.prog.dma("sync", dst[:, 0:NCH * 128].rearrange("p (c n) -> p c n", n=128),
               b.d_xn[:, :, t * 128:(t + 1) * 128].rearrange("c p n -> p c n"))


def emit_rope(b, chunk_ap, width, pos0, rows=128):
    p = b.prog
    ngr = rows // 32
    for dg in range(ngr):
        sg = dg ^ 1
        p.step("vector", lambda e, dg=dg, sg=sg: e.tensor_copy(
            out=b.scb2[32 * dg:32 * (dg + 1), 0:width],
            in_=chunk_ap[32 * sg:32 * (sg + 1), 0:width]))
    p.step("vector", lambda e: e.tensor_mul(
        out=b.scb[0:rows, 0:width], in0=chunk_ap, in1=b.chc[0:rows, pos0:pos0 + width]))
    p.step("vector", lambda e: e.tensor_mul(
        out=b.scb2[0:rows, 0:width], in0=b.scb2[0:rows, 0:width],
        in1=b.chs[0:rows, pos0:pos0 + width]))
    p.step("vector", lambda e: e.tensor_add(
        out=chunk_ap, in0=b.scb[0:rows, 0:width], in1=b.scb2[0:rows, 0:width]))


def emit_add_to_x(b, src_f32_ap, s, oc, width=512):
    p = b.prog
    nt = width // 128
    for st in range(nt):
        t = s * 4 + st
        p.step("tensor", lambda e, st=st: e.transpose(
            b.ps_c[:, 0:128], src_f32_ap[:, st * 128:(st + 1) * 128], b.eyef[:]))
        p.step("vector", lambda e, t=t, oc=oc: e.tensor_add(
            out=b.x[:, t * DM + oc * 128: t * DM + (oc + 1) * 128],
            in0=b.x[:, t * DM + oc * 128: t * DM + (oc + 1) * 128],
            in1=b.ps_c[:, 0:128]))


def emit_l0_attn(b, ctx, psx):
    p, di = b.prog, b.di
    with ExitStack() as ph:
        def sb(name, shape, dtype):
            return ph.enter_context(b.nc.sbuf_tensor(_uq(name), shape, dtype))
        kbuf = sb("l0k", [128, NCH * T], BF16)
        vt = sb("l0vt", [128, NTILE * NH * 65], BF16)

        with ExitStack() as ph2:
            def sb2(name, shape, dtype):
                return ph2.enter_context(b.nc.sbuf_tensor(_uq(name), shape, dtype))
            latn = sb2("l0latn", [128, 2 * T], BF16)
            wdkv = sb2("wdkv", [128, NCH * LAT], BF16)
            wuk = sb2("wuk", [128, 2 * DM], BF16)
            wuv = sb2("wuv", [128, 2 * DM], BF16)
            gcol = sb2("lngc", [128, 2], F32)
            bcol = sb2("lnbc", [128, 2], F32)
            latf = sb2("latf", [128, 1024], F32)
            xnst = sb2("xnst0", [128, NCH * 512], BF16)

            p.dma("gpsimd", wdkv[:].rearrange("p (c l) -> p c l", l=LAT),
                  di["wdkv"].rearrange("(c p) l -> p c l", p=128))
            p.dma("gpsimd", wuk[:].rearrange("p (c d) -> p c d", d=DM),
                  di["wuk"].rearrange("(c p) d -> p c d", p=128))
            p.dma("gpsimd", wuv[:].rearrange("p (c d) -> p c d", d=DM),
                  di["wuv"].rearrange("(c p) d -> p c d", p=128))
            p.dma("sync", b.scf[0:2, 0:128], di["lng"].rearrange("o (c p) -> (o c) p", p=128))
            p.dma("sync", b.scf[32:34, 0:128], di["lnb"].rearrange("o (c p) -> (o c) p", p=128))
            p.step("tensor", lambda e: e.transpose(b.ps_c[:, 0:2], b.scf[0:2, 0:128], b.eyef[0:2, 0:2]))
            p.step("vector", lambda e: e.tensor_copy(out=gcol[:], in_=b.ps_c[:, 0:2]))
            p.step("tensor", lambda e: e.transpose(b.ps_c[:, 0:2], b.scf[32:34, 0:128], b.eyef[32:34, 32:34]))
            p.step("vector", lambda e: e.tensor_copy(out=bcol[:], in_=b.ps_c[:, 0:2]))
            tap(b, f"wdkvt{psx}", lambda: wdkv[:], [128, NCH * LAT], BF16)
            p.step("vector", lambda e: e.memset(
                vt[:].rearrange("p (t h c) -> p t h c", h=NH, c=65)[:, :, :, 64:65], 1.0))

            for s in range(NSTR):
                load_xn_strip(b, xnst, s)
                if s == 0:
                    tap(b, f"xnstA{psx}", lambda: xnst[:], [128, NCH * 512], BF16)
                for lc in range(2):
                    for fc in range(NCH):
                        p.step("tensor", lambda e, lc=lc, fc=fc: e.matmul(
                            b.ps_b[:, lc * 512:(lc + 1) * 512],
                            wdkv[:, fc * LAT + lc * 128: fc * LAT + (lc + 1) * 128],
                            xnst[:, fc * 512:(fc + 1) * 512],
                            start=(fc == 0), stop=(fc == NCH - 1)))
                p.step("vector", lambda e: e.tensor_copy(out=latf[:], in_=b.ps_b[:, 0:1024]))
                if s == 0:
                    tap(b, f"latraw{psx}", lambda: latf[:], [128, 1024], F32)
                p.step("scalar", lambda e: e.activation(
                    out=b.scf[:, 0:1024], in_=latf[:], func=AF.Square))
                for lc in range(2):
                    p.step("tensor", lambda e, lc=lc: e.matmul(
                        b.ps_c[0:1, 0:512], b.onesf_col[:], latf[:, lc * 512:(lc + 1) * 512],
                        start=(lc == 0), stop=(lc == 1)))
                p.step("vector", lambda e: e.tensor_scalar(
                    out=b.r3[:], in0=b.ps_c[0:1, 0:512], scalar1=1.0 / LAT, scalar2=None,
                    op0=ALU.mult))
                if s == 0:
                    tap(b, f"latsum{psx}", lambda: b.r3[:], [1, 512], F32)
                for lc in range(2):
                    p.step("tensor", lambda e, lc=lc: e.matmul(
                        b.ps_c[0:1, 0:512], b.onesf_col[:], b.scf[:, lc * 512:(lc + 1) * 512],
                        start=(lc == 0), stop=(lc == 1)))
                p.step("vector", lambda e: e.tensor_mul(out=b.r4[:], in0=b.r3[:], in1=b.r3[:]))
                p.step("vector", lambda e: e.scalar_tensor_tensor(
                    out=b.r4[:], in0=b.ps_c[0:1, 0:512], scalar=1.0 / LAT, in1=b.r4[:],
                    op0=ALU.mult, op1=ALU.subtract))
                p.step("scalar", lambda e: e.activation(out=b.r4[:], in_=b.r4[:], func=AF.Sqrt, bias=b.eps5[0:1, 0:1]))
                p.step("vector", lambda e: e.reciprocal(out=b.r4[:], in_=b.r4[:]))
                p.step("tensor", lambda e: e.matmul(
                    b.ps_c[:, 0:512], b.onesf_row[0:1, :], b.r3[:], start=True, stop=True))
                for lc in range(2):
                    p.step("vector", lambda e, lc=lc: e.tensor_sub(
                        out=latf[:, lc * 512:(lc + 1) * 512],
                        in0=latf[:, lc * 512:(lc + 1) * 512], in1=b.ps_c[:, 0:512]))
                p.step("tensor", lambda e: e.matmul(
                    b.ps_c[:, 0:512], b.onesf_row[0:1, :], b.r4[:], start=True, stop=True))
                for lc in range(2):
                    p.step("vector", lambda e, lc=lc: e.tensor_mul(
                        out=latf[:, lc * 512:(lc + 1) * 512],
                        in0=latf[:, lc * 512:(lc + 1) * 512], in1=b.ps_c[:, 0:512]))
                for lc in range(2):
                    p.step("vector", lambda e, lc=lc, s=s: e.tensor_scalar(
                        out=latn[:, lc * T + s * 512: lc * T + (s + 1) * 512],
                        in0=latf[:, lc * 512:(lc + 1) * 512],
                        scalar1=gcol[:, lc:lc + 1], scalar2=bcol[:, lc:lc + 1],
                        op0=ALU.mult, op1=ALU.add))
                if s == 0:
                    tap(b, f"latmu{psx}", lambda: b.r3[:], [1, 512], F32)
                    tap(b, f"latrs{psx}", lambda: b.r4[:], [1, 512], F32)
                    tap(b, f"latf{psx}", lambda: latf[:], [128, 1024], F32)
                    tap(b, f"gcol{psx}", lambda: gcol[:], [128, 2], F32)
                    tap(b, f"bcol{psx}", lambda: bcol[:], [128, 2], F32)
            tap(b, f"latn{psx}", lambda: latn[:], [128, 2 * T], BF16)

            for c in range(NCH):
                for s in range(NSTR):
                    for lc in range(2):
                        p.step("tensor", lambda e, c=c, s=s, lc=lc: e.matmul(
                            b.ps_c[:, 0:512],
                            wuk[:, lc * DM + c * 128: lc * DM + (c + 1) * 128],
                            latn[:, lc * T + s * 512: lc * T + (s + 1) * 512],
                            start=(lc == 0), stop=(lc == 1)))
                    p.step("scalar", lambda e, c=c, s=s: e.activation(
                        out=kbuf[:, c * T + s * 512: c * T + (s + 1) * 512],
                        in_=b.ps_c[:, 0:512], func=AF.Identity))
            for c in range(NCH):
                emit_rope(b, kbuf[:, c * T:(c + 1) * T], T, 0)
            tap(b, f"k0_{psx}", lambda: kbuf[:, 0:T], [128, T], BF16)

            for tt in range(NTILE):
                for half in range(2):
                    for lc in range(2):
                        p.step("tensor", lambda e, tt=tt, half=half, lc=lc: e.matmul(
                            b.ps_b[:, half * 512:(half + 1) * 512],
                            latn[:, lc * T + tt * 128: lc * T + (tt + 1) * 128],
                            wuv[:, lc * DM + half * 512: lc * DM + (half + 1) * 512],
                            start=(lc == 0), stop=(lc == 1)))
                p.step("vector", lambda e, tt=tt: e.tensor_copy(
                    out=vt[:, tt * (NH * 65):(tt + 1) * (NH * 65)]
                    .rearrange("p (h c) -> p h c", c=65)[:, :, 0:64],
                    in_=b.ps_b[:, 0:1024].rearrange("p (h c) -> p h c", c=64)))
            tap(b, f"v0_{psx}", lambda: vt[:, 0:NH * 65], [128, NH * 65], BF16)

        with ExitStack() as ph2:
            def sb2(name, shape, dtype):
                return ph2.enter_context(b.nc.sbuf_tensor(_uq(name), shape, dtype))
            xnstb = sb2("xnst0b", [128, NCH * 512], BF16)
            qst = sb2("qst", [128, NCH * 512], BF16)
            ost = sb2("ost", [128, NCH * 512], BF16)
            wqp = sb2("wqp", [128, NCH * 256], BF16)
            wop = sb2("wop", [128, NCH * 256], BF16)
            nk2 = b.t16

            for h in range(NH):
                c, rb = h // 2, (h % 2) * 64
                p.step("scalar", lambda e, c=c, rb=rb: e.activation(
                    out=b.scf[0:64, 0:T], in_=kbuf[rb:rb + 64, c * T:(c + 1) * T],
                    func=AF.Square))
                for s in range(NSTR):
                    p.step("tensor", lambda e, s=s: e.matmul(
                        b.ps_b[0:1, s * 256:s * 256 + 512] if False else b.ps_c[0:1, 0:512],
                        b.onesf_col[0:64, :], b.scf[0:64, s * 512:(s + 1) * 512],
                        start=True, stop=True))
                    p.step("vector", lambda e, s=s: e.tensor_copy(
                        out=b.r3[0:1, 0:512] if s == 0 else b.r4[0:1, 0:512],
                        in_=b.ps_c[0:1, 0:512]))
                    if s > 0:
                        p.step("vector", lambda e: e.tensor_max(
                            out=b.r3[:], in0=b.r3[:], in1=b.r4[:]))
                p.step("vector", lambda e, h=h: e.tensor_reduce(
                    out=nk2[0:1, h:h + 1], in_=b.r3[:], axis=AX.X, op=ALU.max))

            for s in range(NSTR):
                load_xn_strip(b, xnstb, s)
                for cp in range(4):
                    p.dma("gpsimd", wqp[:].rearrange("p (c n) -> p c n", n=256),
                          di["wq"].rearrange("(c p) n -> p c n", p=128)[:, :, cp * 256:(cp + 1) * 256])
                    for j2 in range(2):
                        for fc in range(NCH):
                            p.step("tensor", lambda e, j2=j2, fc=fc: e.matmul(
                                b.ps_b[:, j2 * 512:(j2 + 1) * 512],
                                wqp[:, fc * 256 + j2 * 128: fc * 256 + (j2 + 1) * 128],
                                xnstb[:, fc * 512:(fc + 1) * 512],
                                start=(fc == 0), stop=(fc == NCH - 1)))
                        p.step("scalar", lambda e, cp=cp, j2=j2: e.activation(
                            out=qst[:, (cp * 2 + j2) * 512:(cp * 2 + j2 + 1) * 512],
                            in_=b.ps_b[:, j2 * 512:(j2 + 1) * 512], func=AF.Identity))
                for c in range(NCH):
                    emit_rope(b, qst[:, c * 512:(c + 1) * 512], 512, s * 512)
                if psx == 0 and s == 0:
                    tap(b, "q0", lambda: qst[:, 0:NCH * 512], [128, NCH * 512], BF16)

                for h in range(NH):
                    c, rb = h // 2, (h % 2) * 64
                    qsl = qst[rb:rb + 64, c * 512:(c + 1) * 512]
                    p.step("scalar", lambda e, qsl=qsl: e.activation(
                        out=b.scf[0:64, 0:512], in_=qsl, func=AF.Square))
                    p.step("tensor", lambda e: e.matmul(
                        b.ps_c[0:1, 0:512], b.onesf_col[0:64, :], b.scf[0:64, 0:512],
                        start=True, stop=True))
                    p.step("vector", lambda e, h=h: e.tensor_scalar(
                        out=b.r3[:], in0=b.ps_c[0:1, 0:512], scalar1=nk2[0:1, h:h + 1],
                        scalar2=-0.5, op0=ALU.add, op1=ALU.mult))
                    p.step("vector", lambda e: e.tensor_copy(out=b.rb1[:], in_=b.r3[:]))
                    nkc = 4 * s + 4
                    for kc in range(nkc):
                        p.step("tensor", lambda e, h=h, kc=kc, c=c, rb=rb: e.matmul(
                            b.ps_c[:, 0:512],
                            kbuf[rb:rb + 64, c * T + kc * 128: c * T + (kc + 1) * 128],
                            qst[rb:rb + 64, c * 512:(c + 1) * 512],
                            start=True, stop=False))
                        p.step("tensor", lambda e: e.matmul(
                            b.ps_c[:, 0:512], b.onesb_row[:], b.rb1[:],
                            start=False, stop=True))
                        p.step("scalar", lambda e: e.activation(
                            out=b.ebuf[:], in_=b.ps_c[:, 0:512], func=AF.Exp, scale=0.125))
                        r = kc - 4 * s
                        if r >= 0:
                            p.step("vector", lambda e, r=r: e.tensor_mul(
                                out=b.ebuf[:], in0=b.ebuf[:],
                                in1=b.sup[:, 385 - 128 * r: 897 - 128 * r]))
                        p.step("tensor", lambda e, h=h, kc=kc, last=(kc == nkc - 1): e.matmul(
                            b.ps_a[0:65, 0:512],
                            vt[:, kc * (NH * 65) + h * 65: kc * (NH * 65) + (h + 1) * 65],
                            b.ebuf[:], start=(kc == 0), stop=last))
                    p.step("vector", lambda e: e.tensor_scalar(
                        out=b.r3[:], in0=b.ps_a[64:65, 0:512], scalar1=1e-30, scalar2=None,
                        op0=ALU.add))
                    p.step("vector", lambda e: e.reciprocal(out=b.r4[:], in_=b.r3[:]))
                    p.step("vector", lambda e: e.tensor_copy(out=b.rb1[:], in_=b.r4[:]))
                    p.step("tensor", lambda e: e.matmul(
                        b.ps_c[0:64, 0:512], b.onesb_row[0:1, 0:64], b.rb1[:],
                        start=True, stop=True))
                    p.step("vector", lambda e: e.tensor_copy(
                        out=b.scf[0:64, 0:512], in_=b.ps_c[0:64, 0:512]))
                    p.step("vector", lambda e, h=h, c=c, rb=rb: e.tensor_mul(
                        out=ost[rb:rb + 64, c * 512:(c + 1) * 512],
                        in0=b.ps_a[0:64, 0:512], in1=b.scf[0:64, 0:512]))
                if psx == 0 and s == 0:
                    tap(b, "o0", lambda: ost[:, 0:NCH * 512], [128, NCH * 512], BF16)

                for cp in range(4):
                    p.dma("gpsimd", wop[:].rearrange("p (c n) -> p c n", n=256),
                          di["wo"].rearrange("(c p) n -> p c n", p=128)[:, :, cp * 256:(cp + 1) * 256])
                    for j2 in range(2):
                        oc = cp * 2 + j2
                        for ci in range(NCH):
                            p.step("tensor", lambda e, j2=j2, ci=ci: e.matmul(
                                b.ps_b[:, j2 * 512:(j2 + 1) * 512],
                                wop[:, ci * 256 + j2 * 128: ci * 256 + (j2 + 1) * 128],
                                ost[:, ci * 512:(ci + 1) * 512],
                                start=(ci == 0), stop=(ci == NCH - 1)))
                        p.step("vector", lambda e, j2=j2: e.tensor_copy(
                            out=b.scf[:, 0:512], in_=b.ps_b[:, j2 * 512:(j2 + 1) * 512]))
                        emit_add_to_x(b, b.scf[:, 0:512], s, oc)


def emit_l0_moe(b, ctx, psx):
    p, di = b.prog, b.di
    with ExitStack() as ph:
        def sb(name, shape, dtype):
            return ph.enter_context(b.nc.sbuf_tensor(_uq(name), shape, dtype))
        xnst = sb("xnstm", [128, NCH * 512], BF16)
        w1sb = sb("w1sb", [128, NCH * IM], BF16)
        w2sb = sb("w2sb", [128, NM * 512], BF16)
        hm = sb("hm", [128, NM * 512], BF16)
        acc = sb("macc", [128, 4 * DM], F32)
        cer = sb("cer", [128, 512], BF16)
        cefm = sb("cefm", [16, T], BF16)
        b1t = sb("b1t", [128, NE0 * 16], F32)
        b2sb = sb("b2sb", [16, DM], BF16)
        rwsb = sb("rwsb", [128, NCH * NE0], BF16)
        rbsb = sb("rbsb", [1, NE0], BF16)
        lg = sb("lgts", [128, NE0], F32)

        p.dma("gpsimd", rwsb[:].rearrange("p (c n) -> p c n", n=NE0),
              di["rw"].rearrange("(c p) n -> p c n", p=128))
        p.dma("gpsimd", rbsb[:], di["rb"])
        p.dma("gpsimd", b2sb[:], di["b2"])
        for e0 in range(NE0):
            p.dma("sync", b.scf[0:16, 0:128], di["b1"][e0:e0 + 1, :].rearrange("o (m q) -> (o m) q", q=128))
            p.step("tensor", lambda e, e0=e0: e.transpose(
                b.ps_c[:, 0:16], b.scf[0:16, 0:128], b.eyef[0:16, 0:16]))
            p.step("vector", lambda e, e0=e0: e.tensor_copy(
                out=b1t[:, e0 * 16:(e0 + 1) * 16], in_=b.ps_c[:, 0:16]))

        for t in range(NTILE):
            load_xn_tile(b, xnst, t)
            for fc in range(NCH):
                p.step("tensor", lambda e, fc=fc: e.matmul(
                    b.ps_c[:, 0:NE0],
                    xnst[:, fc * 128:(fc + 1) * 128],
                    rwsb[:, fc * NE0:(fc + 1) * NE0],
                    start=(fc == 0), stop=False))
            p.step("tensor", lambda e: e.matmul(
                b.ps_c[:, 0:NE0], b.onesb_row[:], rbsb[:], start=False, stop=True))
            l0, m1, mr, msk = b.scf[:, 0:NE0], b.col[0], b.col[1], b.scf[:, 64:64 + NE0]
            p.step("vector", lambda e: e.tensor_copy(out=l0, in_=b.ps_c[:, 0:NE0]))
            p.step("vector", lambda e: e.tensor_copy(out=b.scf[:, 32:32 + NE0], in_=l0))
            lw = b.scf[:, 32:32 + NE0]
            for rnd in range(4):
                p.step("vector", lambda e, dst=(m1 if rnd == 0 else mr): e.tensor_reduce(
                    out=dst[:], in_=lw, axis=AX.X, op=ALU.max))
                if rnd < 3:
                    src = m1 if rnd == 0 else mr
                    p.step("vector", lambda e, src=src: e.tensor_scalar(
                        out=msk, in0=lw, scalar1=src[:], scalar2=None, op0=ALU.is_ge))
                    p.step("vector", lambda e: e.scalar_tensor_tensor(
                        out=lw, in0=msk, scalar=-1e30, in1=lw, op0=ALU.mult, op1=ALU.add))
            p.step("vector", lambda e: e.tensor_scalar(
                out=msk, in0=l0, scalar1=mr[:], scalar2=None, op0=ALU.is_ge))
            p.step("vector", lambda e: e.tensor_scalar(
                out=b.col[2][:], in0=m1[:], scalar1=-1.0, scalar2=None, op0=ALU.mult))
            p.step("scalar", lambda e: e.activation(
                out=b.scf[:, 96:96 + NE0], in_=l0, func=AF.Exp, bias=b.col[2][:]))
            p.step("vector", lambda e: e.tensor_mul(
                out=b.scf[:, 96:96 + NE0], in0=b.scf[:, 96:96 + NE0], in1=msk))
            p.step("vector", lambda e: e.tensor_reduce(
                out=b.col[3][:], in_=b.scf[:, 96:96 + NE0], axis=AX.X, op=ALU.add))
            p.step("vector", lambda e: e.reciprocal(out=b.col[3][:], in_=b.col[3][:]))
            p.step("vector", lambda e: e.tensor_scalar(
                out=lg[:], in0=b.scf[:, 96:96 + NE0], scalar1=b.col[3][:], scalar2=None,
                op0=ALU.mult))
            p.step("tensor", lambda e: e.transpose(b.ps_c[0:16, 0:128], lg[:], b.eyef[:]))
            p.step("vector", lambda e, t=t: e.tensor_copy(
                out=cefm[:, t * 128:(t + 1) * 128], in_=b.ps_c[0:16, 0:128]))
        tap(b, f"ce{psx}", lambda: cefm[:], [16, T], BF16)

        for s in range(NSTR):
            load_xn_strip(b, xnst, s)
            for tt in range(4):
                for half in range(2):
                    p.step("tensor", lambda e, tt=tt, half=half, s=s: e.matmul(
                        b.ps_a[:, (tt % 2) * 1024 + half * 512:(tt % 2) * 1024 + (half + 1) * 512],
                        cefm[:, s * 512 + tt * 128: s * 512 + (tt + 1) * 128],
                        b2sb[:, half * 512:(half + 1) * 512],
                        start=True, stop=True))
                for half in range(2):
                    p.step("vector", lambda e, tt=tt, half=half: e.tensor_copy(
                        out=acc[:, tt * 1024 + half * 512: tt * 1024 + (half + 1) * 512],
                        in_=b.ps_a[:, (tt % 2) * 1024 + half * 512:(tt % 2) * 1024 + (half + 1) * 512]))
            for e0 in range(NE0):
                first = (psx == 0 and s == 0)
                if first:
                    p.dma("gpsimd", w1sb[:].rearrange("p (c m) -> p c m", m=IM),
                          di["w1"][e0].rearrange("(c p) m -> p c m", p=128))
                    p.dma("sync", b.d_w1b[e0], w1sb[:])
                else:
                    p.dma("sync", w1sb[:], b.d_w1b[e0])
                p.step("tensor", lambda e, e0=e0, s=s: e.matmul(
                    b.ps_c[:, 0:512], b.e16[:, e0 * 128:(e0 + 1) * 128],
                    cefm[:, s * 512:(s + 1) * 512], start=True, stop=True))
                p.step("vector", lambda e: e.tensor_copy(out=cer[:], in_=b.ps_c[:, 0:512]))
                for mp in range(NM // 2):
                    for k2 in range(2):
                        m = 2 * mp + k2
                        for fc in range(NCH):
                            p.step("tensor", lambda e, m=m, k2=k2, fc=fc: e.matmul(
                                b.ps_b[:, k2 * 512:(k2 + 1) * 512],
                                w1sb[:, fc * IM + m * 128: fc * IM + (m + 1) * 128],
                                xnst[:, fc * 512:(fc + 1) * 512],
                                start=(fc == 0), stop=(fc == NCH - 1)))
                    for k2 in range(2):
                        m = 2 * mp + k2
                        p.step("scalar", lambda e, m=m, k2=k2, e0=e0: e.activation(
                            out=hm[:, m * 512:(m + 1) * 512],
                            in_=b.ps_b[:, k2 * 512:(k2 + 1) * 512], func=AF.Gelu,
                            bias=b1t[:, e0 * 16 + m: e0 * 16 + m + 1]))
                    p.step("vector", lambda e, mp=mp: e.tensor_tensor(
                        out=hm[:, mp * 1024:(mp + 1) * 1024]
                        .rearrange("p (k n) -> p k n", k=2),
                        in0=hm[:, mp * 1024:(mp + 1) * 1024]
                        .rearrange("p (k n) -> p k n", k=2),
                        in1=cer[:].unsqueeze(1).broadcast_to((128, 2, 512)),
                        op=ALU.mult))
                for half in range(2):
                    cache_view = b.d_w2b[e0].rearrange(
                        "p (m o) -> p m o", o=1024)[:, :, half * 512:(half + 1) * 512]
                    if first:
                        p.dma("gpsimd", w2sb[:].rearrange("p (m n) -> p m n", n=512),
                              di["w2"][e0].rearrange("(m p) n -> p m n", p=128)[:, :, half * 512:(half + 1) * 512])
                        p.dma("sync", cache_view, w2sb[:].rearrange("p (m n) -> p m n", n=512))
                    else:
                        p.dma("sync", w2sb[:].rearrange("p (m n) -> p m n", n=512), cache_view)
                    for tt in range(4):
                        for m in range(NM):
                            p.step("tensor", lambda e, tt=tt, half=half, m=m: e.matmul(
                                b.ps_a[:, (tt % 2) * 1024 + half * 512:(tt % 2) * 1024 + (half + 1) * 512],
                                hm[:, m * 512 + tt * 128: m * 512 + (tt + 1) * 128],
                                w2sb[:, m * 512:(m + 1) * 512],
                                start=(m == 0), stop=(m == NM - 1)))
                        p.step("vector", lambda e, tt=tt, half=half: e.tensor_add(
                            out=acc[:, tt * 1024 + half * 512: tt * 1024 + (half + 1) * 512],
                            in0=acc[:, tt * 1024 + half * 512: tt * 1024 + (half + 1) * 512],
                            in1=b.ps_a[:, (tt % 2) * 1024 + half * 512:(tt % 2) * 1024 + (half + 1) * 512]))
            for tt in range(4):
                t = s * 4 + tt
                p.step("vector", lambda e, tt=tt, t=t: e.tensor_add(
                    out=b.x[:, t * DM:(t + 1) * DM],
                    in0=b.x[:, t * DM:(t + 1) * DM],
                    in1=acc[:, tt * 1024:(tt + 1) * 1024]))


def emit_l1_attn(b, ctx, psx):
    p, di = b.prog, b.di
    with ExitStack() as ph:
        def sb(name, shape, dtype):
            return ph.enter_context(b.nc.sbuf_tensor(_uq(name), shape, dtype))
        kh = sb("l1k", [128, NCH * T], BF16)

        Rrow = b.t16
        R2n = b.t16b
        vfac = b.t16c
        p.dma("sync", b.r3[0:1, 0:NH], di["ksc"])
        p.step("vector", lambda e: e.tensor_scalar(
            out=b.r3[0:1, 0:NH], in0=b.r3[0:1, 0:NH], scalar1=10.0, scalar2=KSMAX,
            op0=ALU.mult, op1=ALU.min))
        p.step("scalar", lambda e: e.activation(out=Rrow[:], in_=b.r3[0:1, 0:NH], func=AF.Exp))
        p.step("vector", lambda e: e.tensor_mul(out=R2n[:], in0=Rrow[:], in1=Rrow[:]))
        p.step("vector", lambda e: e.tensor_scalar(
            out=R2n[:], in0=R2n[:], scalar1=-1.0, scalar2=None, op0=ALU.mult))
        p.dma("sync", b.r3[0:1, 0:NH], di["vsc"])
        p.step("scalar", lambda e: e.activation(
            out=vfac[:], in_=b.r3[0:1, 0:NH], func=AF.Exp, scale=10.0))

        def build_wT(name, wt, wnat):
            for ci in range(NCH):
                p.dma("gpsimd", wnat[:], di[name][ci * 128:(ci + 1) * 128, :])
                for co in range(NCH):
                    p.step("tensor", lambda e, co=co: e.transpose(
                        b.ps_tb[:, 0:128], wnat[:, co * 128:(co + 1) * 128], b.eyeb[:]))
                    p.step("vector", lambda e, co=co, ci=ci: e.tensor_copy(
                        out=wt[:, co * 1024 + ci * 128: co * 1024 + (ci + 1) * 128],
                        in_=b.ps_tb[:, 0:128]))

        with ExitStack() as ph2:
            def sb2(name, shape, dtype):
                return ph2.enter_context(b.nc.sbuf_tensor(_uq(name), shape, dtype))
            wt = sb2("l1wtk", [128, NCH * 1024], BF16)
            wnat = sb2("l1wn", [128, 1024], BF16)
            Lt = sb2("lt", [128, NH * 128], BF16)
            dvr = sb2("dvr", [128, NH * 128], BF16)
            betac = sb2("betac", [128, 1], F32)
            xnt = sb2("xnt1", [128, NCH * 128], BF16)
            ring = [sb2(f"ring{i}", [128, 1024], BF16) for i in range(2)]

            build_wT("wk1", wt, wnat)

            p.dma("sync", b.r3[0:1, 0:NH], di["lkb"])
            for h in range(NH):
                p.step("tensor", lambda e, h=h: e.matmul(
                    b.ps_c[:, 0:1], b.onesf_row[:], b.r3[0:1, h:h + 1], start=True, stop=True))
                p.step("scalar", lambda e: e.activation(
                    out=betac[:], in_=b.ps_c[:, 0:1], func=AF.Abs))
                p.step("vector", lambda e: e.tensor_scalar(
                    out=betac[:], in0=betac[:], scalar1=-10.0, scalar2=None, op0=ALU.mult))
                p.step("vector", lambda e: e.tensor_scalar(
                    out=b.scf[:, 0:128], in0=b.dmat[:], scalar1=betac[:], scalar2=0.0,
                    op0=ALU.mult, op1=ALU.min))
                p.step("scalar", lambda e: e.activation(
                    out=b.scf[:, 128:256], in_=b.scf[:, 0:128], func=AF.Exp))
                p.step("vector", lambda e, h=h: e.tensor_mul(
                    out=Lt[:, h * 128:(h + 1) * 128], in0=b.scf[:, 128:256],
                    in1=b.sup[:, 385:513]))
                p.step("vector", lambda e: e.tensor_scalar(
                    out=b.scf[0:1, 0:128], in0=b.dmat[0:1, 0:128],
                    scalar1=1.0, scalar2=None, op0=ALU.add))
                p.step("vector", lambda e: e.tensor_scalar(
                    out=b.scf[0:1, 0:128], in0=b.scf[0:1, 0:128],
                    scalar1=betac[0:1, 0:1], scalar2=None, op0=ALU.mult))
                p.step("scalar", lambda e: e.activation(
                    out=b.scf[0:1, 0:128], in_=b.scf[0:1, 0:128], func=AF.Exp))
                p.step("vector", lambda e: e.tensor_copy(
                    out=b.rb1[0:1, 0:128], in_=b.scf[0:1, 0:128]))
                p.step("tensor", lambda e: e.matmul(
                    b.ps_c[:, 0:128], b.onesb_row[0:1, 0:128], b.rb1[0:1, 0:128],
                    start=True, stop=True))
                p.step("vector", lambda e, h=h: e.tensor_copy(
                    out=dvr[:, h * 128:(h + 1) * 128], in_=b.ps_c[:, 0:128]))
            tap(b, f"lt{psx}", lambda: Lt[:], [128, NH * 128], BF16)

            for t in range(NTILE):
                load_xn_tile(b, xnt, t)
                for half in range(2):
                    for fc in range(NCH):
                        p.step("tensor", lambda e, half=half, fc=fc: e.matmul(
                            b.ps_b[:, half * 512:(half + 1) * 512],
                            xnt[:, fc * 128:(fc + 1) * 128],
                            wt[:, fc * 1024 + half * 512: fc * 1024 + (half + 1) * 512],
                            start=(fc == 0), stop=(fc == NCH - 1)))
                p.step("scalar", lambda e, t=t: e.activation(
                    out=ring[t % 2][:], in_=b.ps_b[:, 0:1024], func=AF.Identity))
                ktm = ring[t % 2]
                for h in range(NH):
                    p.step("tensor", lambda e, h=h, ktm=ktm: e.matmul(
                        b.ps_a[0:64, h * 128:(h + 1) * 128],
                        ktm[:, h * 64:(h + 1) * 64],
                        Lt[:, h * 128:(h + 1) * 128], start=True, stop=True))
                for h in range(NH):
                    c, rb = h // 2, (h % 2) * 64
                    dst = kh[rb:rb + 64, c * T + t * 128: c * T + (t + 1) * 128]
                    pssl = b.ps_a[0:64, h * 128:(h + 1) * 128]
                    if t == 0:
                        p.step("vector", lambda e, dst=dst, pssl=pssl: e.tensor_copy(
                            out=dst, in_=pssl))
                    else:
                        carry = kh[rb:rb + 64, c * T + t * 128 - 1: c * T + t * 128]
                        p.step("vector", lambda e, dst=dst, pssl=pssl, carry=carry, h=h, rb=rb:
                               e.scalar_tensor_tensor(
                                   out=dst, in0=dvr[rb:rb + 64, h * 128:(h + 1) * 128],
                                   scalar=carry, in1=pssl, op0=ALU.mult, op1=ALU.add))
            tap(b, f"ksm{psx}", lambda: kh[:, 0:T], [128, T], BF16)

            for h in range(NH):
                c, rb = h // 2, (h % 2) * 64
                p.step("scalar", lambda e, c=c, rb=rb: e.activation(
                    out=b.scf[0:64, 0:T], in_=kh[rb:rb + 64, c * T:(c + 1) * T],
                    func=AF.Square))
                for s in range(NSTR):
                    p.step("tensor", lambda e, s=s: e.matmul(
                        b.ps_c[0:1, 0:512], b.onesf_col[0:64, :],
                        b.scf[0:64, s * 512:(s + 1) * 512], start=True, stop=True))
                    p.step("scalar", lambda e: e.activation(
                        out=b.r3[:], in_=b.ps_c[0:1, 0:512], func=AF.Sqrt))
                    p.step("vector", lambda e: e.tensor_scalar(
                        out=b.r3[:], in0=b.r3[:], scalar1=1e-10, scalar2=None, op0=ALU.add))
                    p.step("vector", lambda e: e.reciprocal(out=b.r3[:], in_=b.r3[:]))
                    p.step("vector", lambda e, h=h: e.tensor_scalar(
                        out=b.rb1[:], in0=b.r3[:], scalar1=Rrow[0:1, h:h + 1], scalar2=None,
                        op0=ALU.mult))
                    p.step("tensor", lambda e: e.matmul(
                        b.ps_c[:, 0:512], b.onesb_row[0:1, 0:128], b.rb1[:],
                        start=True, stop=True))
                    p.step("vector", lambda e, rb=rb: e.tensor_copy(
                        out=b.scf[rb:rb + 64, 0:512], in_=b.ps_c[0:64, 0:512]))
                    p.step("vector", lambda e, h=h, c=c, rb=rb, s=s: e.tensor_mul(
                        out=kh[rb:rb + 64, c * T + s * 512: c * T + (s + 1) * 512],
                        in0=kh[rb:rb + 64, c * T + s * 512: c * T + (s + 1) * 512],
                        in1=b.scf[rb:rb + 64, 0:512]))
            tap(b, f"khat{psx}", lambda: kh[:, 0:T], [128, T], BF16)

        with ExitStack() as ph2:
            def sb2(name, shape, dtype):
                return ph2.enter_context(b.nc.sbuf_tensor(_uq(name), shape, dtype))
            vt = sb2("l1vt", [128, NTILE * NH * 65], BF16)
            wtv = sb2("l1wtv", [128, NCH * 1024], BF16)
            wnatv = sb2("l1wn2", [128, 1024], BF16)
            xntv = sb2("xnt2", [128, NCH * 128], BF16)
            vring = [sb2(f"vring{i}", [128, 1024], BF16) for i in range(2)]
            vcf = sb2("vcfr", [128, NH], F32)
            vfr = sb2("vfacr", [128, NH], F32)
            nrm = sb2("vnrm", [128, NH], F32)
            r2c = sb2("r2c", [128, NH], F32)
            ost = sb2("ost1", [128, NCH * 512], BF16)

            build_wT("wv1", wtv, wnatv)
            p.step("vector", lambda e: e.memset(
                vt[:].rearrange("p (t h c) -> p t h c", h=NH, c=65)[:, :, :, 64:65], 1.0))
            p.dma("sync", b.r3[0:1, 0:NH], di["vcf"])
            p.step("tensor", lambda e: e.matmul(
                b.ps_c[:, 0:NH], b.onesf_row[:], b.r3[0:1, 0:NH], start=True, stop=True))
            p.step("vector", lambda e: e.tensor_copy(out=vcf[:], in_=b.ps_c[:, 0:NH]))
            p.step("tensor", lambda e: e.matmul(
                b.ps_c[:, 0:NH], b.onesf_row[:], vfac[:], start=True, stop=True))
            p.step("vector", lambda e: e.tensor_copy(out=vfr[:], in_=b.ps_c[:, 0:NH]))

            for t in range(NTILE + 1):
                if t < NTILE:
                    load_xn_tile(b, xntv, t)
                    for half in range(2):
                        for fc in range(NCH):
                            p.step("tensor", lambda e, half=half, fc=fc: e.matmul(
                                b.ps_b[:, half * 512:(half + 1) * 512],
                                xntv[:, fc * 128:(fc + 1) * 128],
                                wtv[:, fc * 1024 + half * 512: fc * 1024 + (half + 1) * 512],
                                start=(fc == 0), stop=(fc == NCH - 1)))
                    p.step("scalar", lambda e, t=t: e.activation(
                        out=vring[t % 2][:], in_=b.ps_b[:, 0:1024], func=AF.Identity))
                if t == 0:
                    continue
                tp = t - 1
                vcur = vring[tp % 2]
                vnxt = vring[t % 2] if t < NTILE else None
                shp = b.scb
                if vnxt is None:
                    p.step("vector", lambda e: e.memset(b.scb[:, 0:1024], 0.0))
                p.dma("sync", shp[0:127, 0:1024], vcur[1:128, :])
                if vnxt is not None:
                    p.dma("sync", shp[127:128, 0:1024], vnxt[0:1, :])
                p.step("vector", lambda e, vcur=vcur: e.tensor_sub(
                    out=b.scf[:, 0:1024], in0=vcur[:], in1=shp[:, 0:1024]))
                p.step("vector", lambda e: e.tensor_tensor(
                    out=b.scf[:, 0:1024].rearrange("p (h c) -> p h c", c=64),
                    in0=b.scf[:, 0:1024].rearrange("p (h c) -> p h c", c=64),
                    in1=vcf[:].unsqueeze(2).broadcast_to((128, NH, 64)),
                    op=ALU.mult))
                p.step("vector", lambda e: e.tensor_add(
                    out=b.scf[:, 0:1024], in0=b.scf[:, 0:1024], in1=shp[:, 0:1024]))
                p.step("scalar", lambda e: e.activation(
                    out=b.scf[:, 1024:2048], in_=b.scf[:, 0:1024], func=AF.Square))
                p.step("vector", lambda e: e.tensor_reduce(
                    out=nrm[:], in_=b.scf[:, 1024:2048].rearrange("p (h c) -> p h c", c=64),
                    axis=AX.X, op=ALU.add))
                p.step("scalar", lambda e: e.activation(out=nrm[:], in_=nrm[:], func=AF.Sqrt))
                p.step("vector", lambda e: e.tensor_scalar(
                    out=nrm[:], in0=nrm[:], scalar1=1e-10, scalar2=None, op0=ALU.add))
                p.step("vector", lambda e: e.reciprocal(out=nrm[:], in_=nrm[:]))
                p.step("vector", lambda e: e.tensor_mul(out=nrm[:], in0=nrm[:], in1=vfr[:]))
                p.step("vector", lambda e, tp=tp: e.tensor_tensor(
                    out=vt[:, tp * (NH * 65):(tp + 1) * (NH * 65)]
                    .rearrange("p (h c) -> p h c", c=65)[:, :, 0:64],
                    in0=b.scf[:, 0:1024].rearrange("p (h c) -> p h c", c=64),
                    in1=nrm[:].unsqueeze(2).broadcast_to((128, NH, 64)),
                    op=ALU.mult))
            tap(b, f"vhat{psx}", lambda: vt[:, 0:NH * 65], [128, NH * 65], BF16)

            build_wT("wc1", wtv, wnatv)

            p.step("tensor", lambda e: e.matmul(
                b.ps_c[:, 0:NH], b.onesf_row[:], R2n[:], start=True, stop=True))
            p.step("vector", lambda e: e.tensor_copy(out=r2c[:], in_=b.ps_c[:, 0:NH]))

            for s in range(NSTR):
                for h in range(NH):
                    c, rb = h // 2, (h % 2) * 64
                    nkc = 4 * s + 4
                    for kc in range(nkc):
                        p.step("tensor", lambda e, kc=kc, c=c, rb=rb, s=s: e.matmul(
                            b.ps_c[:, 0:512],
                            kh[rb:rb + 64, c * T + kc * 128: c * T + (kc + 1) * 128],
                            kh[rb:rb + 64, c * T + s * 512: c * T + (s + 1) * 512],
                            start=True, stop=True))
                        p.step("scalar", lambda e, h=h: e.activation(
                            out=b.ebuf[:], in_=b.ps_c[:, 0:512], func=AF.Exp,
                            bias=r2c[:, h:h + 1]))
                        r = kc - 4 * s
                        if r >= 0:
                            p.step("vector", lambda e, r=r: e.tensor_mul(
                                out=b.ebuf[:], in0=b.ebuf[:],
                                in1=b.sup[:, 384 - 128 * r: 896 - 128 * r]))
                        p.step("tensor", lambda e, kc=kc, h=h, last=(kc == nkc - 1): e.matmul(
                            b.ps_a[0:65, 0:512],
                            vt[:, kc * (NH * 65) + h * 65: kc * (NH * 65) + (h + 1) * 65],
                            b.ebuf[:], start=(kc == 0), stop=last))
                    p.step("vector", lambda e: e.tensor_scalar(
                        out=b.r3[0:1, 0:512], in0=b.ps_a[64:65, 0:512], scalar1=1e-30,
                        scalar2=None, op0=ALU.add))
                    p.step("vector", lambda e: e.reciprocal(
                        out=b.r4[0:1, 0:512], in_=b.r3[0:1, 0:512]))
                    p.step("vector", lambda e: e.tensor_copy(
                        out=b.rb1[0:1, 0:512], in_=b.r4[0:1, 0:512]))
                    p.step("tensor", lambda e: e.matmul(
                        b.ps_c[0:64, 0:512], b.onesb_row[0:1, 0:64], b.rb1[0:1, 0:512],
                        start=True, stop=True))
                    p.step("vector", lambda e: e.tensor_copy(
                        out=b.scf[0:64, 0:512], in_=b.ps_c[0:64, 0:512]))
                    p.step("vector", lambda e, h=h, c=c, rb=rb: e.tensor_mul(
                        out=ost[rb:rb + 64, c * 512:(c + 1) * 512],
                        in0=b.ps_a[0:64, 0:512], in1=b.scf[0:64, 0:512]))
                for oc in range(NCH):
                    for ci in range(NCH):
                        p.step("tensor", lambda e, oc=oc, ci=ci: e.matmul(
                            b.ps_b[:, 0:512],
                            wtv[:, ci * 1024 + oc * 128: ci * 1024 + (oc + 1) * 128],
                            ost[:, ci * 512:(ci + 1) * 512],
                            start=(ci == 0), stop=(ci == NCH - 1)))
                    p.step("vector", lambda e: e.tensor_copy(
                        out=b.scf[:, 0:512], in_=b.ps_b[:, 0:512]))
                    emit_add_to_x(b, b.scf[:, 0:512], s, oc)


def emit_add_shifted(b, src_ap, s, oc, qn):
    p = b.prog
    for st in range((qn + 127) // 128):
        w = min(128, qn - st * 128)
        p.step("tensor", lambda e, st=st, w=w: e.transpose(
            b.ps_c[:, 0:w] if False else b.ps_c[0:w, 0:128],
            src_ap[:, st * 128:st * 128 + w], b.eyef[:]))
        pos0 = s * 512 + 1 + st * 128
        t0, r0 = pos0 // 128, pos0 % 128
        n1 = min(w, 128 - r0)
        p.step("vector", lambda e, t0=t0, r0=r0, n1=n1, oc=oc: e.tensor_add(
            out=b.x[r0:r0 + n1, t0 * DM + oc * 128: t0 * DM + (oc + 1) * 128],
            in0=b.x[r0:r0 + n1, t0 * DM + oc * 128: t0 * DM + (oc + 1) * 128],
            in1=b.ps_c[0:n1, 0:128]))
        if w > n1:
            n2 = w - n1
            p.step("vector", lambda e, t0=t0, n1=n1, n2=n2, oc=oc: e.tensor_add(
                out=b.x[0:n2, (t0 + 1) * DM + oc * 128: (t0 + 1) * DM + (oc + 1) * 128],
                in0=b.x[0:n2, (t0 + 1) * DM + oc * 128: (t0 + 1) * DM + (oc + 1) * 128],
                in1=b.ps_c[n1:n1 + n2, 0:128]))


def emit_l1_ffn(b, ctx, psx):
    p, di = b.prog, b.di
    with ExitStack() as ph:
        def sb(name, shape, dtype):
            return ph.enter_context(b.nc.sbuf_tensor(_uq(name), shape, dtype))
        xnst = sb("xnstf", [128, NCH * 512], BF16)
        z = sb("zrow", [8, T], F32)
        gsel = sb("gsel", [8, T], F32)
        gselb = sb("gselb", [8, T], BF16)
        selT = sb("selT", [128, NTILE * 8], F32)
        biask = sb("biask", [128, NTILE * 8], F32)
        wrT = sb("wrT", [128, NCH * 8], BF16)
        qe = sb("qe", [128, T], BF16)
        ke = sb("ke", [128, T], BF16)
        ve = sb("ve", [128, NTILE * 128], BF16)
        wqkv = sb("wqkv", [128, NCH * 384], BF16)
        wo1 = sb("wo1s", [128, DM], BF16)
        ost = sb("ostf", [128, 512], BF16)
        lo = sb("lo8", [8, 1], F32)
        hi = sb("hi8", [8, 1], F32)
        th = sb("th8", [8, 1], F32)
        cnt = sb("cnt8", [8, 1], F32)
        dput = sb("dput8", [8, 1], F32)

        p.dma("gpsimd", b.scb[0:8, 0:DM], di["wr1"])
        for fc in range(NCH):
            p.step("tensor", lambda e, fc=fc: e.transpose(
                b.ps_tb[:, 0:8], b.scb[0:8, fc * 128:(fc + 1) * 128], b.eyeb[0:8, 0:8]))
            p.step("vector", lambda e, fc=fc: e.tensor_copy(
                out=wrT[:, fc * 8:(fc + 1) * 8], in_=b.ps_tb[:, 0:8]))

        for s in range(NSTR):
            load_xn_strip(b, xnst, s)
            for fc in range(NCH):
                p.step("tensor", lambda e, fc=fc: e.matmul(
                    b.ps_c[0:8, 0:512], wrT[:, fc * 8:(fc + 1) * 8],
                    xnst[:, fc * 512:(fc + 1) * 512],
                    start=(fc == 0), stop=(fc == NCH - 1)))
            p.step("vector", lambda e, s=s: e.tensor_copy(
                out=z[:, s * 512:(s + 1) * 512], in_=b.ps_c[0:8, 0:512]))

        p.step("vector", lambda e: e.memset(lo[:], -100.0))
        p.step("vector", lambda e: e.memset(hi[:], 100.0))
        for it in range(26):
            p.step("vector", lambda e: e.tensor_add(out=th[:], in0=lo[:], in1=hi[:]))
            p.step("vector", lambda e: e.tensor_scalar(
                out=th[:], in0=th[:], scalar1=0.5, scalar2=None, op0=ALU.mult))
            p.step("vector", lambda e: e.tensor_scalar(
                out=b.scf[0:8, 0:T], in0=z[:], scalar1=th[:], scalar2=None, op0=ALU.is_ge))
            p.step("vector", lambda e: e.tensor_reduce(
                out=cnt[:], in_=b.scf[0:8, 0:T], axis=AX.X, op=ALU.add))
            p.step("vector", lambda e: e.tensor_scalar(
                out=dput[:], in0=cnt[:], scalar1=256.0, scalar2=None, op0=ALU.is_ge))
            p.step("vector", lambda e: e.tensor_sub(out=cnt[:], in0=th[:], in1=lo[:]))
            p.step("vector", lambda e: e.scalar_tensor_tensor(
                out=lo[:], in0=dput[:], scalar=cnt[:], in1=lo[:], op0=ALU.mult, op1=ALU.add))
            p.step("vector", lambda e: e.tensor_scalar(
                out=dput[:], in0=dput[:], scalar1=-1.0, scalar2=1.0, op0=ALU.mult, op1=ALU.add))
            p.step("vector", lambda e: e.tensor_sub(out=cnt[:], in0=th[:], in1=hi[:]))
            p.step("vector", lambda e: e.scalar_tensor_tensor(
                out=hi[:], in0=dput[:], scalar=cnt[:], in1=hi[:], op0=ALU.mult, op1=ALU.add))
        p.step("vector", lambda e: e.tensor_scalar(
            out=b.scf[0:8, 0:T], in0=z[:], scalar1=lo[:], scalar2=None, op0=ALU.is_ge))
        p.step("scalar", lambda e: e.activation(out=gsel[:], in_=z[:], func=AF.Sigmoid))
        p.step("vector", lambda e: e.tensor_mul(
            out=gsel[:], in0=gsel[:], in1=b.scf[0:8, 0:T]))
        p.step("vector", lambda e: e.tensor_copy(out=gselb[:], in_=gsel[:]))
        for t in range(NTILE):
            p.step("tensor", lambda e, t=t: e.transpose(
                b.ps_c[:, 0:8], b.scf[0:8, t * 128:(t + 1) * 128], b.eyef[0:8, 0:8]))
            p.step("vector", lambda e, t=t: e.tensor_copy(
                out=selT[:, t * 8:(t + 1) * 8], in_=b.ps_c[:, 0:8]))
        p.step("vector", lambda e: e.tensor_scalar(
            out=biask[:], in0=selT[:], scalar1=-1.0, scalar2=None, op0=ALU.add))
        p.step("vector", lambda e: e.tensor_scalar(
            out=biask[:], in0=biask[:], scalar1=30000.0, scalar2=None, op0=ALU.mult))
        tap(b, f"gsel{psx}", lambda: gsel[:], [8, T], F32)

        for e1 in range(NE1):
            p.dma("gpsimd", wqkv[:].rearrange("p (c n) -> p c n", n=384),
                  di["wqkv1"][e1].rearrange("(c p) n -> p c n", p=128))
            p.dma("gpsimd", wo1[:], di["wo1"][e1])
            for s in range(NSTR):
                load_xn_strip(b, xnst, s)
                for which, dst in ((0, qe), (1, ke)):
                    for fc in range(NCH):
                        p.step("tensor", lambda e, which=which, fc=fc: e.matmul(
                            b.ps_b[:, 0:512],
                            wqkv[:, fc * 384 + which * 128: fc * 384 + (which + 1) * 128],
                            xnst[:, fc * 512:(fc + 1) * 512],
                            start=(fc == 0), stop=(fc == NCH - 1)))
                    p.step("scalar", lambda e, dst=dst, s=s: e.activation(
                        out=dst[:, s * 512:(s + 1) * 512], in_=b.ps_b[:, 0:512],
                        func=AF.Identity))
                for tt in range(4):
                    t = s * 4 + tt
                    for fc in range(NCH):
                        p.step("tensor", lambda e, t=t, tt=tt, fc=fc: e.matmul(
                            b.ps_c[:, 0:128],
                            xnst[:, fc * 512 + tt * 128: fc * 512 + (tt + 1) * 128],
                            wqkv[:, fc * 384 + 256: fc * 384 + 384],
                            start=(fc == 0), stop=(fc == NCH - 1)))
                    p.step("vector", lambda e, t=t: e.tensor_copy(
                        out=ve[:, t * 128:(t + 1) * 128], in_=b.ps_c[:, 0:128]))
            emit_rope(b, qe[0:64, 0:T], T, 0, rows=64)
            emit_rope(b, ke[0:64, 0:T], T, 0, rows=64)
            if psx == 0 and e1 == 0:
                tap(b, "qe0", lambda: qe[:], [128, T], BF16)
                tap(b, "ve0", lambda: ve[:], [128, NTILE * 128], BF16)

            for s in range(NSTR):
                if s == 0:
                    p.step("scalar", lambda e: e.activation(
                        out=b.scf[:, 0:T], in_=ke[:], func=AF.Square))
                    for ss in range(NSTR):
                        p.step("tensor", lambda e, ss=ss: e.matmul(
                            b.ps_c[0:1, 0:512], b.onesf_col[:],
                            b.scf[:, ss * 512:(ss + 1) * 512], start=True, stop=True))
                        p.step("vector", lambda e, ss=ss: e.tensor_copy(
                            out=(b.r3 if ss == 0 else b.r4)[0:1, 0:512],
                            in_=b.ps_c[0:1, 0:512]))
                        if ss > 0:
                            p.step("vector", lambda e: e.tensor_max(
                                out=b.r3[:], in0=b.r3[:], in1=b.r4[:]))
                    p.step("vector", lambda e: e.tensor_reduce(
                        out=b.col[5][0:1, :], in_=b.r3[:], axis=AX.X, op=ALU.max))
                p.step("scalar", lambda e, s=s: e.activation(
                    out=b.scf[:, 0:512], in_=qe[:, s * 512:(s + 1) * 512], func=AF.Square))
                p.step("tensor", lambda e: e.matmul(
                    b.ps_c[0:1, 0:512], b.onesf_col[:], b.scf[:, 0:512],
                    start=True, stop=True))
                p.step("vector", lambda e: e.tensor_scalar(
                    out=b.r3[0:1, 0:512], in0=b.ps_c[0:1, 0:512],
                    scalar1=b.col[5][0:1, 0:1], scalar2=-0.5, op0=ALU.add, op1=ALU.mult))
                p.step("vector", lambda e: e.tensor_copy(out=b.rb1[:], in_=b.r3[0:1, 0:512]))
                nkc = 4 * s + 4
                for kc in range(nkc):
                    p.step("tensor", lambda e, kc=kc, s=s: e.matmul(
                        b.ps_c[:, 0:512],
                        ke[:, kc * 128:(kc + 1) * 128], qe[:, s * 512:(s + 1) * 512],
                        start=True, stop=False))
                    p.step("tensor", lambda e: e.matmul(
                        b.ps_c[:, 0:512], b.onesb_row[:], b.rb1[:], start=False, stop=True))
                    p.step("scalar", lambda e, kc=kc: e.activation(
                        out=b.ebuf[:], in_=b.ps_c[:, 0:512], func=AF.Exp,
                        scale=float(1.0 / np.sqrt(HP)),
                        bias=biask[:, kc * 8 + e1: kc * 8 + e1 + 1]))
                    r = kc - 4 * s
                    if r >= 0:
                        p.step("vector", lambda e, r=r: e.tensor_mul(
                            out=b.ebuf[:], in0=b.ebuf[:],
                            in1=b.sup[:, 385 - 128 * r: 897 - 128 * r]))
                    p.step("tensor", lambda e, kc=kc, last=(kc == nkc - 1): e.matmul(
                        b.ps_a[:, 0:512], ve[:, kc * 128:(kc + 1) * 128], b.ebuf[:],
                        start=(kc == 0), stop=last))
                    p.step("tensor", lambda e, kc=kc, last=(kc == nkc - 1): e.matmul(
                        b.ps_b[0:1, 0:512], b.onesb_col[:], b.ebuf[:],
                        start=(kc == 0), stop=last))
                p.step("vector", lambda e: e.tensor_scalar(
                    out=b.r3[0:1, 0:512], in0=b.ps_b[0:1, 0:512], scalar1=1e-30,
                    scalar2=None, op0=ALU.add))
                p.step("vector", lambda e: e.reciprocal(out=b.r4[0:1, 0:512], in_=b.r3[0:1, 0:512]))
                p.step("vector", lambda e: e.tensor_copy(out=b.rb1[:], in_=b.r4[0:1, 0:512]))
                p.step("tensor", lambda e: e.matmul(
                    b.ps_c[:, 0:512], b.onesb_row[:], b.rb1[:], start=True, stop=True))
                p.step("vector", lambda e: e.tensor_copy(
                    out=b.scf[:, 0:512], in_=b.ps_c[:, 0:512]))
                p.step("tensor", lambda e, e1=e1, s=s: e.matmul(
                    b.ps_c[:, 0:512], b.e8[:, e1 * 128:(e1 + 1) * 128],
                    gselb[:, s * 512:(s + 1) * 512], start=True, stop=True))
                p.step("vector", lambda e: e.tensor_mul(
                    out=b.scf[:, 0:512], in0=b.scf[:, 0:512], in1=b.ps_c[:, 0:512]))
                p.step("vector", lambda e: e.tensor_mul(
                    out=ost[:, 0:512], in0=b.ps_a[:, 0:512], in1=b.scf[:, 0:512]))
                for tt in range(4):
                    t = s * 4 + tt
                    for half in range(2):
                        p.step("tensor", lambda e, tt=tt, half=half: e.matmul(
                            b.ps_b[:, half * 512:(half + 1) * 512],
                            ost[:, tt * 128:(tt + 1) * 128],
                            wo1[:, half * 512:(half + 1) * 512],
                            start=True, stop=True))
                    for half in range(2):
                        p.step("vector", lambda e, t=t, half=half: e.tensor_add(
                            out=b.x[:, t * DM + half * 512: t * DM + (half + 1) * 512],
                            in0=b.x[:, t * DM + half * 512: t * DM + (half + 1) * 512],
                            in1=b.ps_b[:, half * 512:(half + 1) * 512]))


_CACHE = {}


def kernel(**inputs):
    taps = inputs.pop("_taps", ())
    blocks = inputs.pop("_blocks", (0, 1))
    key = (tuple(taps), tuple(blocks))
    if key not in _CACHE:
        _CACHE[key] = build_program(blocks=blocks, taps=taps)
    nc, b = _CACHE[key]

    hc = host_consts()
    wmap = {
        "wq": inputs["L0_attn_W_q"], "wdkv": inputs["L0_attn_W_dkv"],
        "wuk": inputs["L0_attn_W_uk"], "wuv": inputs["L0_attn_W_uv"],
        "wo": inputs["L0_attn_W_o"],
        "lng": inputs["L0_attn_ln_gamma"].reshape(1, LAT),
        "lnb": inputs["L0_attn_ln_beta"].reshape(1, LAT),
        "rw": inputs["L0_ffn_router_w"], "rb": inputs["L0_ffn_router_b"].reshape(1, NE0),
        "w1": inputs["L0_ffn_w1"], "b1": inputs["L0_ffn_b1"],
        "w2": inputs["L0_ffn_w2"], "b2": inputs["L0_ffn_b2"],
        "wk1": inputs["L1_attn_W_k"], "wv1": inputs["L1_attn_W_v"],
        "wc1": inputs["L1_attn_W_c"],
        "lkb": inputs["L1_attn_leaky_key_beta"].reshape(1, NH),
        "ksc": inputs["L1_attn_key_scale"].reshape(1, NH),
        "vcf": inputs["L1_attn_v_coef"].reshape(1, NH),
        "vsc": inputs["L1_attn_val_scale"].reshape(1, NH),
        "wr1": inputs["L1_ffn_Wr"], "wqkv1": inputs["L1_ffn_Wqkv"],
        "wo1": inputs["L1_ffn_Wo"],
    }
    wmap.update(hc)
    x = inputs["L0_attn_x"]
    in_maps = []
    for c in range(8):
        m = dict(wmap)
        m["xin"] = np.ascontiguousarray(x[[c, c + 8]])
        in_maps.append(m)
    res = run_bass_kernel_spmd(nc, in_maps, core_ids=list(range(8)))
    out = np.empty((16, T, DM), dtype=np.float32)
    for c in range(8):
        out[c] = res.results[c]["out"][0]
        out[c + 8] = res.results[c]["out"][1]
    if taps:
        tapout = {nm: [res.results[c]["tap_" + nm] for c in range(8)] for nm in b.taps}
        return out, tapout
    return out


# version 06
# speedup vs baseline: 1.8225x; 1.0172x over previous
import numpy as np
import ml_dtypes
from contextlib import ExitStack

import concourse.bass as bass
import concourse.mybir as mybir
from concourse.bass_utils import run_bass_kernel_spmd

F32 = mybir.dt.float32
BF16 = mybir.dt.bfloat16
I32 = mybir.dt.int32
AF = mybir.ActivationFunctionType
ALU = mybir.AluOpType
AX = mybir.AxisListType

T = 2048
DM = 1024
NH = 16
HD = 64
LAT = 256
NE0 = 16
IM = 2048
NE1 = 8
HP = 128
KSMAX = float(np.log(2.0 ** 16 - 1.0))
NTILE = T // 128
NCH = DM // 128
NSTR = T // 512
NM = IM // 128

ENGINES = ("sync", "gpsimd", "vector", "scalar", "tensor")


class Prog:
    def __init__(self):
        self.steps = []

    def step(self, eng, fn, inc=1):
        self.steps.append((eng, fn, inc))

    def dma(self, eng, out, in_):
        self.step(eng, lambda e, o=out, i=in_: e.dma_start(out=o, in_=i), 16)

    def build(self, block, sem):
        waits = []
        total = 0
        for _, _, inc in self.steps:
            waits.append(total)
            total += inc
        groups = []
        MERGEABLE = ("tensor", "sync", "gpsimd", "scalar")
        for i, (e, fn, inc) in enumerate(self.steps):
            if groups and groups[-1][0] == e and e in MERGEABLE:
                groups[-1][2].append((fn, inc))
            else:
                groups.append((e, waits[i], [(fn, inc)]))
        per_engine = {e: [] for e in ENGINES}
        for e, w, members in groups:
            per_engine[e].append((w, members))
        for eng_name in ENGINES:
            if not per_engine[eng_name]:
                continue
            is_dma = eng_name in ("sync", "gpsimd")

            def make_body(mygroups, is_dma):
                def body(eng):
                    for w, members in mygroups:
                        if w > 0:
                            eng.wait_ge(sem, w)
                        if is_dma:
                            for fn, inc in members:
                                fn(eng).then_inc(sem, inc)
                        else:
                            tot = sum(inc for _, inc in members)
                            for j, (fn, inc) in enumerate(members):
                                inst = fn(eng)
                                if j == len(members) - 1:
                                    inst.then_inc(sem, tot)
                return body
            getattr(block, eng_name)(make_body(per_engine[eng_name], is_dma))
        return total


def host_consts():
    j = np.arange(128)
    u = np.arange(897)
    c = {}
    c["c_eye"] = np.eye(128, dtype=np.float32)
    c["c_sup"] = (j[:, None] <= (u[None, :] - 385)).astype(np.float32)
    c["c_dmat"] = (j[None, :] - j[:, None]).astype(np.float32)
    c["c_iota_row"] = np.arange(T, dtype=np.float32)[None, :]
    c["c_iota_col"] = np.arange(128, dtype=np.float32)[:, None]
    e16 = np.zeros((16, 16 * 128), dtype=np.float32)
    for e in range(16):
        e16[e, e * 128:(e + 1) * 128] = 1.0
    c["c_e16"] = e16
    e8 = np.zeros((8, 8 * 128), dtype=np.float32)
    for e in range(8):
        e8[e, e * 128:(e + 1) * 128] = 1.0
    c["c_e8"] = e8
    return c


class B:
    pass


_UID = [0]


def _uq(name):
    _UID[0] += 1
    return f"{name}_u{_UID[0]}"




def build_program(blocks=(0, 1), n_pass=2, taps=()):
    nc = bass.Bass("TRN2", target_bir_lowering=False)
    b = B()
    b.nc = nc
    b.prog = Prog()
    b.taps = {}
    b.tapset = set(taps)
    p = b.prog

    di = {}
    di["xin"] = nc.dram_tensor("xin", [n_pass, T, DM], F32, kind="ExternalInput").ap()
    for nm, shp in [
        ("wq", [DM, DM]), ("wdkv", [DM, LAT]), ("wuk", [LAT, DM]), ("wuv", [LAT, DM]),
        ("wo", [DM, DM]), ("lng", [1, LAT]), ("lnb", [1, LAT]),
        ("rw", [DM, NE0]), ("rb", [1, NE0]),
        ("w1", [NE0, DM, IM]), ("b1", [NE0, IM]), ("w2", [NE0, IM, DM]), ("b2", [NE0, DM]),
        ("wk1", [DM, DM]), ("wv1", [DM, DM]), ("wc1", [DM, DM]),
        ("lkb", [1, NH]), ("ksc", [1, NH]), ("vcf", [1, NH]), ("vsc", [1, NH]),
        ("wr1", [NE1, DM]), ("wqkv1", [NE1, DM, 3 * HP]), ("wo1", [NE1, HP, DM]),
        ("c_eye", [128, 128]), ("c_sup", [128, 897]), ("c_dmat", [128, 128]),
        ("c_iota_row", [1, T]), ("c_iota_col", [128, 1]),
        ("c_e16", [16, 2048]), ("c_e8", [8, 1024]),
    ]:
        di[nm] = nc.dram_tensor(nm, shp, F32, kind="ExternalInput").ap()
    b.di = di
    b.dout = nc.dram_tensor("out", [n_pass, T, DM], F32, kind="ExternalOutput").ap()
    b.d_xn = nc.dram_tensor("d_xn", [NCH, 128, T], BF16).ap()
    b.d_w1b = nc.dram_tensor("d_w1b", [NE0, 128, NCH * IM], BF16).ap()
    b.d_w2b = nc.dram_tensor("d_w2b", [NE0, 128, NM * DM], BF16).ap()

    with ExitStack() as ctx:
        def sb(name, shape, dtype):
            return ctx.enter_context(nc.sbuf_tensor(name, shape, dtype))
        b.sb_top = sb

        b.x = sb("x", [128, NTILE * DM], F32)
        b.chc = sb("chc", [128, T], BF16)
        b.chs = sb("chs", [128, T], BF16)
        b.eyeb = sb("eyeb", [128, 128], BF16)
        b.eyef = sb("eyef", [128, 128], F32)
        b.sup = sb("supm", [128, 897], BF16)
        b.dmat = sb("dmatc", [128, 128], F32)
        b.e16 = sb("e16c", [16, 2048], BF16)
        b.e8 = sb("e8c", [8, 1024], BF16)
        b.onesb_row = sb("onesb_row", [1, 128], BF16)
        b.onesf_row = sb("onesf_row", [1, 128], F32)
        b.onesb_col = sb("onesb_col", [128, 1], BF16)
        b.onesf_col = sb("onesf_col", [128, 1], F32)
        b.iotac = sb("iotac", [128, 1], F32)
        b.r3 = sb("r3", [1, 512], F32)
        b.r4 = sb("r4", [1, 512], F32)
        b.rb1 = sb("rb1", [1, 512], BF16)
        b.t16 = sb("t16a", [1, 16], F32)
        b.t16b = sb("t16b", [1, 16], F32)
        b.t16c = sb("t16c", [1, 16], F32)
        b.col = [sb(f"col{i}", [128, 1], F32) for i in range(8)]
        b.eps5 = sb("eps5", [128, 1], F32)
        b.scf = sb("scf", [128, T], F32)
        b.scb = sb("scb", [128, T], BF16)
        b.scb2 = sb("scb2", [128, T], BF16)
        b.ebuf = sb("ebuf", [128, 1024], BF16)

        b.ps_a = ctx.enter_context(nc.psum_tensor("ps_a", [128, 2048], F32))
        b.ps_b = ctx.enter_context(nc.psum_tensor("ps_b", [128, 1024], F32))
        b.ps_c = ctx.enter_context(nc.psum_tensor("ps_c", [128, 512], F32))
        b.ps_tb = ctx.enter_context(nc.psum_tensor("ps_tb", [128, 512], BF16))

        sem = ctx.enter_context(nc.semaphore("g"))
        block = ctx.enter_context(nc.Block())

        emit_prologue(b, ctx)
        for psx in range(n_pass):
            for t in range(NTILE):
                p.dma("sync", b.x[:, t * DM:(t + 1) * DM], di["xin"][psx, t * 128:(t + 1) * 128, :])
            import os
            skip = set(os.environ.get("K_SKIP", "").split(","))
            for blk in blocks:
                emit_ln(b, ctx, psx, blk)
                if blk == 0:
                    if "l0attn" not in skip:
                        emit_l0_attn(b, ctx, psx)
                    if "l0moe" not in skip:
                        emit_l0_moe(b, ctx, psx)
                else:
                    if "l1attn" not in skip:
                        emit_l1_attn(b, ctx, psx)
                    if "l1ffn" not in skip:
                        emit_l1_ffn(b, ctx, psx)
            for t in range(NTILE):
                p.dma("sync", b.dout[psx, t * 128:(t + 1) * 128, :], b.x[:, t * DM:(t + 1) * DM])

        total = p.build(block, sem)

        @block.sync
        def _(sync):
            sync.wait_ge(sem, total)

    return nc, b


def tap(b, name, ap_fn, shape, dtype=F32):
    if name not in b.tapset or name in b.taps:
        return
    t = b.nc.dram_tensor("tap_" + name, list(shape), dtype, kind="ExternalOutput").ap()
    b.taps[name] = (list(shape), dtype)
    b.prog.dma("sync", t, ap_fn())


def emit_prologue(b, ctx):
    import os
    skip = set(os.environ.get("K_SKIP", "").split(","))
    p, di = b.prog, b.di
    p.dma("gpsimd", b.eyeb[:], di["c_eye"])
    if "eyef" not in skip:
        p.dma("sync", b.eyef[:], di["c_eye"])
    if "sup" not in skip:
        p.dma("gpsimd", b.sup[:], di["c_sup"])
    if "dmatx" not in skip:
        p.dma("sync", b.dmat[:], di["c_dmat"])
    if "e16" not in skip:
        p.dma("gpsimd", b.e16[:], di["c_e16"])
    if "e8" not in skip:
        p.dma("gpsimd", b.e8[:], di["c_e8"])
    if "iotac" not in skip:
        p.dma("sync", b.iotac[:], di["c_iota_col"])
    if "consts" in skip:
        p.step("vector", lambda e: e.memset(b.onesb_row[:], 1.0))
        p.step("vector", lambda e: e.memset(b.onesf_row[:], 1.0))
        p.step("vector", lambda e: e.memset(b.onesb_col[:], 1.0))
        p.step("vector", lambda e: e.memset(b.onesf_col[:], 1.0))
        p.step("vector", lambda e: e.memset(b.eps5[:], 1e-5))
        return
    p.step("vector", lambda e: e.memset(b.onesb_row[:], 1.0))
    p.step("vector", lambda e: e.memset(b.onesf_row[:], 1.0))
    p.step("vector", lambda e: e.memset(b.onesb_col[:], 1.0))
    p.step("vector", lambda e: e.memset(b.onesf_col[:], 1.0))
    p.step("vector", lambda e: e.memset(b.eps5[:], 1e-5))

    import os
    if "rope" in set(os.environ.get("K_SKIP", "").split(",")):
        return
    O2PI = float(1.0 / (2 * np.pi))
    TWO_PI = float(2 * np.pi)
    LN10K = float(np.log(10000.0))
    with ExitStack() as ph:
        iotar = ph.enter_context(b.nc.sbuf_tensor(_uq("iotar"), [1, T], F32))
        ang = ph.enter_context(b.nc.sbuf_tensor(_uq("ang"), [32, T], F32))
        tmpi = ph.enter_context(b.nc.sbuf_tensor(_uq("tmpi"), [32, T], I32))
        tmpf = ph.enter_context(b.nc.sbuf_tensor(_uq("tmpf"), [32, T], F32))
        invf = b.col[7]
        p.dma("sync", iotar[:], di["c_iota_row"])
        p.step("scalar", lambda e: e.activation(
            out=invf[0:32, :], in_=b.iotac[0:32, :], func=AF.Exp, scale=-LN10K / 32.0))
        for s in range(NSTR):
            p.step("tensor", lambda e, s=s: e.matmul(
                b.ps_a[0:32, s * 512:(s + 1) * 512], b.onesf_row[0:1, 0:32],
                iotar[0:1, s * 512:(s + 1) * 512], start=True, stop=True))

        def reduce_sin(dst, phase):
            p.step("vector", lambda e: e.tensor_scalar(
                out=ang[:], in0=b.ps_a[0:32, 0:T], scalar1=invf[0:32, :],
                scalar2=phase, op0=ALU.mult, op1=ALU.add))
            p.step("vector", lambda e: e.tensor_scalar(
                out=tmpi[:], in0=ang[:], scalar1=O2PI, scalar2=None, op0=ALU.mult))
            p.step("vector", lambda e: e.tensor_copy(out=tmpf[:], in_=tmpi[:]))
            p.step("vector", lambda e: e.scalar_tensor_tensor(
                out=ang[:], in0=tmpf[:], scalar=-TWO_PI, in1=ang[:],
                op0=ALU.mult, op1=ALU.add))
            p.step("scalar", lambda e, dst=dst: e.activation(out=dst, in_=ang[:], func=AF.Sin))

        reduce_sin(b.scb[0:32, 0:T], 0.0)
        reduce_sin(b.scb2[0:32, 0:T], float(np.pi / 2))
        p.step("vector", lambda e: e.tensor_scalar(
            out=b.scb[32:64, 0:T], in0=b.scb[0:32, 0:T], scalar1=-1.0, scalar2=None,
            op0=ALU.mult))
        for r in range(4):
            p.dma("sync", b.chc[32 * r:32 * (r + 1), 0:T], b.scb2[0:32, 0:T])
            p.dma("sync", b.chs[32 * r:32 * (r + 1), 0:T],
                  b.scb[32:64, 0:T] if r % 2 == 0 else b.scb[0:32, 0:T])
    tap(b, "ropec", lambda: b.chc[:], [128, T], BF16)
    tap(b, "ropes", lambda: b.chs[:], [128, T], BF16)


def emit_ln(b, ctx, psx, blk):
    p = b.prog
    with ExitStack() as ph:
        stage = ph.enter_context(b.nc.sbuf_tensor(_uq("lnstage"), [128, NCH * 512], BF16))
        for s in range(NSTR):
            for tt in range(4):
                t = s * 4 + tt
                xt = b.x[:, t * DM:(t + 1) * DM]
                c0, c1, mu, var, rstd = b.col[0], b.col[1], b.col[2], b.col[3], b.col[4]
                p.step("scalar", lambda e, xt=xt, c0=c0: e.activation(
                    out=b.scf[:, 0:DM], in_=xt, func=AF.Square, accum_out=c0[:]))
                p.step("scalar", lambda e, xt=xt, c1=c1: e.activation(
                    out=b.scf[:, DM:2 * DM], in_=xt, func=AF.Identity, accum_out=c1[:]))
                p.step("vector", lambda e, c1=c1, mu=mu: e.tensor_scalar(
                    out=mu[:], in0=c1[:], scalar1=1.0 / DM, scalar2=None, op0=ALU.mult))
                p.step("vector", lambda e, mu=mu, var=var: e.tensor_mul(
                    out=var[:], in0=mu[:], in1=mu[:]))
                p.step("vector", lambda e, c0=c0, var=var: e.scalar_tensor_tensor(
                    out=var[:], in0=c0[:], scalar=1.0 / DM, in1=var[:],
                    op0=ALU.mult, op1=ALU.subtract))
                p.step("scalar", lambda e, var=var, rstd=rstd: e.activation(
                    out=rstd[:], in_=var[:], func=AF.Sqrt, bias=b.eps5[:]))
                p.step("vector", lambda e, rstd=rstd: e.reciprocal(out=rstd[:], in_=rstd[:]))
                p.step("vector", lambda e, xt=xt, mu=mu, rstd=rstd: e.tensor_scalar(
                    out=b.scb[:, 0:DM], in0=xt, scalar1=mu[:], scalar2=rstd[:],
                    op0=ALU.subtract, op1=ALU.mult))
                if t == 0:
                    tap(b, f"lnx_b{blk}_p{psx}", lambda xt=xt: xt, [128, DM], F32)
                    tap(b, f"lnc0_b{blk}_p{psx}", lambda c0=c0: c0[:], [128, 1], F32)
                    tap(b, f"lnmu_b{blk}_p{psx}", lambda mu=mu: mu[:], [128, 1], F32)
                    tap(b, f"lnrs_b{blk}_p{psx}", lambda rstd=rstd: rstd[:], [128, 1], F32)
                    tap(b, f"lnscb_b{blk}_p{psx}", lambda: b.scb[:, 0:DM], [128, DM], BF16)
                for c in range(NCH):
                    p.step("tensor", lambda e, c=c: e.transpose(
                        b.ps_tb[:, 0:128], b.scb[:, c * 128:(c + 1) * 128], b.eyeb[:]))
                    p.step("vector", lambda e, c=c, tt=tt: e.tensor_copy(
                        out=stage[:, c * 512 + tt * 128: c * 512 + (tt + 1) * 128],
                        in_=b.ps_tb[:, 0:128]))
            for c in range(NCH):
                p.dma("sync", b.d_xn[c, :, s * 512:(s + 1) * 512],
                      stage[:, c * 512:(c + 1) * 512])
    tap(b, f"xn_b{blk}_p{psx}", lambda: b.scb[:, 0:DM], [128, DM], BF16)


def load_xn_strip(b, dst, s):
    b.prog.dma("sync", dst[:, 0:NCH * 512].rearrange("p (c n) -> p c n", n=512),
               b.d_xn[:, :, s * 512:(s + 1) * 512].rearrange("c p n -> p c n"))


def load_xn_tile(b, dst, t):
    b.prog.dma("sync", dst[:, 0:NCH * 128].rearrange("p (c n) -> p c n", n=128),
               b.d_xn[:, :, t * 128:(t + 1) * 128].rearrange("c p n -> p c n"))


def emit_rope(b, chunk_ap, width, pos0, rows=128):
    p = b.prog
    ngr = rows // 32
    for dg in range(ngr):
        sg = dg ^ 1
        p.step("vector", lambda e, dg=dg, sg=sg: e.tensor_copy(
            out=b.scb2[32 * dg:32 * (dg + 1), 0:width],
            in_=chunk_ap[32 * sg:32 * (sg + 1), 0:width]))
    p.step("vector", lambda e: e.tensor_mul(
        out=b.scb[0:rows, 0:width], in0=chunk_ap, in1=b.chc[0:rows, pos0:pos0 + width]))
    p.step("vector", lambda e: e.tensor_mul(
        out=b.scb2[0:rows, 0:width], in0=b.scb2[0:rows, 0:width],
        in1=b.chs[0:rows, pos0:pos0 + width]))
    p.step("vector", lambda e: e.tensor_add(
        out=chunk_ap, in0=b.scb[0:rows, 0:width], in1=b.scb2[0:rows, 0:width]))


def emit_add_to_x(b, src_f32_ap, s, oc, width=512):
    p = b.prog
    nt = width // 128
    for st in range(nt):
        t = s * 4 + st
        p.step("tensor", lambda e, st=st: e.transpose(
            b.ps_c[:, 0:128], src_f32_ap[:, st * 128:(st + 1) * 128], b.eyef[:]))
        p.step("vector", lambda e, t=t, oc=oc: e.tensor_add(
            out=b.x[:, t * DM + oc * 128: t * DM + (oc + 1) * 128],
            in0=b.x[:, t * DM + oc * 128: t * DM + (oc + 1) * 128],
            in1=b.ps_c[:, 0:128]))


def emit_l0_attn(b, ctx, psx):
    p, di = b.prog, b.di
    with ExitStack() as ph:
        def sb(name, shape, dtype):
            return ph.enter_context(b.nc.sbuf_tensor(_uq(name), shape, dtype))
        kbuf = sb("l0k", [128, NCH * T], BF16)
        vt = sb("l0vt", [128, NTILE * NH * 65], BF16)

        with ExitStack() as ph2:
            def sb2(name, shape, dtype):
                return ph2.enter_context(b.nc.sbuf_tensor(_uq(name), shape, dtype))
            latn = sb2("l0latn", [128, 2 * T], BF16)
            wdkv = sb2("wdkv", [128, NCH * LAT], BF16)
            wuk = sb2("wuk", [128, 2 * DM], BF16)
            wuv = sb2("wuv", [128, 2 * DM], BF16)
            gcol = sb2("lngc", [128, 2], F32)
            bcol = sb2("lnbc", [128, 2], F32)
            latf = sb2("latf", [128, 1024], F32)
            xnst = sb2("xnst0", [128, NCH * 512], BF16)

            p.dma("gpsimd", wdkv[:].rearrange("p (c l) -> p c l", l=LAT),
                  di["wdkv"].rearrange("(c p) l -> p c l", p=128))
            p.dma("gpsimd", wuk[:].rearrange("p (c d) -> p c d", d=DM),
                  di["wuk"].rearrange("(c p) d -> p c d", p=128))
            p.dma("gpsimd", wuv[:].rearrange("p (c d) -> p c d", d=DM),
                  di["wuv"].rearrange("(c p) d -> p c d", p=128))
            p.dma("sync", b.scf[0:2, 0:128], di["lng"].rearrange("o (c p) -> (o c) p", p=128))
            p.dma("sync", b.scf[32:34, 0:128], di["lnb"].rearrange("o (c p) -> (o c) p", p=128))
            p.step("tensor", lambda e: e.transpose(b.ps_c[:, 0:2], b.scf[0:2, 0:128], b.eyef[0:2, 0:2]))
            p.step("vector", lambda e: e.tensor_copy(out=gcol[:], in_=b.ps_c[:, 0:2]))
            p.step("tensor", lambda e: e.transpose(b.ps_c[:, 0:2], b.scf[32:34, 0:128], b.eyef[32:34, 32:34]))
            p.step("vector", lambda e: e.tensor_copy(out=bcol[:], in_=b.ps_c[:, 0:2]))
            tap(b, f"wdkvt{psx}", lambda: wdkv[:], [128, NCH * LAT], BF16)
            p.step("vector", lambda e: e.memset(
                vt[:].rearrange("p (t h c) -> p t h c", h=NH, c=65)[:, :, :, 64:65], 1.0))

            for s in range(NSTR):
                load_xn_strip(b, xnst, s)
                if s == 0:
                    tap(b, f"xnstA{psx}", lambda: xnst[:], [128, NCH * 512], BF16)
                for lc in range(2):
                    for fc in range(NCH):
                        p.step("tensor", lambda e, lc=lc, fc=fc: e.matmul(
                            b.ps_b[:, lc * 512:(lc + 1) * 512],
                            wdkv[:, fc * LAT + lc * 128: fc * LAT + (lc + 1) * 128],
                            xnst[:, fc * 512:(fc + 1) * 512],
                            start=(fc == 0), stop=(fc == NCH - 1)))
                p.step("vector", lambda e: e.tensor_copy(out=latf[:], in_=b.ps_b[:, 0:1024]))
                if s == 0:
                    tap(b, f"latraw{psx}", lambda: latf[:], [128, 1024], F32)
                p.step("scalar", lambda e: e.activation(
                    out=b.scf[:, 0:1024], in_=latf[:], func=AF.Square))
                for lc in range(2):
                    p.step("tensor", lambda e, lc=lc: e.matmul(
                        b.ps_c[0:1, 0:512], b.onesf_col[:], latf[:, lc * 512:(lc + 1) * 512],
                        start=(lc == 0), stop=(lc == 1)))
                p.step("vector", lambda e: e.tensor_scalar(
                    out=b.r3[:], in0=b.ps_c[0:1, 0:512], scalar1=1.0 / LAT, scalar2=None,
                    op0=ALU.mult))
                if s == 0:
                    tap(b, f"latsum{psx}", lambda: b.r3[:], [1, 512], F32)
                for lc in range(2):
                    p.step("tensor", lambda e, lc=lc: e.matmul(
                        b.ps_c[0:1, 0:512], b.onesf_col[:], b.scf[:, lc * 512:(lc + 1) * 512],
                        start=(lc == 0), stop=(lc == 1)))
                p.step("vector", lambda e: e.tensor_mul(out=b.r4[:], in0=b.r3[:], in1=b.r3[:]))
                p.step("vector", lambda e: e.scalar_tensor_tensor(
                    out=b.r4[:], in0=b.ps_c[0:1, 0:512], scalar=1.0 / LAT, in1=b.r4[:],
                    op0=ALU.mult, op1=ALU.subtract))
                p.step("scalar", lambda e: e.activation(out=b.r4[:], in_=b.r4[:], func=AF.Sqrt, bias=b.eps5[0:1, 0:1]))
                p.step("vector", lambda e: e.reciprocal(out=b.r4[:], in_=b.r4[:]))
                p.step("tensor", lambda e: e.matmul(
                    b.ps_c[:, 0:512], b.onesf_row[0:1, :], b.r3[:], start=True, stop=True))
                for lc in range(2):
                    p.step("vector", lambda e, lc=lc: e.tensor_sub(
                        out=latf[:, lc * 512:(lc + 1) * 512],
                        in0=latf[:, lc * 512:(lc + 1) * 512], in1=b.ps_c[:, 0:512]))
                p.step("tensor", lambda e: e.matmul(
                    b.ps_c[:, 0:512], b.onesf_row[0:1, :], b.r4[:], start=True, stop=True))
                for lc in range(2):
                    p.step("vector", lambda e, lc=lc: e.tensor_mul(
                        out=latf[:, lc * 512:(lc + 1) * 512],
                        in0=latf[:, lc * 512:(lc + 1) * 512], in1=b.ps_c[:, 0:512]))
                for lc in range(2):
                    p.step("vector", lambda e, lc=lc, s=s: e.tensor_scalar(
                        out=latn[:, lc * T + s * 512: lc * T + (s + 1) * 512],
                        in0=latf[:, lc * 512:(lc + 1) * 512],
                        scalar1=gcol[:, lc:lc + 1], scalar2=bcol[:, lc:lc + 1],
                        op0=ALU.mult, op1=ALU.add))
                if s == 0:
                    tap(b, f"latmu{psx}", lambda: b.r3[:], [1, 512], F32)
                    tap(b, f"latrs{psx}", lambda: b.r4[:], [1, 512], F32)
                    tap(b, f"latf{psx}", lambda: latf[:], [128, 1024], F32)
                    tap(b, f"gcol{psx}", lambda: gcol[:], [128, 2], F32)
                    tap(b, f"bcol{psx}", lambda: bcol[:], [128, 2], F32)
            tap(b, f"latn{psx}", lambda: latn[:], [128, 2 * T], BF16)

            for c in range(NCH):
                for s in range(NSTR):
                    for lc in range(2):
                        p.step("tensor", lambda e, c=c, s=s, lc=lc: e.matmul(
                            b.ps_c[:, 0:512],
                            wuk[:, lc * DM + c * 128: lc * DM + (c + 1) * 128],
                            latn[:, lc * T + s * 512: lc * T + (s + 1) * 512],
                            start=(lc == 0), stop=(lc == 1)))
                    p.step("scalar", lambda e, c=c, s=s: e.activation(
                        out=kbuf[:, c * T + s * 512: c * T + (s + 1) * 512],
                        in_=b.ps_c[:, 0:512], func=AF.Identity))
            for c in range(NCH):
                emit_rope(b, kbuf[:, c * T:(c + 1) * T], T, 0)
            tap(b, f"k0_{psx}", lambda: kbuf[:, 0:T], [128, T], BF16)

            for tt in range(NTILE):
                for half in range(2):
                    for lc in range(2):
                        p.step("tensor", lambda e, tt=tt, half=half, lc=lc: e.matmul(
                            b.ps_b[:, half * 512:(half + 1) * 512],
                            latn[:, lc * T + tt * 128: lc * T + (tt + 1) * 128],
                            wuv[:, lc * DM + half * 512: lc * DM + (half + 1) * 512],
                            start=(lc == 0), stop=(lc == 1)))
                p.step("vector", lambda e, tt=tt: e.tensor_copy(
                    out=vt[:, tt * (NH * 65):(tt + 1) * (NH * 65)]
                    .rearrange("p (h c) -> p h c", c=65)[:, :, 0:64],
                    in_=b.ps_b[:, 0:1024].rearrange("p (h c) -> p h c", c=64)))
            tap(b, f"v0_{psx}", lambda: vt[:, 0:NH * 65], [128, NH * 65], BF16)

        with ExitStack() as ph2:
            def sb2(name, shape, dtype):
                return ph2.enter_context(b.nc.sbuf_tensor(_uq(name), shape, dtype))
            xnstb = sb2("xnst0b", [128, NCH * 512], BF16)
            qst = sb2("qst", [128, NCH * 512], BF16)
            ost = sb2("ost", [128, NCH * 512], BF16)
            wqp = sb2("wqp", [128, NCH * 256], BF16)
            wop = sb2("wop", [128, NCH * 256], BF16)
            nk2 = b.t16

            for h in range(NH):
                c, rb = h // 2, (h % 2) * 64
                p.step("scalar", lambda e, c=c, rb=rb: e.activation(
                    out=b.scf[0:64, 0:T], in_=kbuf[rb:rb + 64, c * T:(c + 1) * T],
                    func=AF.Square))
                for s in range(NSTR):
                    p.step("tensor", lambda e, s=s: e.matmul(
                        b.ps_b[0:1, s * 256:s * 256 + 512] if False else b.ps_c[0:1, 0:512],
                        b.onesf_col[0:64, :], b.scf[0:64, s * 512:(s + 1) * 512],
                        start=True, stop=True))
                    p.step("vector", lambda e, s=s: e.tensor_copy(
                        out=b.r3[0:1, 0:512] if s == 0 else b.r4[0:1, 0:512],
                        in_=b.ps_c[0:1, 0:512]))
                    if s > 0:
                        p.step("vector", lambda e: e.tensor_max(
                            out=b.r3[:], in0=b.r3[:], in1=b.r4[:]))
                p.step("vector", lambda e, h=h: e.tensor_reduce(
                    out=nk2[0:1, h:h + 1], in_=b.r3[:], axis=AX.X, op=ALU.max))

            for s in range(NSTR):
                load_xn_strip(b, xnstb, s)
                for cp in range(4):
                    p.dma("gpsimd", wqp[:].rearrange("p (c n) -> p c n", n=256),
                          di["wq"].rearrange("(c p) n -> p c n", p=128)[:, :, cp * 256:(cp + 1) * 256])
                    for j2 in range(2):
                        for fc in range(NCH):
                            p.step("tensor", lambda e, j2=j2, fc=fc: e.matmul(
                                b.ps_b[:, j2 * 512:(j2 + 1) * 512],
                                wqp[:, fc * 256 + j2 * 128: fc * 256 + (j2 + 1) * 128],
                                xnstb[:, fc * 512:(fc + 1) * 512],
                                start=(fc == 0), stop=(fc == NCH - 1)))
                        p.step("scalar", lambda e, cp=cp, j2=j2: e.activation(
                            out=qst[:, (cp * 2 + j2) * 512:(cp * 2 + j2 + 1) * 512],
                            in_=b.ps_b[:, j2 * 512:(j2 + 1) * 512], func=AF.Identity))
                for c in range(NCH):
                    emit_rope(b, qst[:, c * 512:(c + 1) * 512], 512, s * 512)
                if psx == 0 and s == 0:
                    tap(b, "q0", lambda: qst[:, 0:NCH * 512], [128, NCH * 512], BF16)

                for h in range(NH):
                    c, rb = h // 2, (h % 2) * 64
                    qsl = qst[rb:rb + 64, c * 512:(c + 1) * 512]
                    p.step("scalar", lambda e, qsl=qsl: e.activation(
                        out=b.scf[0:64, 0:512], in_=qsl, func=AF.Square))
                    p.step("tensor", lambda e: e.matmul(
                        b.ps_c[0:1, 0:512], b.onesf_col[0:64, :], b.scf[0:64, 0:512],
                        start=True, stop=True))
                    p.step("vector", lambda e, h=h: e.tensor_scalar(
                        out=b.r3[:], in0=b.ps_c[0:1, 0:512], scalar1=nk2[0:1, h:h + 1],
                        scalar2=-0.5, op0=ALU.add, op1=ALU.mult))
                    p.step("vector", lambda e: e.tensor_copy(out=b.rb1[:], in_=b.r3[:]))
                    nkc = 4 * s + 4
                    for kcp in range(2 * s):
                        k0, k1 = 2 * kcp, 2 * kcp + 1
                        for j, kc in enumerate((k0, k1)):
                            p.step("tensor", lambda e, kc=kc, j=j, c=c, rb=rb: e.matmul(
                                b.ps_b[:, j * 512:(j + 1) * 512],
                                kbuf[rb:rb + 64, c * T + kc * 128: c * T + (kc + 1) * 128],
                                qst[rb:rb + 64, c * 512:(c + 1) * 512],
                                start=True, stop=False))
                            p.step("tensor", lambda e, j=j: e.matmul(
                                b.ps_b[:, j * 512:(j + 1) * 512], b.onesb_row[:], b.rb1[:],
                                start=False, stop=True))
                        p.step("scalar", lambda e: e.activation(
                            out=b.ebuf[:], in_=b.ps_b[:, 0:1024], func=AF.Exp, scale=0.125))
                        for j, kc in enumerate((k0, k1)):
                            p.step("tensor", lambda e, h=h, kc=kc, j=j: e.matmul(
                                b.ps_a[0:65, 0:512],
                                vt[:, kc * (NH * 65) + h * 65: kc * (NH * 65) + (h + 1) * 65],
                                b.ebuf[:, j * 512:(j + 1) * 512],
                                start=(kc == 0), stop=False))
                    for kc in range(4 * s, nkc):
                        p.step("tensor", lambda e, kc=kc, c=c, rb=rb: e.matmul(
                            b.ps_c[:, 0:512],
                            kbuf[rb:rb + 64, c * T + kc * 128: c * T + (kc + 1) * 128],
                            qst[rb:rb + 64, c * 512:(c + 1) * 512],
                            start=True, stop=False))
                        p.step("tensor", lambda e: e.matmul(
                            b.ps_c[:, 0:512], b.onesb_row[:], b.rb1[:],
                            start=False, stop=True))
                        p.step("scalar", lambda e: e.activation(
                            out=b.ebuf[:, 0:512], in_=b.ps_c[:, 0:512], func=AF.Exp, scale=0.125))
                        r = kc - 4 * s
                        p.step("vector", lambda e, r=r: e.tensor_mul(
                            out=b.ebuf[:, 0:512], in0=b.ebuf[:, 0:512],
                            in1=b.sup[:, 385 - 128 * r: 897 - 128 * r]))
                        p.step("tensor", lambda e, h=h, kc=kc, last=(kc == nkc - 1): e.matmul(
                            b.ps_a[0:65, 0:512],
                            vt[:, kc * (NH * 65) + h * 65: kc * (NH * 65) + (h + 1) * 65],
                            b.ebuf[:, 0:512], start=(kc == 0), stop=last))
                    p.step("vector", lambda e: e.tensor_scalar(
                        out=b.r3[:], in0=b.ps_a[64:65, 0:512], scalar1=1e-30, scalar2=None,
                        op0=ALU.add))
                    p.step("vector", lambda e: e.reciprocal(out=b.r4[:], in_=b.r3[:]))
                    p.step("vector", lambda e: e.tensor_copy(out=b.rb1[:], in_=b.r4[:]))
                    p.step("tensor", lambda e: e.matmul(
                        b.ps_c[0:64, 0:512], b.onesb_row[0:1, 0:64], b.rb1[:],
                        start=True, stop=True))
                    p.step("vector", lambda e: e.tensor_copy(
                        out=b.scf[0:64, 0:512], in_=b.ps_c[0:64, 0:512]))
                    p.step("vector", lambda e, h=h, c=c, rb=rb: e.tensor_mul(
                        out=ost[rb:rb + 64, c * 512:(c + 1) * 512],
                        in0=b.ps_a[0:64, 0:512], in1=b.scf[0:64, 0:512]))
                if psx == 0 and s == 0:
                    tap(b, "o0", lambda: ost[:, 0:NCH * 512], [128, NCH * 512], BF16)

                for cp in range(4):
                    p.dma("gpsimd", wop[:].rearrange("p (c n) -> p c n", n=256),
                          di["wo"].rearrange("(c p) n -> p c n", p=128)[:, :, cp * 256:(cp + 1) * 256])
                    for j2 in range(2):
                        oc = cp * 2 + j2
                        for ci in range(NCH):
                            p.step("tensor", lambda e, j2=j2, ci=ci: e.matmul(
                                b.ps_b[:, j2 * 512:(j2 + 1) * 512],
                                wop[:, ci * 256 + j2 * 128: ci * 256 + (j2 + 1) * 128],
                                ost[:, ci * 512:(ci + 1) * 512],
                                start=(ci == 0), stop=(ci == NCH - 1)))
                        p.step("vector", lambda e, j2=j2: e.tensor_copy(
                            out=b.scf[:, 0:512], in_=b.ps_b[:, j2 * 512:(j2 + 1) * 512]))
                        emit_add_to_x(b, b.scf[:, 0:512], s, oc)


def emit_l0_moe(b, ctx, psx):
    p, di = b.prog, b.di
    with ExitStack() as ph:
        def sb(name, shape, dtype):
            return ph.enter_context(b.nc.sbuf_tensor(_uq(name), shape, dtype))
        xnst = sb("xnstm", [128, NCH * 512], BF16)
        w1sb = sb("w1sb", [128, NCH * IM], BF16)
        w2sb = sb("w2sb", [128, NM * 512], BF16)
        hm = sb("hm", [128, NM * 512], BF16)
        acc = sb("macc", [128, 4 * DM], F32)
        cer = sb("cer", [128, 512], BF16)
        cefm = sb("cefm", [16, T], BF16)
        b1t = sb("b1t", [128, NE0 * 16], F32)
        b2sb = sb("b2sb", [16, DM], BF16)
        rwsb = sb("rwsb", [128, NCH * NE0], BF16)
        rbsb = sb("rbsb", [1, NE0], BF16)
        lg = sb("lgts", [128, NE0], F32)

        p.dma("gpsimd", rwsb[:].rearrange("p (c n) -> p c n", n=NE0),
              di["rw"].rearrange("(c p) n -> p c n", p=128))
        p.dma("gpsimd", rbsb[:], di["rb"])
        p.dma("gpsimd", b2sb[:], di["b2"])
        for e0 in range(NE0):
            p.dma("sync", b.scf[0:16, 0:128], di["b1"][e0:e0 + 1, :].rearrange("o (m q) -> (o m) q", q=128))
            p.step("tensor", lambda e, e0=e0: e.transpose(
                b.ps_c[:, 0:16], b.scf[0:16, 0:128], b.eyef[0:16, 0:16]))
            p.step("vector", lambda e, e0=e0: e.tensor_copy(
                out=b1t[:, e0 * 16:(e0 + 1) * 16], in_=b.ps_c[:, 0:16]))

        for t in range(NTILE):
            load_xn_tile(b, xnst, t)
            for fc in range(NCH):
                p.step("tensor", lambda e, fc=fc: e.matmul(
                    b.ps_c[:, 0:NE0],
                    xnst[:, fc * 128:(fc + 1) * 128],
                    rwsb[:, fc * NE0:(fc + 1) * NE0],
                    start=(fc == 0), stop=False))
            p.step("tensor", lambda e: e.matmul(
                b.ps_c[:, 0:NE0], b.onesb_row[:], rbsb[:], start=False, stop=True))
            l0, m1, mr, msk = b.scf[:, 0:NE0], b.col[0], b.col[1], b.scf[:, 64:64 + NE0]
            p.step("vector", lambda e: e.tensor_copy(out=l0, in_=b.ps_c[:, 0:NE0]))
            p.step("vector", lambda e: e.tensor_copy(out=b.scf[:, 32:32 + NE0], in_=l0))
            lw = b.scf[:, 32:32 + NE0]
            for rnd in range(4):
                p.step("vector", lambda e, dst=(m1 if rnd == 0 else mr): e.tensor_reduce(
                    out=dst[:], in_=lw, axis=AX.X, op=ALU.max))
                if rnd < 3:
                    src = m1 if rnd == 0 else mr
                    p.step("vector", lambda e, src=src: e.tensor_scalar(
                        out=msk, in0=lw, scalar1=src[:], scalar2=None, op0=ALU.is_ge))
                    p.step("vector", lambda e: e.scalar_tensor_tensor(
                        out=lw, in0=msk, scalar=-1e30, in1=lw, op0=ALU.mult, op1=ALU.add))
            p.step("vector", lambda e: e.tensor_scalar(
                out=msk, in0=l0, scalar1=mr[:], scalar2=None, op0=ALU.is_ge))
            p.step("vector", lambda e: e.tensor_scalar(
                out=b.col[2][:], in0=m1[:], scalar1=-1.0, scalar2=None, op0=ALU.mult))
            p.step("scalar", lambda e: e.activation(
                out=b.scf[:, 96:96 + NE0], in_=l0, func=AF.Exp, bias=b.col[2][:]))
            p.step("vector", lambda e: e.tensor_mul(
                out=b.scf[:, 96:96 + NE0], in0=b.scf[:, 96:96 + NE0], in1=msk))
            p.step("vector", lambda e: e.tensor_reduce(
                out=b.col[3][:], in_=b.scf[:, 96:96 + NE0], axis=AX.X, op=ALU.add))
            p.step("vector", lambda e: e.reciprocal(out=b.col[3][:], in_=b.col[3][:]))
            p.step("vector", lambda e: e.tensor_scalar(
                out=lg[:], in0=b.scf[:, 96:96 + NE0], scalar1=b.col[3][:], scalar2=None,
                op0=ALU.mult))
            p.step("tensor", lambda e: e.transpose(b.ps_c[0:16, 0:128], lg[:], b.eyef[:]))
            p.step("vector", lambda e, t=t: e.tensor_copy(
                out=cefm[:, t * 128:(t + 1) * 128], in_=b.ps_c[0:16, 0:128]))
        tap(b, f"ce{psx}", lambda: cefm[:], [16, T], BF16)

        for s in range(NSTR):
            load_xn_strip(b, xnst, s)
            for tt in range(4):
                for half in range(2):
                    p.step("tensor", lambda e, tt=tt, half=half, s=s: e.matmul(
                        b.ps_a[:, (tt % 2) * 1024 + half * 512:(tt % 2) * 1024 + (half + 1) * 512],
                        cefm[:, s * 512 + tt * 128: s * 512 + (tt + 1) * 128],
                        b2sb[:, half * 512:(half + 1) * 512],
                        start=True, stop=True))
                for half in range(2):
                    p.step("vector", lambda e, tt=tt, half=half: e.tensor_copy(
                        out=acc[:, tt * 1024 + half * 512: tt * 1024 + (half + 1) * 512],
                        in_=b.ps_a[:, (tt % 2) * 1024 + half * 512:(tt % 2) * 1024 + (half + 1) * 512]))
            for e0 in range(NE0):
                first = (psx == 0 and s == 0)
                if first:
                    p.dma("gpsimd", w1sb[:].rearrange("p (c m) -> p c m", m=IM),
                          di["w1"][e0].rearrange("(c p) m -> p c m", p=128))
                    p.dma("sync", b.d_w1b[e0], w1sb[:])
                else:
                    p.dma("sync", w1sb[:], b.d_w1b[e0])
                p.step("tensor", lambda e, e0=e0, s=s: e.matmul(
                    b.ps_c[:, 0:512], b.e16[:, e0 * 128:(e0 + 1) * 128],
                    cefm[:, s * 512:(s + 1) * 512], start=True, stop=True))
                p.step("vector", lambda e: e.tensor_copy(out=cer[:], in_=b.ps_c[:, 0:512]))
                for mp in range(NM // 2):
                    for k2 in range(2):
                        m = 2 * mp + k2
                        for fc in range(NCH):
                            p.step("tensor", lambda e, m=m, k2=k2, fc=fc: e.matmul(
                                b.ps_b[:, k2 * 512:(k2 + 1) * 512],
                                w1sb[:, fc * IM + m * 128: fc * IM + (m + 1) * 128],
                                xnst[:, fc * 512:(fc + 1) * 512],
                                start=(fc == 0), stop=(fc == NCH - 1)))
                    for k2 in range(2):
                        m = 2 * mp + k2
                        p.step("scalar", lambda e, m=m, k2=k2, e0=e0: e.activation(
                            out=hm[:, m * 512:(m + 1) * 512],
                            in_=b.ps_b[:, k2 * 512:(k2 + 1) * 512], func=AF.Gelu,
                            bias=b1t[:, e0 * 16 + m: e0 * 16 + m + 1]))
                    p.step("vector", lambda e, mp=mp: e.tensor_tensor(
                        out=hm[:, mp * 1024:(mp + 1) * 1024]
                        .rearrange("p (k n) -> p k n", k=2),
                        in0=hm[:, mp * 1024:(mp + 1) * 1024]
                        .rearrange("p (k n) -> p k n", k=2),
                        in1=cer[:].unsqueeze(1).broadcast_to((128, 2, 512)),
                        op=ALU.mult))
                for half in range(2):
                    cache_view = b.d_w2b[e0].rearrange(
                        "p (m o) -> p m o", o=1024)[:, :, half * 512:(half + 1) * 512]
                    if first:
                        p.dma("gpsimd", w2sb[:].rearrange("p (m n) -> p m n", n=512),
                              di["w2"][e0].rearrange("(m p) n -> p m n", p=128)[:, :, half * 512:(half + 1) * 512])
                        p.dma("sync", cache_view, w2sb[:].rearrange("p (m n) -> p m n", n=512))
                    else:
                        p.dma("sync", w2sb[:].rearrange("p (m n) -> p m n", n=512), cache_view)
                    for tt in range(4):
                        for m in range(NM):
                            p.step("tensor", lambda e, tt=tt, half=half, m=m: e.matmul(
                                b.ps_a[:, (tt % 2) * 1024 + half * 512:(tt % 2) * 1024 + (half + 1) * 512],
                                hm[:, m * 512 + tt * 128: m * 512 + (tt + 1) * 128],
                                w2sb[:, m * 512:(m + 1) * 512],
                                start=(m == 0), stop=(m == NM - 1)))
                        p.step("vector", lambda e, tt=tt, half=half: e.tensor_add(
                            out=acc[:, tt * 1024 + half * 512: tt * 1024 + (half + 1) * 512],
                            in0=acc[:, tt * 1024 + half * 512: tt * 1024 + (half + 1) * 512],
                            in1=b.ps_a[:, (tt % 2) * 1024 + half * 512:(tt % 2) * 1024 + (half + 1) * 512]))
            for tt in range(4):
                t = s * 4 + tt
                p.step("vector", lambda e, tt=tt, t=t: e.tensor_add(
                    out=b.x[:, t * DM:(t + 1) * DM],
                    in0=b.x[:, t * DM:(t + 1) * DM],
                    in1=acc[:, tt * 1024:(tt + 1) * 1024]))


def emit_l1_attn(b, ctx, psx):
    p, di = b.prog, b.di
    with ExitStack() as ph:
        def sb(name, shape, dtype):
            return ph.enter_context(b.nc.sbuf_tensor(_uq(name), shape, dtype))
        kh = sb("l1k", [128, NCH * T], BF16)

        Rrow = b.t16
        R2n = b.t16b
        vfac = b.t16c
        p.dma("sync", b.r3[0:1, 0:NH], di["ksc"])
        p.step("vector", lambda e: e.tensor_scalar(
            out=b.r3[0:1, 0:NH], in0=b.r3[0:1, 0:NH], scalar1=10.0, scalar2=KSMAX,
            op0=ALU.mult, op1=ALU.min))
        p.step("scalar", lambda e: e.activation(out=Rrow[:], in_=b.r3[0:1, 0:NH], func=AF.Exp))
        p.step("vector", lambda e: e.tensor_mul(out=R2n[:], in0=Rrow[:], in1=Rrow[:]))
        p.step("vector", lambda e: e.tensor_scalar(
            out=R2n[:], in0=R2n[:], scalar1=-1.0, scalar2=None, op0=ALU.mult))
        p.dma("sync", b.r3[0:1, 0:NH], di["vsc"])
        p.step("scalar", lambda e: e.activation(
            out=vfac[:], in_=b.r3[0:1, 0:NH], func=AF.Exp, scale=10.0))

        def build_wT(name, wt, wnat):
            for ci in range(NCH):
                p.dma("gpsimd", wnat[:], di[name][ci * 128:(ci + 1) * 128, :])
                for co in range(NCH):
                    p.step("tensor", lambda e, co=co: e.transpose(
                        b.ps_tb[:, 0:128], wnat[:, co * 128:(co + 1) * 128], b.eyeb[:]))
                    p.step("vector", lambda e, co=co, ci=ci: e.tensor_copy(
                        out=wt[:, co * 1024 + ci * 128: co * 1024 + (ci + 1) * 128],
                        in_=b.ps_tb[:, 0:128]))

        with ExitStack() as ph2:
            def sb2(name, shape, dtype):
                return ph2.enter_context(b.nc.sbuf_tensor(_uq(name), shape, dtype))
            wt = sb2("l1wtk", [128, NCH * 1024], BF16)
            wnat = sb2("l1wn", [128, 1024], BF16)
            Lt = sb2("lt", [128, NH * 128], BF16)
            dvr = sb2("dvr", [128, NH * 128], BF16)
            betac = sb2("betac", [128, 1], F32)
            xnt = sb2("xnt1", [128, NCH * 128], BF16)
            ring = [sb2(f"ring{i}", [128, 1024], BF16) for i in range(2)]

            build_wT("wk1", wt, wnat)

            p.dma("sync", b.r3[0:1, 0:NH], di["lkb"])
            for h in range(NH):
                p.step("tensor", lambda e, h=h: e.matmul(
                    b.ps_c[:, 0:1], b.onesf_row[:], b.r3[0:1, h:h + 1], start=True, stop=True))
                p.step("scalar", lambda e: e.activation(
                    out=betac[:], in_=b.ps_c[:, 0:1], func=AF.Abs))
                p.step("vector", lambda e: e.tensor_scalar(
                    out=betac[:], in0=betac[:], scalar1=-10.0, scalar2=None, op0=ALU.mult))
                p.step("vector", lambda e: e.tensor_scalar(
                    out=b.scf[:, 0:128], in0=b.dmat[:], scalar1=betac[:], scalar2=0.0,
                    op0=ALU.mult, op1=ALU.min))
                p.step("scalar", lambda e: e.activation(
                    out=b.scf[:, 128:256], in_=b.scf[:, 0:128], func=AF.Exp))
                p.step("vector", lambda e, h=h: e.tensor_mul(
                    out=Lt[:, h * 128:(h + 1) * 128], in0=b.scf[:, 128:256],
                    in1=b.sup[:, 385:513]))
                p.step("vector", lambda e: e.tensor_scalar(
                    out=b.scf[0:1, 0:128], in0=b.dmat[0:1, 0:128],
                    scalar1=1.0, scalar2=None, op0=ALU.add))
                p.step("vector", lambda e: e.tensor_scalar(
                    out=b.scf[0:1, 0:128], in0=b.scf[0:1, 0:128],
                    scalar1=betac[0:1, 0:1], scalar2=None, op0=ALU.mult))
                p.step("scalar", lambda e: e.activation(
                    out=b.scf[0:1, 0:128], in_=b.scf[0:1, 0:128], func=AF.Exp))
                p.step("vector", lambda e: e.tensor_copy(
                    out=b.rb1[0:1, 0:128], in_=b.scf[0:1, 0:128]))
                p.step("tensor", lambda e: e.matmul(
                    b.ps_c[:, 0:128], b.onesb_row[0:1, 0:128], b.rb1[0:1, 0:128],
                    start=True, stop=True))
                p.step("vector", lambda e, h=h: e.tensor_copy(
                    out=dvr[:, h * 128:(h + 1) * 128], in_=b.ps_c[:, 0:128]))
            tap(b, f"lt{psx}", lambda: Lt[:], [128, NH * 128], BF16)

            for t in range(NTILE):
                load_xn_tile(b, xnt, t)
                for half in range(2):
                    for fc in range(NCH):
                        p.step("tensor", lambda e, half=half, fc=fc: e.matmul(
                            b.ps_b[:, half * 512:(half + 1) * 512],
                            xnt[:, fc * 128:(fc + 1) * 128],
                            wt[:, fc * 1024 + half * 512: fc * 1024 + (half + 1) * 512],
                            start=(fc == 0), stop=(fc == NCH - 1)))
                p.step("scalar", lambda e, t=t: e.activation(
                    out=ring[t % 2][:], in_=b.ps_b[:, 0:1024], func=AF.Identity))
                ktm = ring[t % 2]
                for h in range(NH):
                    p.step("tensor", lambda e, h=h, ktm=ktm: e.matmul(
                        b.ps_a[0:64, h * 128:(h + 1) * 128],
                        ktm[:, h * 64:(h + 1) * 64],
                        Lt[:, h * 128:(h + 1) * 128], start=True, stop=True))
                for h in range(NH):
                    c, rb = h // 2, (h % 2) * 64
                    dst = kh[rb:rb + 64, c * T + t * 128: c * T + (t + 1) * 128]
                    pssl = b.ps_a[0:64, h * 128:(h + 1) * 128]
                    if t == 0:
                        p.step("vector", lambda e, dst=dst, pssl=pssl: e.tensor_copy(
                            out=dst, in_=pssl))
                    else:
                        carry = kh[rb:rb + 64, c * T + t * 128 - 1: c * T + t * 128]
                        p.step("vector", lambda e, dst=dst, pssl=pssl, carry=carry, h=h, rb=rb:
                               e.scalar_tensor_tensor(
                                   out=dst, in0=dvr[rb:rb + 64, h * 128:(h + 1) * 128],
                                   scalar=carry, in1=pssl, op0=ALU.mult, op1=ALU.add))
            tap(b, f"ksm{psx}", lambda: kh[:, 0:T], [128, T], BF16)

            for h in range(NH):
                c, rb = h // 2, (h % 2) * 64
                p.step("scalar", lambda e, c=c, rb=rb: e.activation(
                    out=b.scf[0:64, 0:T], in_=kh[rb:rb + 64, c * T:(c + 1) * T],
                    func=AF.Square))
                for s in range(NSTR):
                    p.step("tensor", lambda e, s=s: e.matmul(
                        b.ps_c[0:1, 0:512], b.onesf_col[0:64, :],
                        b.scf[0:64, s * 512:(s + 1) * 512], start=True, stop=True))
                    p.step("scalar", lambda e: e.activation(
                        out=b.r3[:], in_=b.ps_c[0:1, 0:512], func=AF.Sqrt))
                    p.step("vector", lambda e: e.tensor_scalar(
                        out=b.r3[:], in0=b.r3[:], scalar1=1e-10, scalar2=None, op0=ALU.add))
                    p.step("vector", lambda e: e.reciprocal(out=b.r3[:], in_=b.r3[:]))
                    p.step("vector", lambda e, h=h: e.tensor_scalar(
                        out=b.rb1[:], in0=b.r3[:], scalar1=Rrow[0:1, h:h + 1], scalar2=None,
                        op0=ALU.mult))
                    p.step("tensor", lambda e: e.matmul(
                        b.ps_c[:, 0:512], b.onesb_row[0:1, 0:128], b.rb1[:],
                        start=True, stop=True))
                    p.step("vector", lambda e, rb=rb: e.tensor_copy(
                        out=b.scf[rb:rb + 64, 0:512], in_=b.ps_c[0:64, 0:512]))
                    p.step("vector", lambda e, h=h, c=c, rb=rb, s=s: e.tensor_mul(
                        out=kh[rb:rb + 64, c * T + s * 512: c * T + (s + 1) * 512],
                        in0=kh[rb:rb + 64, c * T + s * 512: c * T + (s + 1) * 512],
                        in1=b.scf[rb:rb + 64, 0:512]))
            tap(b, f"khat{psx}", lambda: kh[:, 0:T], [128, T], BF16)

        with ExitStack() as ph2:
            def sb2(name, shape, dtype):
                return ph2.enter_context(b.nc.sbuf_tensor(_uq(name), shape, dtype))
            vt = sb2("l1vt", [128, NTILE * NH * 65], BF16)
            wtv = sb2("l1wtv", [128, NCH * 1024], BF16)
            wnatv = sb2("l1wn2", [128, 1024], BF16)
            xntv = sb2("xnt2", [128, NCH * 128], BF16)
            vring = [sb2(f"vring{i}", [128, 1024], BF16) for i in range(2)]
            vcf = sb2("vcfr", [128, NH], F32)
            vfr = sb2("vfacr", [128, NH], F32)
            nrm = sb2("vnrm", [128, NH], F32)
            r2c = sb2("r2c", [128, NH], F32)
            ost = sb2("ost1", [128, NCH * 512], BF16)

            build_wT("wv1", wtv, wnatv)
            p.step("vector", lambda e: e.memset(
                vt[:].rearrange("p (t h c) -> p t h c", h=NH, c=65)[:, :, :, 64:65], 1.0))
            p.dma("sync", b.r3[0:1, 0:NH], di["vcf"])
            p.step("tensor", lambda e: e.matmul(
                b.ps_c[:, 0:NH], b.onesf_row[:], b.r3[0:1, 0:NH], start=True, stop=True))
            p.step("vector", lambda e: e.tensor_copy(out=vcf[:], in_=b.ps_c[:, 0:NH]))
            p.step("tensor", lambda e: e.matmul(
                b.ps_c[:, 0:NH], b.onesf_row[:], vfac[:], start=True, stop=True))
            p.step("vector", lambda e: e.tensor_copy(out=vfr[:], in_=b.ps_c[:, 0:NH]))

            for t in range(NTILE + 1):
                if t < NTILE:
                    load_xn_tile(b, xntv, t)
                    for half in range(2):
                        for fc in range(NCH):
                            p.step("tensor", lambda e, half=half, fc=fc: e.matmul(
                                b.ps_b[:, half * 512:(half + 1) * 512],
                                xntv[:, fc * 128:(fc + 1) * 128],
                                wtv[:, fc * 1024 + half * 512: fc * 1024 + (half + 1) * 512],
                                start=(fc == 0), stop=(fc == NCH - 1)))
                    p.step("scalar", lambda e, t=t: e.activation(
                        out=vring[t % 2][:], in_=b.ps_b[:, 0:1024], func=AF.Identity))
                if t == 0:
                    continue
                tp = t - 1
                vcur = vring[tp % 2]
                vnxt = vring[t % 2] if t < NTILE else None
                shp = b.scb
                if vnxt is None:
                    p.step("vector", lambda e: e.memset(b.scb[:, 0:1024], 0.0))
                p.dma("sync", shp[0:127, 0:1024], vcur[1:128, :])
                if vnxt is not None:
                    p.dma("sync", shp[127:128, 0:1024], vnxt[0:1, :])
                p.step("vector", lambda e, vcur=vcur: e.tensor_sub(
                    out=b.scf[:, 0:1024], in0=vcur[:], in1=shp[:, 0:1024]))
                p.step("vector", lambda e: e.tensor_tensor(
                    out=b.scf[:, 0:1024].rearrange("p (h c) -> p h c", c=64),
                    in0=b.scf[:, 0:1024].rearrange("p (h c) -> p h c", c=64),
                    in1=vcf[:].unsqueeze(2).broadcast_to((128, NH, 64)),
                    op=ALU.mult))
                p.step("vector", lambda e: e.tensor_add(
                    out=b.scf[:, 0:1024], in0=b.scf[:, 0:1024], in1=shp[:, 0:1024]))
                p.step("scalar", lambda e: e.activation(
                    out=b.scf[:, 1024:2048], in_=b.scf[:, 0:1024], func=AF.Square))
                p.step("vector", lambda e: e.tensor_reduce(
                    out=nrm[:], in_=b.scf[:, 1024:2048].rearrange("p (h c) -> p h c", c=64),
                    axis=AX.X, op=ALU.add))
                p.step("scalar", lambda e: e.activation(out=nrm[:], in_=nrm[:], func=AF.Sqrt))
                p.step("vector", lambda e: e.tensor_scalar(
                    out=nrm[:], in0=nrm[:], scalar1=1e-10, scalar2=None, op0=ALU.add))
                p.step("vector", lambda e: e.reciprocal(out=nrm[:], in_=nrm[:]))
                p.step("vector", lambda e: e.tensor_mul(out=nrm[:], in0=nrm[:], in1=vfr[:]))
                p.step("vector", lambda e, tp=tp: e.tensor_tensor(
                    out=vt[:, tp * (NH * 65):(tp + 1) * (NH * 65)]
                    .rearrange("p (h c) -> p h c", c=65)[:, :, 0:64],
                    in0=b.scf[:, 0:1024].rearrange("p (h c) -> p h c", c=64),
                    in1=nrm[:].unsqueeze(2).broadcast_to((128, NH, 64)),
                    op=ALU.mult))
            tap(b, f"vhat{psx}", lambda: vt[:, 0:NH * 65], [128, NH * 65], BF16)

            build_wT("wc1", wtv, wnatv)

            p.step("tensor", lambda e: e.matmul(
                b.ps_c[:, 0:NH], b.onesf_row[:], R2n[:], start=True, stop=True))
            p.step("vector", lambda e: e.tensor_copy(out=r2c[:], in_=b.ps_c[:, 0:NH]))

            for s in range(NSTR):
                for h in range(NH):
                    c, rb = h // 2, (h % 2) * 64
                    nkc = 4 * s + 4
                    for kcp in range(2 * s):
                        k0, k1 = 2 * kcp, 2 * kcp + 1
                        for j, kc in enumerate((k0, k1)):
                            p.step("tensor", lambda e, kc=kc, j=j, c=c, rb=rb, s=s: e.matmul(
                                b.ps_b[:, j * 512:(j + 1) * 512],
                                kh[rb:rb + 64, c * T + kc * 128: c * T + (kc + 1) * 128],
                                kh[rb:rb + 64, c * T + s * 512: c * T + (s + 1) * 512],
                                start=True, stop=True))
                        p.step("scalar", lambda e, h=h: e.activation(
                            out=b.ebuf[:], in_=b.ps_b[:, 0:1024], func=AF.Exp,
                            bias=r2c[:, h:h + 1]))
                        for j, kc in enumerate((k0, k1)):
                            p.step("tensor", lambda e, kc=kc, h=h, j=j: e.matmul(
                                b.ps_a[0:65, 0:512],
                                vt[:, kc * (NH * 65) + h * 65: kc * (NH * 65) + (h + 1) * 65],
                                b.ebuf[:, j * 512:(j + 1) * 512],
                                start=(kc == 0), stop=False))
                    for kc in range(4 * s, nkc):
                        p.step("tensor", lambda e, kc=kc, c=c, rb=rb, s=s: e.matmul(
                            b.ps_c[:, 0:512],
                            kh[rb:rb + 64, c * T + kc * 128: c * T + (kc + 1) * 128],
                            kh[rb:rb + 64, c * T + s * 512: c * T + (s + 1) * 512],
                            start=True, stop=True))
                        p.step("scalar", lambda e, h=h: e.activation(
                            out=b.ebuf[:, 0:512], in_=b.ps_c[:, 0:512], func=AF.Exp,
                            bias=r2c[:, h:h + 1]))
                        r = kc - 4 * s
                        p.step("vector", lambda e, r=r: e.tensor_mul(
                            out=b.ebuf[:, 0:512], in0=b.ebuf[:, 0:512],
                            in1=b.sup[:, 384 - 128 * r: 896 - 128 * r]))
                        p.step("tensor", lambda e, kc=kc, h=h, last=(kc == nkc - 1): e.matmul(
                            b.ps_a[0:65, 0:512],
                            vt[:, kc * (NH * 65) + h * 65: kc * (NH * 65) + (h + 1) * 65],
                            b.ebuf[:, 0:512], start=(kc == 0), stop=last))
                    p.step("vector", lambda e: e.tensor_scalar(
                        out=b.r3[0:1, 0:512], in0=b.ps_a[64:65, 0:512], scalar1=1e-30,
                        scalar2=None, op0=ALU.add))
                    p.step("vector", lambda e: e.reciprocal(
                        out=b.r4[0:1, 0:512], in_=b.r3[0:1, 0:512]))
                    p.step("vector", lambda e: e.tensor_copy(
                        out=b.rb1[0:1, 0:512], in_=b.r4[0:1, 0:512]))
                    p.step("tensor", lambda e: e.matmul(
                        b.ps_c[0:64, 0:512], b.onesb_row[0:1, 0:64], b.rb1[0:1, 0:512],
                        start=True, stop=True))
                    p.step("vector", lambda e: e.tensor_copy(
                        out=b.scf[0:64, 0:512], in_=b.ps_c[0:64, 0:512]))
                    p.step("vector", lambda e, h=h, c=c, rb=rb: e.tensor_mul(
                        out=ost[rb:rb + 64, c * 512:(c + 1) * 512],
                        in0=b.ps_a[0:64, 0:512], in1=b.scf[0:64, 0:512]))
                for oc in range(NCH):
                    for ci in range(NCH):
                        p.step("tensor", lambda e, oc=oc, ci=ci: e.matmul(
                            b.ps_b[:, 0:512],
                            wtv[:, ci * 1024 + oc * 128: ci * 1024 + (oc + 1) * 128],
                            ost[:, ci * 512:(ci + 1) * 512],
                            start=(ci == 0), stop=(ci == NCH - 1)))
                    p.step("vector", lambda e: e.tensor_copy(
                        out=b.scf[:, 0:512], in_=b.ps_b[:, 0:512]))
                    emit_add_to_x(b, b.scf[:, 0:512], s, oc)


def emit_add_shifted(b, src_ap, s, oc, qn):
    p = b.prog
    for st in range((qn + 127) // 128):
        w = min(128, qn - st * 128)
        p.step("tensor", lambda e, st=st, w=w: e.transpose(
            b.ps_c[:, 0:w] if False else b.ps_c[0:w, 0:128],
            src_ap[:, st * 128:st * 128 + w], b.eyef[:]))
        pos0 = s * 512 + 1 + st * 128
        t0, r0 = pos0 // 128, pos0 % 128
        n1 = min(w, 128 - r0)
        p.step("vector", lambda e, t0=t0, r0=r0, n1=n1, oc=oc: e.tensor_add(
            out=b.x[r0:r0 + n1, t0 * DM + oc * 128: t0 * DM + (oc + 1) * 128],
            in0=b.x[r0:r0 + n1, t0 * DM + oc * 128: t0 * DM + (oc + 1) * 128],
            in1=b.ps_c[0:n1, 0:128]))
        if w > n1:
            n2 = w - n1
            p.step("vector", lambda e, t0=t0, n1=n1, n2=n2, oc=oc: e.tensor_add(
                out=b.x[0:n2, (t0 + 1) * DM + oc * 128: (t0 + 1) * DM + (oc + 1) * 128],
                in0=b.x[0:n2, (t0 + 1) * DM + oc * 128: (t0 + 1) * DM + (oc + 1) * 128],
                in1=b.ps_c[n1:n1 + n2, 0:128]))


def emit_l1_ffn(b, ctx, psx):
    p, di = b.prog, b.di
    with ExitStack() as ph:
        def sb(name, shape, dtype):
            return ph.enter_context(b.nc.sbuf_tensor(_uq(name), shape, dtype))
        xnst = sb("xnstf", [128, NCH * 512], BF16)
        z = sb("zrow", [8, T], F32)
        gsel = sb("gsel", [8, T], F32)
        gselb = sb("gselb", [8, T], BF16)
        selT = sb("selT", [128, NTILE * 8], F32)
        biask = sb("biask", [128, NTILE * 8], F32)
        wrT = sb("wrT", [128, NCH * 8], BF16)
        qe = sb("qe", [128, T], BF16)
        ke = sb("ke", [128, T], BF16)
        ve = sb("ve", [128, NTILE * 128], BF16)
        wqkv = sb("wqkv", [128, NCH * 384], BF16)
        wo1 = sb("wo1s", [128, DM], BF16)
        ost = sb("ostf", [128, 512], BF16)
        lo = sb("lo8", [8, 1], F32)
        hi = sb("hi8", [8, 1], F32)
        th = sb("th8", [8, 1], F32)
        cnt = sb("cnt8", [8, 1], F32)
        dput = sb("dput8", [8, 1], F32)

        p.dma("gpsimd", b.scb[0:8, 0:DM], di["wr1"])
        for fc in range(NCH):
            p.step("tensor", lambda e, fc=fc: e.transpose(
                b.ps_tb[:, 0:8], b.scb[0:8, fc * 128:(fc + 1) * 128], b.eyeb[0:8, 0:8]))
            p.step("vector", lambda e, fc=fc: e.tensor_copy(
                out=wrT[:, fc * 8:(fc + 1) * 8], in_=b.ps_tb[:, 0:8]))

        for s in range(NSTR):
            load_xn_strip(b, xnst, s)
            for fc in range(NCH):
                p.step("tensor", lambda e, fc=fc: e.matmul(
                    b.ps_c[0:8, 0:512], wrT[:, fc * 8:(fc + 1) * 8],
                    xnst[:, fc * 512:(fc + 1) * 512],
                    start=(fc == 0), stop=(fc == NCH - 1)))
            p.step("vector", lambda e, s=s: e.tensor_copy(
                out=z[:, s * 512:(s + 1) * 512], in_=b.ps_c[0:8, 0:512]))

        p.step("vector", lambda e: e.memset(lo[:], -100.0))
        p.step("vector", lambda e: e.memset(hi[:], 100.0))
        for it in range(26):
            p.step("vector", lambda e: e.tensor_add(out=th[:], in0=lo[:], in1=hi[:]))
            p.step("vector", lambda e: e.tensor_scalar(
                out=th[:], in0=th[:], scalar1=0.5, scalar2=None, op0=ALU.mult))
            p.step("vector", lambda e: e.tensor_scalar(
                out=b.scf[0:8, 0:T], in0=z[:], scalar1=th[:], scalar2=None, op0=ALU.is_ge))
            p.step("vector", lambda e: e.tensor_reduce(
                out=cnt[:], in_=b.scf[0:8, 0:T], axis=AX.X, op=ALU.add))
            p.step("vector", lambda e: e.tensor_scalar(
                out=dput[:], in0=cnt[:], scalar1=256.0, scalar2=None, op0=ALU.is_ge))
            p.step("vector", lambda e: e.tensor_sub(out=cnt[:], in0=th[:], in1=lo[:]))
            p.step("vector", lambda e: e.scalar_tensor_tensor(
                out=lo[:], in0=dput[:], scalar=cnt[:], in1=lo[:], op0=ALU.mult, op1=ALU.add))
            p.step("vector", lambda e: e.tensor_scalar(
                out=dput[:], in0=dput[:], scalar1=-1.0, scalar2=1.0, op0=ALU.mult, op1=ALU.add))
            p.step("vector", lambda e: e.tensor_sub(out=cnt[:], in0=th[:], in1=hi[:]))
            p.step("vector", lambda e: e.scalar_tensor_tensor(
                out=hi[:], in0=dput[:], scalar=cnt[:], in1=hi[:], op0=ALU.mult, op1=ALU.add))
        p.step("vector", lambda e: e.tensor_scalar(
            out=b.scf[0:8, 0:T], in0=z[:], scalar1=lo[:], scalar2=None, op0=ALU.is_ge))
        p.step("scalar", lambda e: e.activation(out=gsel[:], in_=z[:], func=AF.Sigmoid))
        p.step("vector", lambda e: e.tensor_mul(
            out=gsel[:], in0=gsel[:], in1=b.scf[0:8, 0:T]))
        p.step("vector", lambda e: e.tensor_copy(out=gselb[:], in_=gsel[:]))
        for t in range(NTILE):
            p.step("tensor", lambda e, t=t: e.transpose(
                b.ps_c[:, 0:8], b.scf[0:8, t * 128:(t + 1) * 128], b.eyef[0:8, 0:8]))
            p.step("vector", lambda e, t=t: e.tensor_copy(
                out=selT[:, t * 8:(t + 1) * 8], in_=b.ps_c[:, 0:8]))
        p.step("vector", lambda e: e.tensor_scalar(
            out=biask[:], in0=selT[:], scalar1=-1.0, scalar2=None, op0=ALU.add))
        p.step("vector", lambda e: e.tensor_scalar(
            out=biask[:], in0=biask[:], scalar1=30000.0, scalar2=None, op0=ALU.mult))
        tap(b, f"gsel{psx}", lambda: gsel[:], [8, T], F32)

        for e1 in range(NE1):
            p.dma("gpsimd", wqkv[:].rearrange("p (c n) -> p c n", n=384),
                  di["wqkv1"][e1].rearrange("(c p) n -> p c n", p=128))
            p.dma("gpsimd", wo1[:], di["wo1"][e1])
            for s in range(NSTR):
                load_xn_strip(b, xnst, s)
                for which, dst in ((0, qe), (1, ke)):
                    for fc in range(NCH):
                        p.step("tensor", lambda e, which=which, fc=fc: e.matmul(
                            b.ps_b[:, 0:512],
                            wqkv[:, fc * 384 + which * 128: fc * 384 + (which + 1) * 128],
                            xnst[:, fc * 512:(fc + 1) * 512],
                            start=(fc == 0), stop=(fc == NCH - 1)))
                    p.step("scalar", lambda e, dst=dst, s=s: e.activation(
                        out=dst[:, s * 512:(s + 1) * 512], in_=b.ps_b[:, 0:512],
                        func=AF.Identity))
                for tt in range(4):
                    t = s * 4 + tt
                    for fc in range(NCH):
                        p.step("tensor", lambda e, t=t, tt=tt, fc=fc: e.matmul(
                            b.ps_c[:, 0:128],
                            xnst[:, fc * 512 + tt * 128: fc * 512 + (tt + 1) * 128],
                            wqkv[:, fc * 384 + 256: fc * 384 + 384],
                            start=(fc == 0), stop=(fc == NCH - 1)))
                    p.step("vector", lambda e, t=t: e.tensor_copy(
                        out=ve[:, t * 128:(t + 1) * 128], in_=b.ps_c[:, 0:128]))
            emit_rope(b, qe[0:64, 0:T], T, 0, rows=64)
            emit_rope(b, ke[0:64, 0:T], T, 0, rows=64)
            if psx == 0 and e1 == 0:
                tap(b, "qe0", lambda: qe[:], [128, T], BF16)
                tap(b, "ve0", lambda: ve[:], [128, NTILE * 128], BF16)

            for s in range(NSTR):
                if s == 0:
                    p.step("scalar", lambda e: e.activation(
                        out=b.scf[:, 0:T], in_=ke[:], func=AF.Square))
                    for ss in range(NSTR):
                        p.step("tensor", lambda e, ss=ss: e.matmul(
                            b.ps_c[0:1, 0:512], b.onesf_col[:],
                            b.scf[:, ss * 512:(ss + 1) * 512], start=True, stop=True))
                        p.step("vector", lambda e, ss=ss: e.tensor_copy(
                            out=(b.r3 if ss == 0 else b.r4)[0:1, 0:512],
                            in_=b.ps_c[0:1, 0:512]))
                        if ss > 0:
                            p.step("vector", lambda e: e.tensor_max(
                                out=b.r3[:], in0=b.r3[:], in1=b.r4[:]))
                    p.step("vector", lambda e: e.tensor_reduce(
                        out=b.col[5][0:1, :], in_=b.r3[:], axis=AX.X, op=ALU.max))
                p.step("scalar", lambda e, s=s: e.activation(
                    out=b.scf[:, 0:512], in_=qe[:, s * 512:(s + 1) * 512], func=AF.Square))
                p.step("tensor", lambda e: e.matmul(
                    b.ps_c[0:1, 0:512], b.onesf_col[:], b.scf[:, 0:512],
                    start=True, stop=True))
                p.step("vector", lambda e: e.tensor_scalar(
                    out=b.r3[0:1, 0:512], in0=b.ps_c[0:1, 0:512],
                    scalar1=b.col[5][0:1, 0:1], scalar2=-0.5, op0=ALU.add, op1=ALU.mult))
                p.step("vector", lambda e: e.tensor_copy(out=b.rb1[:], in_=b.r3[0:1, 0:512]))
                nkc = 4 * s + 4
                for kc in range(nkc):
                    p.step("tensor", lambda e, kc=kc, s=s: e.matmul(
                        b.ps_c[:, 0:512],
                        ke[:, kc * 128:(kc + 1) * 128], qe[:, s * 512:(s + 1) * 512],
                        start=True, stop=False))
                    p.step("tensor", lambda e: e.matmul(
                        b.ps_c[:, 0:512], b.onesb_row[:], b.rb1[:], start=False, stop=True))
                    p.step("scalar", lambda e, kc=kc: e.activation(
                        out=b.ebuf[:, 0:512], in_=b.ps_c[:, 0:512], func=AF.Exp,
                        scale=float(1.0 / np.sqrt(HP)),
                        bias=biask[:, kc * 8 + e1: kc * 8 + e1 + 1]))
                    r = kc - 4 * s
                    if r >= 0:
                        p.step("vector", lambda e, r=r: e.tensor_mul(
                            out=b.ebuf[:, 0:512], in0=b.ebuf[:, 0:512],
                            in1=b.sup[:, 385 - 128 * r: 897 - 128 * r]))
                    p.step("tensor", lambda e, kc=kc, last=(kc == nkc - 1): e.matmul(
                        b.ps_a[:, 0:512], ve[:, kc * 128:(kc + 1) * 128], b.ebuf[:, 0:512],
                        start=(kc == 0), stop=last))
                    p.step("tensor", lambda e, kc=kc, last=(kc == nkc - 1): e.matmul(
                        b.ps_a[0:1, 512:1024], b.onesb_col[:], b.ebuf[:, 0:512],
                        start=(kc == 0), stop=last))
                p.step("vector", lambda e: e.tensor_scalar(
                    out=b.r3[0:1, 0:512], in0=b.ps_a[0:1, 512:1024], scalar1=1e-30,
                    scalar2=None, op0=ALU.add))
                p.step("vector", lambda e: e.reciprocal(out=b.r4[0:1, 0:512], in_=b.r3[0:1, 0:512]))
                p.step("vector", lambda e: e.tensor_copy(out=b.rb1[:], in_=b.r4[0:1, 0:512]))
                p.step("tensor", lambda e: e.matmul(
                    b.ps_c[:, 0:512], b.onesb_row[:], b.rb1[:], start=True, stop=True))
                p.step("vector", lambda e: e.tensor_copy(
                    out=b.scf[:, 0:512], in_=b.ps_c[:, 0:512]))
                p.step("tensor", lambda e, e1=e1, s=s: e.matmul(
                    b.ps_c[:, 0:512], b.e8[:, e1 * 128:(e1 + 1) * 128],
                    gselb[:, s * 512:(s + 1) * 512], start=True, stop=True))
                p.step("vector", lambda e: e.tensor_mul(
                    out=b.scf[:, 0:512], in0=b.scf[:, 0:512], in1=b.ps_c[:, 0:512]))
                p.step("vector", lambda e: e.tensor_mul(
                    out=ost[:, 0:512], in0=b.ps_a[:, 0:512], in1=b.scf[:, 0:512]))
                for tt in range(4):
                    t = s * 4 + tt
                    for half in range(2):
                        p.step("tensor", lambda e, tt=tt, half=half: e.matmul(
                            b.ps_b[:, half * 512:(half + 1) * 512],
                            ost[:, tt * 128:(tt + 1) * 128],
                            wo1[:, half * 512:(half + 1) * 512],
                            start=True, stop=True))
                    for half in range(2):
                        p.step("vector", lambda e, t=t, half=half: e.tensor_add(
                            out=b.x[:, t * DM + half * 512: t * DM + (half + 1) * 512],
                            in0=b.x[:, t * DM + half * 512: t * DM + (half + 1) * 512],
                            in1=b.ps_b[:, half * 512:(half + 1) * 512]))


_CACHE = {}


def kernel(**inputs):
    taps = inputs.pop("_taps", ())
    blocks = inputs.pop("_blocks", (0, 1))
    key = (tuple(taps), tuple(blocks))
    if key not in _CACHE:
        _CACHE[key] = build_program(blocks=blocks, taps=taps)
    nc, b = _CACHE[key]

    hc = host_consts()
    wmap = {
        "wq": inputs["L0_attn_W_q"], "wdkv": inputs["L0_attn_W_dkv"],
        "wuk": inputs["L0_attn_W_uk"], "wuv": inputs["L0_attn_W_uv"],
        "wo": inputs["L0_attn_W_o"],
        "lng": inputs["L0_attn_ln_gamma"].reshape(1, LAT),
        "lnb": inputs["L0_attn_ln_beta"].reshape(1, LAT),
        "rw": inputs["L0_ffn_router_w"], "rb": inputs["L0_ffn_router_b"].reshape(1, NE0),
        "w1": inputs["L0_ffn_w1"], "b1": inputs["L0_ffn_b1"],
        "w2": inputs["L0_ffn_w2"], "b2": inputs["L0_ffn_b2"],
        "wk1": inputs["L1_attn_W_k"], "wv1": inputs["L1_attn_W_v"],
        "wc1": inputs["L1_attn_W_c"],
        "lkb": inputs["L1_attn_leaky_key_beta"].reshape(1, NH),
        "ksc": inputs["L1_attn_key_scale"].reshape(1, NH),
        "vcf": inputs["L1_attn_v_coef"].reshape(1, NH),
        "vsc": inputs["L1_attn_val_scale"].reshape(1, NH),
        "wr1": inputs["L1_ffn_Wr"], "wqkv1": inputs["L1_ffn_Wqkv"],
        "wo1": inputs["L1_ffn_Wo"],
    }
    wmap.update(hc)
    x = inputs["L0_attn_x"]
    in_maps = []
    for c in range(8):
        m = dict(wmap)
        m["xin"] = np.ascontiguousarray(x[[c, c + 8]])
        in_maps.append(m)
    res = run_bass_kernel_spmd(nc, in_maps, core_ids=list(range(8)))
    out = np.empty((16, T, DM), dtype=np.float32)
    for c in range(8):
        out[c] = res.results[c]["out"][0]
        out[c + 8] = res.results[c]["out"][1]
    if taps:
        tapout = {nm: [res.results[c]["tap_" + nm] for c in range(8)] for nm in b.taps}
        return out, tapout
    return out
